# version 25
# baseline (speedup 1.0000x reference)
import numpy as np
import ml_dtypes
from contextlib import ExitStack
from types import SimpleNamespace

import concourse.bass as bass
import concourse.mybir as mybir
from concourse.bass_utils import run_bass_kernel_spmd

F32 = mybir.dt.float32
BF16 = mybir.dt.bfloat16
AF = mybir.ActivationFunctionType
ALU = mybir.AluOpType
AX = mybir.AxisListType

D = 1024
I = 2816
DC = D // 128
IC = I // 128
NBD = D // 512
NBI = (I + 511) // 512
ZR = 2
LNEPS = 1e-5
RECIP_NORM = float(np.float32(32.000001))


def iblk(n):
    out = []
    o = 0
    while o < n:
        out.append((o, min(512, n - o)))
        o += 512
    return out


class Cap:

    def __init__(self, eng):
        object.__setattr__(self, "_e", eng)
        object.__setattr__(self, "last", None)

    def __getattr__(self, nm):
        a = getattr(self._e, nm)
        if callable(a):
            def w(*args, **kw):
                r = a(*args, **kw)
                if hasattr(r, "then_inc"):
                    object.__setattr__(self, "last", r)
                return r
            return w
        return a


class B:

    def __init__(self, E, eng, sems, safe=False):
        self.E = E
        self.eng = eng
        self.cap = Cap(eng) if eng is not None else None
        self.safe = safe
        self.sems = sems
        self.C = {k: 0 for k in sems}

    def w(self, engine, sem, val=None):
        if val is None:
            val = self.C[sem]
        if self.E == engine and val > 0:
            self.eng.wait_ge(self.sems[sem], int(val))

    def dma(self, issuer, sem, out, in_):
        self.C[sem] += 16
        if self.E == issuer:
            if issuer == "scalar":
                self.cap.drain()
            self.eng.dma_start(out=out, in_=in_).then_inc(self.sems[sem], 16)
        return self.C[sem]

    def t(self, fn):
        if self.E == "tensor":
            fn(self.cap)

    def tpub(self):
        self.C["SM"] += 1
        if self.E == "tensor":
            self.cap.last.then_inc(self.sems["SM"], 1)
        return self.C["SM"]

    dall_v = False
    dall_s = False

    def v(self, fn):
        if self.E == "vector":
            fn(self.cap)
            if self.dall_v:
                self.cap.drain()

    def vn(self, fn):
        if self.E == "vector":
            fn(self.cap)

    def vd(self):
        if self.E == "vector":
            self.cap.drain()

    def vpub(self):
        self.C["SV"] += 1
        if self.E == "vector":
            self.cap.drain()
            self.cap.last.then_inc(self.sems["SV"], 1)
        return self.C["SV"]

    def s(self, fn):
        if self.E == "scalar":
            fn(self.cap)
            if self.dall_s:
                self.cap.drain()

    def spub(self):
        self.C["SS"] += 1
        if self.E == "scalar":
            self.cap.drain()
            self.cap.last.then_inc(self.sems["SS"], 1)
        return self.C["SS"]

    def barrier(self):
        for e in ("sync", "tensor", "vector", "scalar"):
            for sem in self.sems:
                self.w(e, sem)


def build(S, safe=False):
    NT = S // 128
    GQ = min(1024, S)
    NG = S // GQ
    TPG = GQ // 128
    NBG = GQ // 512
    NTB = NT // ZR

    nc = bass.Bass("TRN2", target_bir_lowering=False, debug=False,
                   detect_race_conditions=bool(safe))

    h = SimpleNamespace()
    h.x = nc.dram_tensor("x", [S, D], F32, kind="ExternalInput").ap()
    for nm, shp in [("wq", [D, D]), ("wk", [D, D]), ("wv", [D, D]), ("wo", [D, D]),
                    ("wg", [I, D]), ("wu", [I, D]), ("wd", [D, I])]:
        setattr(h, nm, nc.dram_tensor(nm, shp, F32, kind="ExternalInput").ap())
    h.bqT = nc.dram_tensor("bqT", [128, DC], F32, kind="ExternalInput").ap()
    h.bkT = nc.dram_tensor("bkT", [128, DC], F32, kind="ExternalInput").ap()
    h.bv = nc.dram_tensor("bv_row", [1, D], F32, kind="ExternalInput").ap()
    h.bo = nc.dram_tensor("bo_row", [1, D], F32, kind="ExternalInput").ap()
    h.ident = nc.dram_tensor("ident", [128, 128], BF16, kind="ExternalInput").ap()
    h.y = nc.dram_tensor("y", [S, D], F32, kind="ExternalOutput").ap()

    h.qT_scr = nc.dram_tensor("qT_scr", [DC, 128, S], BF16, kind="Internal").ap()
    h.kT_scr = nc.dram_tensor("kT_scr", [DC, 128, S], BF16, kind="Internal").ap()
    h.v_scr = nc.dram_tensor("v_scr", [NT, 128, D], BF16, kind="Internal").ap()
    h.x2_scr = nc.dram_tensor("x2_scr", [S, D], F32, kind="Internal").ap()
    h.h2qT_scr = nc.dram_tensor("h2qT_scr", [NT, 128, D], BF16, kind="Internal").ap()
    h.wgT_scr = nc.dram_tensor("wgT_scr", [DC, 128, I], BF16, kind="Internal").ap()
    h.wuT_scr = nc.dram_tensor("wuT_scr", [DC, 128, I], BF16, kind="Internal").ap()
    h.wdT_scr = nc.dram_tensor("wdT_scr", [IC, 128, D], BF16, kind="Internal").ap()

    ctx = ExitStack()
    a = SimpleNamespace()

    def sb(name, shape, dt, scope=ctx):
        return scope.enter_context(nc.sbuf_tensor(name, shape, dt))

    def ps(name, shape, dt, scope):
        return scope.enter_context(nc.psum_tensor(name, shape, dt))

    sem_names = ["SM", "SV", "SS",
                 "dA0", "dA1", "dW", "dM", "dSt0", "dSt1",
                 "dKt0", "dKt1", "dQt", "dVt0", "dVt1", "dXr0", "dXr1",
                 "dX20", "dX21", "dZ", "dDw", "dXl0", "dXl1", "dY0", "dY1"]

    with ctx:
        sems = {nm: ctx.enter_context(nc.semaphore(nm)) for nm in sem_names}

        a.ident = sb("identB", [128, 128], BF16)
        a.ones_c = sb("ones_c", [128, 1], BF16)
        a.ones_r = sb("ones_r", [64, 128], BF16)
        a.bqT = sb("bqT_s", [128, DC], F32)
        a.bkT = sb("bkT_s", [128, DC], F32)

        a.sD1 = sb("sD1", [128, NT], F32)
        a.sD2 = sb("sD2", [128, NT], F32)
        a.rD = sb("rD", [128, NT], F32)
        a.nD = sb("nD", [128, NT], F32)
        a.tmp1 = sb("tmp1", [128, NT], F32)
        a.tieb = sb("tieb", [128, 1], F32)
        a.cnrm = sb("cnrm", [128, 1], F32)

        sLong = ExitStack()
        a.woT = sb("woT", [128, DC * 1024], BF16, sLong)
        a.bvB = sb("bvB", [1, D], BF16, sLong)
        a.boB = sb("boB", [1, D], BF16, sLong)
        with ExitStack() as sAB:
            a.w0f = [sb(f"w0f{i}", [128, 1024], F32, sAB) for i in range(2)]
            a.w0b = sb("w0b", [128, 1024], BF16, sAB)
            a.bv32 = sb("bv32", [1, D], F32, sAB)
            a.bo32 = sb("bo32", [1, D], F32, sAB)
            a.xnT = sb("xnT", [128, DC * S], BF16, sAB)
            a.sA1 = sb("sA1", [128, NT], F32, sAB)
            a.sA2 = sb("sA2", [128, NT], F32, sAB)
            a.rA = sb("rA", [128, NT], F32, sAB)
            a.nA = sb("nA", [128, NT], F32, sAB)
            a.wqT = sb("wqT", [128, DC * 1024], BF16, sAB)
            a.wkT = sb("wkT", [128, DC * 1024], BF16, sAB)
            a.wvT = sb("wvT", [128, DC * 1024], BF16, sAB)
            a.xb = [sb(f"xb{i}", [128, D], F32, sAB) for i in range(2)]
            a.xn = [sb(f"xn{i}", [128, D], BF16, sAB) for i in range(2)]
            a.stageB = [sb(f"stageB{i}", [128, S], BF16, sAB) for i in range(2)]
            a.vst = [sb(f"vst{i}", [128, D], BF16, sAB) for i in range(2)]
            a.sqd = sb("sqdA", [128, D], F32, sAB)
        with ExitStack() as sC:
            a.wf = [sb(f"wf{i}", [128, 1408], F32, sC) for i in range(2)]
            a.bmW = sb("bmW", [128, IC * 8], F32, sC)
            a.shW = sb("shW", [128, IC * 8], F32, sC)
            a.rcW = sb("rcW", [128, IC * 8], F32, sC)
            a.xnW = sb("xnW", [128, 1408], F32, sC)
            a.sgW = sb("sgW", [128, 1408], BF16, sC)
            a.acW = sb("acW", [128, 1408], BF16, sC)
            a.bbW = sb("bbW", [128, 1408], BF16, sC)
            a.dqW = sb("dqW", [128, 1408], BF16, sC)
            a.cW1 = sb("cW1", [128, 1408], BF16, sC)
            a.cW2 = sb("cW2", [128, 1408], BF16, sC)
            a.wst = [sb("wst0", [128, 1408], BF16, sC)]
            a.pT = sb("pT", [128, NT * GQ], BF16, sC)
            a.qTg = sb("qTg", [128, DC * GQ], BF16, sC)
            a.ktb = [sb(f"ktb{i}", [128, ZR * DC * 128], BF16, sC) for i in range(2)]
            a.vrn = [sb(f"vrn{i}", [128, D], BF16, sC) for i in range(2)]
            a.oT = sb("oT", [128, DC * GQ], BF16, sC)
            a.zmat = sb("zmat", [64, GQ], F32, sC)
            a.zmatB = sb("zmatB", [64, GQ], BF16, sC)
            a.xres = [sb("xres0", [128, D], F32, sC)]
            a.x2st = [sb("x2st0", [128, D], F32, sC)]

        sLong.close()
        with ExitStack() as sD:
            a.wgT = sb("wgT", [128, DC * I], BF16, sD)
            a.wuT = sb("wuT", [128, DC * I], BF16, sD)
            a.wdT = sb("wdT", [128, IC * D], BF16, sD)
            a.x2t = [sb(f"x2t{i}", [128, D], F32, sD) for i in range(2)]
            a.h2qT = [sb(f"h2qT{i}", [128, D], BF16, sD) for i in range(2)]
            a.bmD = sb("bmD", [128, IC * 8], F32, sD)
            a.shD = sb("shD", [128, IC * 8], F32, sD)
            a.rcD = sb("rcD", [128, IC * 8], F32, sD)
            a.xnD = sb("xnD", [128, I], BF16, sD)
            a.sgnD = sb("sgnD", [128, I], BF16, sD)
            a.acD = sb("acD", [128, I], BF16, sD)
            a.bbD = sb("bbD", [128, I], BF16, sD)
            a.cD1 = sb("cD1", [128, I], BF16, sD)
            a.cD2 = sb("cD2", [128, I], BF16, sD)
            a.h2b = a.cD2[:, 0:D]
            a.h3D = a.cD1
            a.sgD = sb("sgD", [128, I], BF16, sD)
            a.h3qT = sb("h3qT", [128, I], BF16, sD)
            a.yst = [sb(f"yst{i}", [128, D], F32, sD) for i in range(2)]

        pLong = ExitStack()
        a.psWT = ps("psWT", [128, 1024], BF16, pLong)
        with ExitStack() as p0:
            a.psW0 = ps("psW0", [128, 1024], BF16, p0)
            a.psTA = ps("psTA", [128, 1024], BF16, p0)
        with ExitStack() as p1:
            a.psQK = ps("psQK", [128, 2048], F32, p1)
        with ExitStack() as p2:
            a.psV = [ps(f"psV{i}", [128, D], F32, p2) for i in range(2)]
        with ExitStack() as p4:
            a.psS = [ps(f"psS{i}", [128, GQ], F32, p4) for i in range(2)]
            a.psB = [ps("psB0", [128, GQ], F32, p4)]
        with ExitStack() as p5:
            a.psO = [ps(f"psO{i}", [128, GQ], F32, p5) for i in range(3)]
        with ExitStack() as p6:
            a.psAF = [ps(f"psAF{i}", [128, D], F32, p6) for i in range(2)]
        pLong.close()
        with ExitStack() as p8:
            a.psGU = ps("psGU", [128, NBI * 512], F32, p8)
            with ExitStack() as p7:
                a.psT2 = ps("psT2", [128, 1024], BF16, p7)
            with ExitStack() as p9:
                a.psD = ps("psD", [128, D], F32, p9)

        def schedule(b: B):
            b.dall_v = True
            b.dall_s = True
            ssn = {}
            for hap, dst in [(h.ident, a.ident), (h.bqT, a.bqT), (h.bkT, a.bkT),
                             (h.bv, a.bv32), (h.bo, a.bo32)]:
                b.dma("sync", "dM", dst[:], hap[:])
            b.w("vector", "dM")
            b.v(lambda e: e.memset(a.ones_c[:], 1.0))
            b.v(lambda e: e.memset(a.ones_r[:], 1.0))
            b.v(lambda e: e.tensor_copy(a.bvB[:], a.bv32[:]))
            b.v(lambda e: e.tensor_copy(a.boB[:], a.bo32[:]))
            b.v(lambda e: e.tensor_scalar(a.bqT[:], a.bqT[:], 1.0 / 32.0, None, ALU.mult))
            b.v(lambda e: e.memset(a.tieb[:], 2.0 ** -49))
            b.v(lambda e: e.tensor_scalar(a.tieb[:], a.tieb[:], -1.0, 1.0, ALU.mult, ALU.add))
            b.v(lambda e: e.memset(a.cnrm[:], 1e-6))
            b.v(lambda e: e.tensor_scalar(a.cnrm[:], a.cnrm[:], 1.0 / 32.0, 1.0, ALU.mult, ALU.add))
            sv_misc = b.vpub()

            wT_map = [("wq", h.wq, a.wqT), ("wk", h.wk, a.wkT),
                      ("wv", h.wv, a.wvT), ("wo", h.wo, a.woT)]
            ss_wevac = 0
            sv_wcast = {}
            wld = {}
            wt_i = 0
            for wi, (nm, hap, dst) in enumerate(wT_map):
                for t in range(DC):
                    par = wt_i % 2
                    if wt_i >= 2:
                        b.w("sync", "SV", sv_wcast[wt_i - 2])
                    wld[wt_i] = b.dma("sync", f"dA{par}", a.w0f[par][:],
                                      hap[t * 128:(t + 1) * 128, :])
                    b.w("vector", f"dA{par}", wld[wt_i])
                    if wt_i >= 1:
                        b.w("vector", "SM", ssn.get(("w0tr", wt_i - 1), 0))
                    b.v(lambda e, par=par: e.tensor_copy(a.w0b[:], a.w0f[par][:]))
                    sv_wcast[wt_i] = b.vpub()
                    b.w("tensor", "SV", sv_wcast[wt_i])
                    b.w("tensor", "SV", sv_misc)
                    if wt_i > 0:
                        b.w("tensor", "SS", ss_wevac)
                    def trs(e):
                        for dc in range(DC):
                            e.transpose(a.psW0[:, dc * 128:(dc + 1) * 128],
                                        a.w0b[:, dc * 128:(dc + 1) * 128],
                                        a.ident[:])
                    b.t(trs)
                    sm = b.tpub()
                    ssn[("w0tr", wt_i)] = sm
                    b.w("scalar", "SM", sm)
                    def evac(e, dst=dst, t=t):
                        view = dst[:].rearrange("p (dc q) -> p dc q", q=1024)
                        e.copy(view[:, :, t * 128:(t + 1) * 128], a.psW0[:])
                    b.s(evac)
                    ss_wevac = b.spub()
                    wt_i += 1

            sv_stats = {}
            ss_stats = {}
            NW0 = 4 * DC
            for i in range(NT):
                par = i % 2
                if i >= 2:
                    b.w("sync", "SS", ss_stats[i - 2])
                else:
                    b.w("sync", "SV", sv_wcast[NW0 - 2 + i])
                b.dma("sync", f"dA{par}", a.xb[par][:], h.x[i * 128:(i + 1) * 128, :])
                b.w("vector", f"dA{par}")
                b.v(lambda e, i=i, par=par: e.tensor_reduce(
                    a.sA1[:, i:i + 1], a.xb[par][:], AX.X, ALU.add))
                sv_stats[i] = b.vpub()
                b.w("scalar", f"dA{par}")
                b.w("scalar", "SV", sv_stats[i])
                b.s(lambda e, i=i, par=par: e.activation(
                    a.sqd[:], a.xb[par][:], AF.Square, accum_out=a.sA2[:, i:i + 1]))
                ss_stats[i] = b.spub()

            def ln_batch(b, s1, s2, rst, nb, tmp):
                b.v(lambda e: e.tensor_scalar(tmp[:], s1[:], 1.0 / D, None, ALU.mult))
                b.v(lambda e: e.tensor_scalar(s2[:], s2[:], 1.0 / D, None, ALU.mult))
                b.v(lambda e: e.scalar_tensor_tensor(rst[:], tmp[:], -1.0, tmp[:], ALU.mult, ALU.mult))
                b.v(lambda e: e.tensor_add(rst[:], rst[:], s2[:]))
                b.v(lambda e: e.tensor_scalar(rst[:], rst[:], LNEPS, None, ALU.add))
                sv = b.vpub()
                b.w("scalar", "SV", sv)
                b.s(lambda e: e.activation(rst[:], rst[:], AF.Sqrt))
                ssx = b.spub()
                b.w("vector", "SS", ssx)
                b.v(lambda e: e.reciprocal(rst[:], rst[:]))
                b.v(lambda e: e.scalar_tensor_tensor(nb[:], tmp[:], -1.0, rst[:], ALU.mult, ALU.mult))
                return b.vpub()

            b.w("vector", "SS", ss_stats[NT - 1])
            sv_lnA = ln_batch(b, a.sA1, a.sA2, a.rA, a.nA, a.tmp1)

            sv_norm_done = {}
            ss_h = {}
            sm_tr = {}
            sv_xnT = {}
            for i in range(NT):
                par = i % 2
                if i >= 2:
                    b.w("sync", "SS", ss_h[i - 2])
                else:
                    b.w("sync", "SS", ss_stats[NT - 2 + i])
                b.dma("sync", f"dA{par}", a.xb[par][:], h.x[i * 128:(i + 1) * 128, :])
                b.w("scalar", f"dA{par}")
                b.w("scalar", "SV", sv_lnA)
                if i >= 2:
                    b.w("scalar", "SM", sm_tr[i - 2])
                b.s(lambda e, i=i, par=par: e.activation(
                    a.xn[par][:], a.xb[par][:], AF.Identity,
                    bias=a.nA[:, i:i + 1], scale=a.rA[:, i:i + 1]))
                ss_h[i] = b.spub()
                b.w("tensor", "SS", ss_h[i])
                if i >= 1:
                    b.w("tensor", "SV", sv_xnT[i - 1])
                def trs(e, par=par):
                    for dc in range(DC):
                        e.transpose(a.psTA[:, dc * 128:(dc + 1) * 128],
                                    a.xn[par][:, dc * 128:(dc + 1) * 128], a.ident[:])
                b.t(trs)
                sm_tr[i] = b.tpub()
                b.w("vector", "SM", sm_tr[i])
                def evac(e, i=i):
                    view = a.xnT[:].rearrange("p (dc s) -> p dc s", s=S)
                    e.tensor_copy(view[:, :, i * 128:(i + 1) * 128], a.psTA[:])
                b.v(evac)
                sv_xnT[i] = b.vpub()

            b.w("tensor", "SV", sv_xnT[NT - 1])
            ss_bevac = 0
            stp = 0
            st_of_stage = {}
            SH = min(2048, S)
            for (wTn, scr, scale, biasap) in [
                    (a.wqT, h.qT_scr, 1.0 / 32.0, a.bqT),
                    (a.wkT, h.kT_scr, 1.0, a.bkT)]:
                for dq in range(DC):
                    par = stp % 2
                    stp += 1
                    for half in range(S // SH):
                        if not (wTn is a.wqT and dq == 0 and half == 0):
                            b.w("tensor", "SS", ss_bevac)
                        def mmq(e, wTn=wTn, dq=dq, half=half):
                            for dc in range(DC):
                                lhsT = wTn[:, dc * 1024 + dq * 128: dc * 1024 + (dq + 1) * 128]
                                for (off, ln) in iblk(SH):
                                    so = half * SH + off
                                    e.matmul(a.psQK[:, off:off + ln],
                                             lhsT,
                                             a.xnT[:, dc * S + so: dc * S + so + ln],
                                             start=(dc == 0), stop=(dc == DC - 1))
                        b.t(mmq)
                        sm = b.tpub()
                        b.w("scalar", "SM", sm)
                        if half == 0 and par in st_of_stage:
                            b.w("scalar", f"dSt{par}", st_of_stage[par])
                        b.s(lambda e, par=par, scale=scale, biasap=biasap, dq=dq, half=half:
                            e.activation(a.stageB[par][:, half * SH:(half + 1) * SH],
                                         a.psQK[:, 0:SH], AF.Identity,
                                         bias=biasap[:, dq:dq + 1], scale=scale))
                        ss_bevac = b.spub()
                    st_of_stage[par] = b.dma("scalar", f"dSt{par}",
                                             scr[dq], a.stageB[par][:])
            ss_vevac = {}
            for i in range(NT):
                par2 = i % 2
                if i >= 2:
                    b.w("tensor", "SS", ss_vevac[i - 2])
                else:
                    b.w("tensor", "SS", ss_bevac)
                def mmv(e, i=i, par2=par2):
                    for (off, ln) in iblk(D):
                        e.matmul(a.psV[par2][:, off:off + ln], a.ones_r[0:1, :],
                                 a.bvB[0:1, off:off + ln], start=True, stop=False)
                    for dc in range(DC):
                        lhsT = a.xnT[:, dc * S + i * 128: dc * S + (i + 1) * 128]
                        for (off, ln) in iblk(D):
                            e.matmul(a.psV[par2][:, off:off + ln], lhsT,
                                     a.wvT[:, dc * 1024 + off: dc * 1024 + off + ln],
                                     start=False, stop=(dc == DC - 1))
                b.t(mmv)
                sm = b.tpub()
                par = stp % 2
                stp += 1
                b.w("scalar", "SM", sm)
                if par in st_of_stage:
                    b.w("scalar", f"dSt{par}", st_of_stage[par])
                b.s(lambda e, par=par, par2=par2: e.copy(a.vst[par2][:], a.psV[par2][:]))
                ss_vevac[i] = b.spub()
                st_of_stage[par] = b.dma("scalar", f"dSt{par}", h.v_scr[i], a.vst[par2][:])

            b.barrier()

            def quant_ladder_w(b, src_f32, cols, sv_gate):
                blocks = cols // 16
                bm3 = src_f32[:, 0:cols].rearrange("p (bl k) -> p bl k", k=16)
                b.v(lambda e: e.tensor_reduce(a.bmW[:, 0:blocks], bm3, AX.X, ALU.max,
                                              apply_absolute_value=True))
                b.v(lambda e: e.reciprocal(a.rcW[:, 0:blocks], a.bmW[:, 0:blocks]))
                b.v(lambda e: e.tensor_scalar(a.rcW[:, 0:blocks], a.rcW[:, 0:blocks],
                                              6.0, None, ALU.mult))
                b.v(lambda e: e.reciprocal(a.shW[:, 0:blocks], a.rcW[:, 0:blocks]))
                b.v(lambda e: e.tensor_scalar(a.shW[:, 0:blocks], a.shW[:, 0:blocks],
                                              1e-12, None, ALU.max))
                b.v(lambda e: e.reciprocal(a.rcW[:, 0:blocks], a.shW[:, 0:blocks]))
                b.v(lambda e: e.tensor_scalar(a.rcW[:, 0:blocks], a.rcW[:, 0:blocks],
                                              a.tieb[:], None, ALU.mult))
                rc_b = a.rcW[:, 0:blocks].unsqueeze(2).broadcast_to([128, blocks, 16])
                sh_b = a.shW[:, 0:blocks].unsqueeze(2).broadcast_to([128, blocks, 16])
                x3 = src_f32[:, 0:cols].rearrange("p (bl k) -> p bl k", k=16)
                xn3 = a.xnW[:, 0:cols].rearrange("p (bl k) -> p bl k", k=16)
                b.v(lambda e: e.tensor_tensor(xn3, x3, rc_b, ALU.mult))
                svx = b.vpub()
                b.w("scalar", "SV", svx)
                b.s(lambda e: e.sign(a.sgW[:, 0:cols], a.xnW[:, 0:cols]))
                b.s(lambda e: e.activation(a.xnW[:, 0:cols], a.xnW[:, 0:cols], AF.Abs))
                sss = b.spub()
                b.w("vector", "SS", sss)
                aw = a.xnW[:, 0:cols]
                b.v(lambda e: e.tensor_scalar(a.acW[:, 0:cols], aw, 0.25, None, ALU.is_gt))
                for th in (0.75, 1.25, 1.75):
                    b.v(lambda e, th=th: e.scalar_tensor_tensor(
                        a.acW[:, 0:cols], aw, th, a.acW[:, 0:cols], ALU.is_gt, ALU.add))
                b.v(lambda e: e.tensor_scalar(a.bbW[:, 0:cols], aw, 2.5, None, ALU.is_gt))
                b.v(lambda e: e.scalar_tensor_tensor(
                    a.bbW[:, 0:cols], aw, 3.5, a.bbW[:, 0:cols], ALU.is_gt, ALU.add))
                b.v(lambda e: e.tensor_scalar(a.dqW[:, 0:cols], aw, 5.0, 2.0,
                                              ALU.is_gt, ALU.mult))
                b.v(lambda e: e.tensor_add(a.bbW[:, 0:cols], a.bbW[:, 0:cols],
                                           a.dqW[:, 0:cols]))
                b.v(lambda e: e.scalar_tensor_tensor(
                    a.bbW[:, 0:cols], a.acW[:, 0:cols], 0.5, a.bbW[:, 0:cols],
                    ALU.mult, ALU.add))
                b.v(lambda e: e.tensor_mul(a.bbW[:, 0:cols], a.bbW[:, 0:cols],
                                           a.sgW[:, 0:cols]))
                dq3 = a.dqW[:, 0:cols].rearrange("p (bl k) -> p bl k", k=16)
                qs3 = a.bbW[:, 0:cols].rearrange("p (bl k) -> p bl k", k=16)
                b.v(lambda e: e.tensor_tensor(dq3, qs3, sh_b, ALU.mult))
                return b.vpub()

            w1_items = ([(h.wg, h.wgT_scr, t, D, DC, 0, 0) for t in range(IC)] +
                        [(h.wu, h.wuT_scr, t, D, DC, 0, 0) for t in range(IC)] +
                        [(h.wd, h.wdT_scr, t, 1408, 11, hh * 1408, hh * 11)
                         for t in range(DC) for hh in range(2)])
            w1s = {"wp": 0, "sm_wtr": 0, "store": {}}

            def w1_load(wp):
                hsrc, scr, t, cols, nchunk, coloff, choff = w1_items[wp]
                par = wp % 2
                if wp >= 2:
                    b.w("sync", "SV", ssn[("w1ld", wp - 2)])
                b.dma("sync", f"dA{par}", a.wf[par][:, 0:cols],
                      hsrc[t * 128:(t + 1) * 128, coloff:coloff + cols])
                ssn[("w1ldc", wp)] = b.C[f"dA{par}"]

            def w1_step():
                wp = w1s["wp"]
                if wp >= len(w1_items):
                    return
                if wp == 0:
                    w1_load(0)
                    w1_load(1)
                hsrc, scr, t, cols, nchunk, coloff, choff = w1_items[wp]
                par = wp % 2
                b.w("vector", f"dA{par}", ssn[("w1ldc", wp)])
                if wp >= 1:
                    b.w("vector", "SM", w1s["sm_wtr"])
                sv_wdeq = quant_ladder_w(b, a.wf[par], cols, None)
                ssn[("w1ld", wp)] = sv_wdeq
                if wp + 2 < len(w1_items):
                    w1_load(wp + 2)
                spar = wp % 2
                if wp >= 1:
                    osp = (wp - 1) % 2
                    b.w("tensor", f"dSt{osp}", w1s["store"].get(osp, 0))
                if wp >= 2:
                    b.w("tensor", f"dSt{spar}", w1s["store"].get(spar, 0))
                rounds = [range(r, min(r + 8, nchunk)) for r in range(0, nchunk, 8)]
                for ri, rng in enumerate(rounds):
                    b.w("tensor", "SV", sv_wdeq)
                    if ri > 0 or wp > 0:
                        b.w("tensor", "SV", ssn.get(("w1evac_last",), 0))
                    def trs(e, rng=rng):
                        for j, ch in enumerate(rng):
                            e.transpose(a.psWT[:, j * 128:(j + 1) * 128],
                                        a.dqW[:, ch * 128:(ch + 1) * 128], a.ident[:])
                    b.t(trs)
                    smx = b.tpub()
                    w1s["sm_wtr"] = smx
                    b.w("vector", "SM", smx)
                    def evac(e, rng=rng):
                        n = len(rng) * 128
                        c0 = rng[0] * 128
                        e.tensor_copy(a.wst[0][:, c0:c0 + n], a.psWT[:, 0:n])
                    b.v(evac)
                    ssn[("w1evac_last",)] = b.vpub()
                b.w("scalar", "SV", ssn[("w1evac_last",)])
                dst = scr[choff:choff + nchunk, :, t * 128:(t + 1) * 128].rearrange(
                    "ch p j -> p ch j")
                src = a.wst[0][:, 0:nchunk * 128].rearrange(
                    "p (ch j) -> p ch j", j=128)
                w1s["store"][spar] = b.dma("scalar", f"dSt{spar}", dst, src)
                w1s["wp"] = wp + 1

            b.v(lambda e: e.memset(a.zmat[:], 1.0))
            sv_zinit = b.vpub()
            b.w("sync", "SS")
            sv_x2 = {}
            ss_x2sq = {}
            st_x2 = {}
            sm_scores = {}
            ss_exp = {}
            ss_zevac = {}
            sm_bc = {}
            sv_normt = {}
            sm_group_end = 0
            xr_cnt = {}
            zspread_last = 0
            zrecip_last = 0
            zflat_busy = 0
            ktdone_last = {}
            sm_pv_last = {}
            sm_pv_end = 0
            sv_x2_last = 0
            for g in range(NG):
                if g > 0:
                    b.w("sync", "SM", sm_group_end)
                qsrc = h.qT_scr[:, :, g * GQ:(g + 1) * GQ]
                b.dma("sync", "dQt", a.qTg[:].rearrange("p (dc q) -> p dc q", q=GQ),
                      qsrc.rearrange("dc p q -> p dc q"))
                dqt_cnt = b.C["dQt"]
                ktb_cnt = {}
                for t in range(NT):
                    tb = t // ZR
                    pb = tb % 2
                    if t % ZR == 0:
                        if pb in ktdone_last:
                            b.w("sync", "SM", ktdone_last[pb])
                        ksrc = h.kT_scr[:, :, tb * ZR * 128:(tb + 1) * ZR * 128]
                        b.dma("sync", f"dKt{pb}",
                              a.ktb[pb][:].rearrange("p (dc tj) -> p dc tj", tj=ZR * 128),
                              ksrc.rearrange("dc p tj -> p dc tj"))
                        ktb_cnt[tb] = b.C[f"dKt{pb}"]
                    pS = t % 2
                    b.w("tensor", f"dKt{pb}", ktb_cnt[tb])
                    b.w("tensor", "dQt", dqt_cnt)
                    if (g, t - 2) in ss_zevac:
                        b.w("tensor", "SS", ss_zevac[(g, t - 2)])
                    elif g > 0 and t < 2:
                        b.w("tensor", "SS", ss_zevac[(g - 1, NT - 1)])
                        b.w("tensor", "SV", sv_x2_last)
                    def mms(e, t=t, pS=pS, pb=pb):
                        toff = (t % ZR) * 128
                        for dc in range(DC):
                            lhsT = a.ktb[pb][:, dc * ZR * 128 + toff:
                                             dc * ZR * 128 + toff + 128]
                            for (off, ln) in iblk(GQ):
                                e.matmul(a.psS[pS][:, off:off + ln], lhsT,
                                         a.qTg[:, dc * GQ + off: dc * GQ + off + ln],
                                         start=(dc == 0), stop=(dc == DC - 1))
                    b.t(mms)
                    sm_scores[(g, t)] = b.tpub()
                    if t == NT - 1:
                        sm_group_end = sm_scores[(g, t)]
                    b.w("scalar", "SM", sm_scores[(g, t)])
                    if t == 0 and g > 0:
                        b.w("scalar", "SM", sm_pv_end)
                    b.s(lambda e, t=t, pS=pS: e.activation(
                        a.pT[:, t * GQ:(t + 1) * GQ], a.psS[pS][:], AF.Exp))
                    ss_exp[(g, t)] = b.spub()
                    b.w("tensor", "SS", ss_exp[(g, t)])
                    def mmz(e, t=t, pS=pS):
                        for (off, ln) in iblk(GQ):
                            e.matmul(a.psS[pS][0:1, off:off + ln], a.ones_c[:],
                                     a.pT[:, t * GQ + off: t * GQ + off + ln],
                                     start=True, stop=True)
                    b.t(mmz)
                    smz = b.tpub()
                    b.w("scalar", "SM", smz)
                    b.w("scalar", "SV", zrecip_last if zrecip_last else sv_zinit)
                    b.s(lambda e, t=t, pS=pS: e.activation(
                        a.zmat[(t % ZR) * 32:(t % ZR) * 32 + 1, :],
                        a.psS[pS][0:1, :], AF.Copy, scale=32.0))
                    ss_zevac[(g, t)] = b.spub()
                    w1_step()
                    if t % ZR == ZR - 1:
                        ktdone_last[pb] = sm_scores[(g, t)]
                        r = t // ZR
                        b.w("vector", "SS", ss_zevac[(g, t)])
                        b.v(lambda e: e.tensor_scalar(a.zmat[:], a.zmat[:],
                                                      a.cnrm[0:64, :], None, ALU.mult))
                        b.v(lambda e: e.reciprocal(a.zmat[:], a.zmat[:]))
                        b.w("vector", "SM", zflat_busy)
                        b.v(lambda e: e.tensor_copy(a.zmatB[:], a.zmat[:]))
                        zrecip_last = b.vpub()
                        for tt in range(r * ZR, (r + 1) * ZR):
                            b.w("tensor", "SV", zrecip_last)
                            if (g, tt - 1) in sv_normt:
                                b.w("tensor", "SV", sv_normt[(g, tt - 1)])
                            def mmb(e, tt=tt):
                                ro = (tt % ZR) * 32
                                for (off, ln) in iblk(GQ):
                                    e.matmul(a.psB[0][:, off:off + ln],
                                             a.ones_r[ro:ro + 1, :],
                                             a.zmatB[ro:ro + 1, off:off + ln],
                                             start=True, stop=True)
                            b.t(mmb)
                            sm_bc[(g, tt)] = b.tpub()
                            b.w("vector", "SM", sm_bc[(g, tt)])
                            b.w("vector", "SS", ss_exp[(g, tt)])
                            def nrm(e, tt=tt):
                                sl = a.pT[:, tt * GQ:(tt + 1) * GQ]
                                e.tensor_tensor(sl, sl, a.psB[0][:], ALU.mult)
                            b.v(nrm)
                            sv_normt[(g, tt)] = b.vpub()
                        zflat_busy = sm_bc[(g, (r + 1) * ZR - 1)]

                ss_oT = 0
                vt_cnt = {}
                WAVES = [(0, 3), (3, 6), (6, 8)]
                for wave, (dlo, dhi) in enumerate(WAVES):
                    for t in range(NT):
                        pv = t % 2
                        if pv in sm_pv_last:
                            b.w("sync", "SM", sm_pv_last[pv])
                        vt_cnt[(wave, t)] = b.dma("sync", f"dVt{pv}",
                                                  a.vrn[pv][:], h.v_scr[t])
                        b.w("tensor", f"dVt{pv}", vt_cnt[(wave, t)])
                        if wave == 0 and t == 0:
                            b.w("tensor", "SV", sv_normt[(g, NT - 1)])
                            if g > 0:
                                b.w("tensor", "SS", ssn[("afdone", g - 1)])
                        if wave > 0 and t == 0:
                            b.w("tensor", "SS", ss_oT)
                        def mpv(e, dlo=dlo, dhi=dhi, t=t, pv=pv):
                            for dcw in range(dhi - dlo):
                                dc = dlo + dcw
                                lhsT = a.vrn[pv][:, dc * 128:(dc + 1) * 128]
                                for (off, ln) in iblk(GQ):
                                    e.matmul(a.psO[dcw][:, off:off + ln], lhsT,
                                             a.pT[:, t * GQ + off: t * GQ + off + ln],
                                             start=(t == 0), stop=(t == NT - 1))
                        b.t(mpv)
                        sm_pv_last[pv] = b.tpub()
                    b.w("scalar", "SM", sm_pv_last[(NT - 1) % 2])
                    def oevac(e, dlo=dlo, dhi=dhi):
                        for dcw in range(dhi - dlo):
                            dc = dlo + dcw
                            e.copy(a.oT[:, dc * GQ:(dc + 1) * GQ], a.psO[dcw][:])
                    b.s(oevac)
                    ss_oT = b.spub()
                sm_pv_end = sm_pv_last[(NT - 1) % 2]

                for qt in range(TPG):
                    pq = qt % 2
                    row0 = g * GQ + qt * 128
                    prev = (g, qt - 1) if qt >= 1 else ((g - 1, TPG - 1) if g > 0 else None)
                    if prev is not None:
                        b.w("sync", "SV", sv_x2[prev])
                        b.w("sync", "SS", ss_x2sq[prev])
                    xr_cnt[(g, qt)] = b.dma("sync", f"dXr{pq}", a.xres[0][:],
                                            h.x[row0:row0 + 128, :])
                    b.w("tensor", "SS", ss_oT)
                    if (g, qt - 2) in sv_x2:
                        b.w("tensor", "SV", sv_x2[(g, qt - 2)])
                    elif g > 0:
                        b.w("tensor", "SV", sv_x2[(g - 1, TPG - 2 + qt)])
                    def mwo(e, qt=qt, pq=pq):
                        for (off, ln) in iblk(D):
                            e.matmul(a.psAF[pq][:, off:off + ln], a.ones_r[0:1, :],
                                     a.boB[0:1, off:off + ln], start=True, stop=False)
                        for kc in range(DC):
                            lhsT = a.oT[:, kc * GQ + qt * 128: kc * GQ + (qt + 1) * 128]
                            for (off, ln) in iblk(D):
                                e.matmul(a.psAF[pq][:, off:off + ln], lhsT,
                                         a.woT[:, kc * 1024 + off: kc * 1024 + off + ln],
                                         start=False, stop=(kc == DC - 1))
                    b.t(mwo)
                    smw = b.tpub()
                    if g == NG - 1 and qt == TPG - 1:
                        ssn[("afdone_last",)] = smw
                    b.w("vector", "SM", smw)
                    b.w("vector", f"dXr{pq}", xr_cnt[(g, qt)])
                    if prev is not None:
                        b.w("vector", f"dX2{prev[1] % 2}", st_x2[prev])
                        b.w("vector", "SS", ss_x2sq[prev])
                    b.v(lambda e, pq=pq: e.scalar_tensor_tensor(
                        a.x2st[0][:], a.psAF[pq][:], 1.0, a.xres[0][:],
                        ALU.mult, ALU.add))
                    idx = g * TPG + qt
                    b.v(lambda e, idx=idx: e.tensor_reduce(
                        a.sD1[:, idx:idx + 1], a.x2st[0][:], AX.X, ALU.add))
                    sv_x2[(g, qt)] = b.vpub()
                    sv_x2_last = sv_x2[(g, qt)]
                    b.w("scalar", "SV", sv_x2[(g, qt)])
                    b.s(lambda e, idx=idx: e.activation(
                        a.xres[0][:], a.x2st[0][:], AF.Square, accum_out=a.sD2[:, idx:idx + 1]))
                    ss_x2sq[(g, qt)] = b.spub()
                    st_x2[(g, qt)] = b.dma("scalar", f"dX2{pq}",
                                           h.x2_scr[row0:row0 + 128, :], a.x2st[0][:])
                ssn[("afdone", g)] = ss_x2sq[(g, TPG - 1)]

            while w1s["wp"] < len(w1_items):
                w1_step()

            b.barrier()
            b.w("sync", "dX20", b.C["dX20"])
            b.w("sync", "dX21", b.C["dX21"])

            sv_lnD = ln_batch(b, a.sD1, a.sD2, a.rD, a.nD, a.tmp1)
            b.dma("sync", "dDw", a.wgT[:].rearrange("p (dc i) -> p dc i", i=I),
                  h.wgT_scr.rearrange("dc p i -> p dc i"))
            b.dma("sync", "dDw", a.wuT[:].rearrange("p (dc i) -> p dc i", i=I),
                  h.wuT_scr.rearrange("dc p i -> p dc i"))
            b.dma("sync", "dDw", a.wdT[:].rearrange("p (ic d) -> p ic d", d=D),
                  h.wdT_scr.rearrange("ic p d -> p ic d"))
            dw_cnt = b.C["dDw"]

            def quant_ladder_a(b, src_bf, cols):
                blocks = cols // 16
                s3 = src_bf[:, 0:cols].rearrange("p (bl k) -> p bl k", k=16)
                b.vn(lambda e: e.tensor_reduce(a.bmD[:, 0:blocks], s3, AX.X, ALU.max,
                                               apply_absolute_value=True))
                b.vd()
                b.vn(lambda e: e.reciprocal(a.rcD[:, 0:blocks], a.bmD[:, 0:blocks]))
                b.vd()
                b.vn(lambda e: e.tensor_scalar(a.rcD[:, 0:blocks], a.rcD[:, 0:blocks],
                                               6.0, None, ALU.mult))
                b.vn(lambda e: e.tensor_scalar(a.shD[:, 0:blocks], a.bmD[:, 0:blocks],
                                               1.0 / 6.0, None, ALU.mult))
                b.vd()
                rc_b = a.rcD[:, 0:blocks].unsqueeze(2).broadcast_to([128, blocks, 16])
                sh_b = a.shD[:, 0:blocks].unsqueeze(2).broadcast_to([128, blocks, 16])
                xn3 = a.xnD[:, 0:cols].rearrange("p (bl k) -> p bl k", k=16)
                b.v(lambda e: e.tensor_tensor(xn3, s3, rc_b, ALU.mult))
                svx = b.vpub()
                b.w("scalar", "SV", svx)
                b.s(lambda e: e.sign(a.sgnD[:, 0:cols], a.xnD[:, 0:cols]))
                b.s(lambda e: e.activation(a.xnD[:, 0:cols], a.xnD[:, 0:cols], AF.Abs))
                sss = b.spub()
                b.w("vector", "SS", sss)
                aw = a.xnD[:, 0:cols]
                AC, BB, C1, C2_, SG = (a.acD[:, 0:cols], a.bbD[:, 0:cols],
                                       a.cD1[:, 0:cols], a.cD2[:, 0:cols],
                                       a.sgD[:, 0:cols])
                b.vn(lambda e: e.tensor_scalar(AC, aw, 0.25, None, ALU.is_gt))
                b.vn(lambda e: e.tensor_scalar(BB, aw, 0.75, None, ALU.is_gt))
                b.vn(lambda e: e.tensor_scalar(C1, aw, 1.25, None, ALU.is_gt))
                b.vn(lambda e: e.tensor_scalar(C2_, aw, 1.75, None, ALU.is_gt))
                b.vd()
                b.vn(lambda e: e.tensor_add(AC, AC, BB))
                b.vn(lambda e: e.tensor_add(C1, C1, C2_))
                b.vn(lambda e: e.tensor_scalar(SG, aw, 2.5, None, ALU.is_gt))
                b.vd()
                b.vn(lambda e: e.tensor_add(AC, AC, C1))
                b.vn(lambda e: e.tensor_scalar(BB, aw, 3.5, None, ALU.is_gt))
                b.vn(lambda e: e.tensor_scalar(C2_, aw, 5.0, 2.0, ALU.is_gt, ALU.mult))
                b.vd()
                b.vn(lambda e: e.tensor_add(BB, BB, SG))
                b.vd()
                b.vn(lambda e: e.tensor_add(BB, BB, C2_))
                b.vd()
                b.vn(lambda e: e.scalar_tensor_tensor(BB, AC, 0.5, BB, ALU.mult, ALU.add))
                b.vd()
                b.vn(lambda e: e.tensor_mul(BB, BB, a.sgnD[:, 0:cols]))
                b.vd()
                dq3 = a.sgD[:, 0:cols].rearrange("p (bl k) -> p bl k", k=16)
                qs3 = a.bbD[:, 0:cols].rearrange("p (bl k) -> p bl k", k=16)
                b.v(lambda e: e.tensor_tensor(dq3, qs3, sh_b, ALU.mult))
                return b.vpub()

            xl_cnt = {}
            st_h2 = {}
            for i in range(NT):
                par = i % 2
                if i >= 2:
                    b.w("sync", "SV", ssn[("h2xn", i - 2)])
                xl_cnt[i] = b.dma("sync", f"dXl{par}", a.x2t[par][:],
                                  h.x2_scr[i * 128:(i + 1) * 128, :])
                b.w("scalar", f"dXl{par}", xl_cnt[i])
                b.w("scalar", "SV", sv_lnD)
                if i >= 1:
                    b.w("scalar", "SV", ssn[("h2xn", i - 1)])
                b.s(lambda e, i=i, par=par: e.activation(
                    a.h2b, a.x2t[par][:], AF.Identity,
                    bias=a.nD[:, i:i + 1], scale=a.rD[:, i:i + 1]))
                ss_h2 = b.spub()
                b.w("vector", "SS", ss_h2)
                if i >= 1:
                    b.w("vector", "SM", ssn[("h2tr", i - 1)])
                sv_q2c = quant_ladder_a(b, a.h2b, D)
                ssn[("h2xn", i)] = sv_q2c
                b.w("tensor", "SV", sv_q2c)
                if i >= 1:
                    b.w("tensor", "SV", ssn[("h2ev", i - 1)])
                def tr2(e):
                    for dc in range(DC):
                        e.transpose(a.psT2[:, dc * 128:(dc + 1) * 128],
                                    a.sgD[:, dc * 128:(dc + 1) * 128], a.ident[:])
                b.t(tr2)
                smt = b.tpub()
                ssn[("h2tr", i)] = smt
                b.w("vector", "SM", smt)
                pq = i % 2
                if i >= 2:
                    b.w("vector", f"dSt{pq}", st_h2[i - 2])
                b.v(lambda e, pq=pq: e.tensor_copy(a.h2qT[pq][:], a.psT2[:]))
                ssn[("h2ev", i)] = b.vpub()
                b.w("scalar", "SV", ssn[("h2ev", i)])
                st_h2[i] = b.dma("scalar", f"dSt{pq}", h.h2qT_scr[i], a.h2qT[pq][:])

            b.dall_v = False
            y_cnt = {}
            sv_y = {}
            h2l_cnt = {}
            for i in range(NT):
                par = i % 2
                if i >= 2:
                    b.w("sync", "SM", ssn[("up_done", i - 2)])
                else:
                    b.w("sync", f"dSt{par}", st_h2[NT - 2 + par])
                h2l_cnt[i] = b.dma("sync", f"dKt{par}", a.h2qT[par][:], h.h2qT_scr[i])
                if i >= 2:
                    b.w("sync", "SV", sv_y[i - 2])
                xl2 = b.dma("sync", f"dXl{par}", a.x2t[par][:],
                            h.x2_scr[i * 128:(i + 1) * 128, :])
                ssn[("xl2", i)] = xl2
                b.w("tensor", f"dKt{par}", h2l_cnt[i])
                b.w("tensor", "dDw", dw_cnt)
                if i >= 1:
                    b.w("tensor", "SV", ssn[("h3mul", i - 1)])
                def mg(e, par=par):
                    for dc in range(DC):
                        lhsT = a.h2qT[par][:, dc * 128:(dc + 1) * 128]
                        for (off, ln) in iblk(I):
                            e.matmul(a.psGU[:, off:off + ln], lhsT,
                                     a.wgT[:, dc * I + off: dc * I + off + ln],
                                     start=(dc == 0), stop=(dc == DC - 1))
                b.t(mg)
                smg = b.tpub()
                b.w("scalar", "SM", smg)
                if i >= 1:
                    b.w("scalar", "SM", ssn[("tr3", i - 1)])
                b.s(lambda e: e.activation(a.sgD[:, 0:I], a.psGU[:, 0:I], AF.Silu))
                ss_sg = b.spub()
                b.w("tensor", "SS", ss_sg)
                def mu(e, par=par):
                    for dc in range(DC):
                        lhsT = a.h2qT[par][:, dc * 128:(dc + 1) * 128]
                        for (off, ln) in iblk(I):
                            e.matmul(a.psGU[:, off:off + ln], lhsT,
                                     a.wuT[:, dc * I + off: dc * I + off + ln],
                                     start=(dc == 0), stop=(dc == DC - 1))
                b.t(mu)
                smu = b.tpub()
                ssn[("up_done", i)] = smu
                b.w("vector", "SM", smu)
                if i >= 1:
                    b.w("vector", "SV", ssn[("h3q", i - 1)])
                b.v(lambda e: e.tensor_tensor(a.h3D[:, 0:I], a.psGU[:, 0:I],
                                              a.sgD[:, 0:I], ALU.mult))
                ssn[("h3mul", i)] = b.vpub()
                b.w("vector", "SV", ssn[("h3mul", i)])
                sv_q3 = quant_ladder_a(b, a.h3D, I)
                ssn[("h3q", i)] = sv_q3
                rounds = [range(r, min(r + 8, IC)) for r in range(0, IC, 8)]
                last_evac = 0
                for ri, rng in enumerate(rounds):
                    b.w("tensor", "SV", sv_q3 if ri == 0 else last_evac)
                    if ri == 0 and i >= 1:
                        b.w("tensor", "SV", ssn[("h3ev", i - 1)])
                        b.w("tensor", "SV", sv_y[i - 1])
                    def tr3(e, rng=rng):
                        for j, ch in enumerate(rng):
                            e.transpose(a.psT2[:, j * 128:(j + 1) * 128],
                                        a.sgD[:, ch * 128:(ch + 1) * 128], a.ident[:])
                    b.t(tr3)
                    smr = b.tpub()
                    b.w("vector", "SM", smr)
                    def ev3(e, rng=rng):
                        n = len(rng) * 128
                        c0 = rng[0] * 128
                        e.tensor_copy(a.h3qT[:, c0:c0 + n], a.psT2[:, 0:n])
                    b.v(ev3)
                    last_evac = b.vpub()
                ssn[("tr3", i)] = b.C["SM"]
                ssn[("h3ev", i)] = last_evac
                b.w("tensor", "SV", last_evac)
                if i >= 1:
                    b.w("tensor", "SV", sv_y[i - 1])
                def md(e):
                    for ic in range(IC):
                        lhsT = a.h3qT[:, ic * 128:(ic + 1) * 128]
                        for (off, ln) in iblk(D):
                            e.matmul(a.psD[:, off:off + ln], lhsT,
                                     a.wdT[:, ic * D + off: ic * D + off + ln],
                                     start=(ic == 0), stop=(ic == IC - 1))
                b.t(md)
                smd = b.tpub()
                b.w("vector", "SM", smd)
                b.w("vector", f"dXl{par}", ssn[("xl2", i)])
                if i >= 2:
                    b.w("vector", f"dY{par}", y_cnt[i - 2])
                b.v(lambda e, par=par: e.scalar_tensor_tensor(
                    a.yst[par][:], a.psD[:], 1.0, a.x2t[par][:], ALU.mult, ALU.add))
                sv_y[i] = b.vpub()
                b.w("scalar", "SV", sv_y[i])
                y_cnt[i] = b.dma("scalar", f"dY{par}",
                                 h.y[i * 128:(i + 1) * 128, :], a.yst[par][:])

            b.w("scalar", "dY0", b.C["dY0"])
            b.w("scalar", "dY1", b.C["dY1"])

        with nc.Block() as block:
            refC = {}

            def mk(E):
                def body(eng):
                    bb = B(E, eng, sems, safe=safe)
                    schedule(bb)
                    refC[E] = dict(bb.C)
                return body
            block.sync(mk("sync"))
            block.tensor(mk("tensor"))
            block.vector(mk("vector"))
            block.scalar(mk("scalar"))
            block.gpsimd(lambda eng: None)
            vals = list(refC.values())
            for v in vals[1:]:
                assert v == vals[0], "pass counter mismatch"

    return nc


_NC_CACHE = {}


def _get_nc(S):
    if S not in _NC_CACHE:
        _NC_CACHE[S] = build(S)
    return _NC_CACHE[S]


def make_inputs(x_b, wq, bq, wk, bk, wv, bv, wo, bo, wg, wu, wd):
    return {
        "x": np.ascontiguousarray(x_b, dtype=np.float32),
        "wq": np.ascontiguousarray(wq, dtype=np.float32),
        "wk": np.ascontiguousarray(wk, dtype=np.float32),
        "wv": np.ascontiguousarray(wv, dtype=np.float32),
        "wo": np.ascontiguousarray(wo, dtype=np.float32),
        "wg": np.ascontiguousarray(wg, dtype=np.float32),
        "wu": np.ascontiguousarray(wu, dtype=np.float32),
        "wd": np.ascontiguousarray(wd, dtype=np.float32),
        "bqT": np.ascontiguousarray(np.asarray(bq, dtype=np.float32).reshape(DC, 128).T),
        "bkT": np.ascontiguousarray(np.asarray(bk, dtype=np.float32).reshape(DC, 128).T),
        "bv_row": np.ascontiguousarray(np.asarray(bv, dtype=np.float32).reshape(1, D)),
        "bo_row": np.ascontiguousarray(np.asarray(bo, dtype=np.float32).reshape(1, D)),
        "ident": np.eye(128, dtype=ml_dtypes.bfloat16),
    }


def kernel(L0_attn_batch, L0_attn_Wq, L0_attn_bq, L0_attn_Wk, L0_attn_bk,
           L0_attn_Wv, L0_attn_bv, L0_attn_Wo, L0_attn_bo,
           L0_ffn_w_gate, L0_ffn_w_up, L0_ffn_w_down):
    xs = np.ascontiguousarray(np.asarray(L0_attn_batch, dtype=np.float32))
    Bn, S, Dm = xs.shape
    assert Dm == D
    nc = _get_nc(S)
    in_maps = []
    for c in range(Bn):
        in_maps.append(make_inputs(xs[c], L0_attn_Wq, L0_attn_bq, L0_attn_Wk,
                                   L0_attn_bk, L0_attn_Wv, L0_attn_bv,
                                   L0_attn_Wo, L0_attn_bo,
                                   L0_ffn_w_gate, L0_ffn_w_up, L0_ffn_w_down))
    res = run_bass_kernel_spmd(nc, in_maps, core_ids=list(range(Bn)))
    outs = [np.asarray(r["y"]) for r in res.results]
    if outs[0].dtype != np.float64:
        outs = [o.astype(np.float32, copy=False) for o in outs]
    return np.stack(outs)


# version 29
# speedup vs baseline: 1.0255x; 1.0255x over previous
import numpy as np
import ml_dtypes
from contextlib import ExitStack
from types import SimpleNamespace

import concourse.bass as bass
import concourse.mybir as mybir
from concourse.bass_utils import run_bass_kernel_spmd

F32 = mybir.dt.float32
BF16 = mybir.dt.bfloat16
AF = mybir.ActivationFunctionType
ALU = mybir.AluOpType
AX = mybir.AxisListType

D = 1024
I = 2816
DC = D // 128
IC = I // 128
NBD = D // 512
NBI = (I + 511) // 512
ZR = 2
LNEPS = 1e-5
RECIP_NORM = float(np.float32(32.000001))


def iblk(n):
    out = []
    o = 0
    while o < n:
        out.append((o, min(512, n - o)))
        o += 512
    return out


class Cap:

    def __init__(self, eng):
        object.__setattr__(self, "_e", eng)
        object.__setattr__(self, "last", None)

    def __getattr__(self, nm):
        a = getattr(self._e, nm)
        if callable(a):
            def w(*args, **kw):
                r = a(*args, **kw)
                if hasattr(r, "then_inc"):
                    object.__setattr__(self, "last", r)
                return r
            return w
        return a


class B:

    def __init__(self, E, eng, sems, safe=False):
        self.E = E
        self.eng = eng
        self.cap = Cap(eng) if eng is not None else None
        self.safe = safe
        self.sems = sems
        self.C = {k: 0 for k in sems}

    def w(self, engine, sem, val=None):
        if val is None:
            val = self.C[sem]
        if self.E == engine and val > 0:
            self.eng.wait_ge(self.sems[sem], int(val))

    def dma(self, issuer, sem, out, in_):
        self.C[sem] += 16
        if self.E == issuer:
            if issuer == "scalar":
                self.cap.drain()
            self.eng.dma_start(out=out, in_=in_).then_inc(self.sems[sem], 16)
        return self.C[sem]

    def t(self, fn):
        if self.E == "tensor":
            fn(self.cap)

    def tpub(self):
        self.C["SM"] += 1
        if self.E == "tensor":
            self.cap.last.then_inc(self.sems["SM"], 1)
        return self.C["SM"]

    dall_v = False
    dall_s = False

    def v(self, fn):
        if self.E == "vector":
            fn(self.cap)
            if self.dall_v:
                self.cap.drain()

    def vn(self, fn):
        if self.E == "vector":
            fn(self.cap)

    def vd(self):
        if self.E == "vector":
            self.cap.drain()

    def vpub(self):
        self.C["SV"] += 1
        if self.E == "vector":
            self.cap.drain()
            self.cap.last.then_inc(self.sems["SV"], 1)
        return self.C["SV"]

    def s(self, fn):
        if self.E == "scalar":
            fn(self.cap)
            if self.dall_s:
                self.cap.drain()

    def spub(self):
        self.C["SS"] += 1
        if self.E == "scalar":
            self.cap.drain()
            self.cap.last.then_inc(self.sems["SS"], 1)
        return self.C["SS"]

    def barrier(self):
        for e in ("sync", "tensor", "vector", "scalar"):
            for sem in self.sems:
                self.w(e, sem)


def build(S, safe=False):
    NT = S // 128
    GQ = min(1024, S)
    NG = S // GQ
    TPG = GQ // 128
    NBG = GQ // 512
    NTB = NT // ZR

    nc = bass.Bass("TRN2", target_bir_lowering=False, debug=False,
                   detect_race_conditions=bool(safe))

    h = SimpleNamespace()
    h.x = nc.dram_tensor("x", [S, D], F32, kind="ExternalInput").ap()
    for nm, shp in [("wq", [D, D]), ("wk", [D, D]), ("wv", [D, D]), ("wo", [D, D]),
                    ("wg", [I, D]), ("wu", [I, D]), ("wd", [D, I])]:
        setattr(h, nm, nc.dram_tensor(nm, shp, F32, kind="ExternalInput").ap())
    h.bqT = nc.dram_tensor("bqT", [128, DC], F32, kind="ExternalInput").ap()
    h.bkT = nc.dram_tensor("bkT", [128, DC], F32, kind="ExternalInput").ap()
    h.bv = nc.dram_tensor("bv_row", [1, D], F32, kind="ExternalInput").ap()
    h.bo = nc.dram_tensor("bo_row", [1, D], F32, kind="ExternalInput").ap()
    h.ident = nc.dram_tensor("ident", [128, 128], BF16, kind="ExternalInput").ap()
    h.y = nc.dram_tensor("y", [S, D], F32, kind="ExternalOutput").ap()

    h.qT_scr = nc.dram_tensor("qT_scr", [DC, 128, S], BF16, kind="Internal").ap()
    h.kT_scr = nc.dram_tensor("kT_scr", [DC, 128, S], BF16, kind="Internal").ap()
    h.v_scr = nc.dram_tensor("v_scr", [NT, 128, D], BF16, kind="Internal").ap()
    h.x2_scr = nc.dram_tensor("x2_scr", [S, D], F32, kind="Internal").ap()
    h.h2qT_scr = nc.dram_tensor("h2qT_scr", [NT, 128, D], BF16, kind="Internal").ap()
    h.wgT_scr = nc.dram_tensor("wgT_scr", [DC, 128, I], BF16, kind="Internal").ap()
    h.wuT_scr = nc.dram_tensor("wuT_scr", [DC, 128, I], BF16, kind="Internal").ap()
    h.wdT_scr = nc.dram_tensor("wdT_scr", [IC, 128, D], BF16, kind="Internal").ap()

    ctx = ExitStack()
    a = SimpleNamespace()

    def sb(name, shape, dt, scope=ctx):
        return scope.enter_context(nc.sbuf_tensor(name, shape, dt))

    def ps(name, shape, dt, scope):
        return scope.enter_context(nc.psum_tensor(name, shape, dt))

    sem_names = ["SM", "SV", "SS",
                 "dA0", "dA1", "dW", "dM", "dSt0", "dSt1",
                 "dKt0", "dKt1", "dQt", "dVt0", "dVt1", "dXr0", "dXr1",
                 "dX20", "dX21", "dZ", "dDw", "dXl0", "dXl1", "dY0", "dY1"]

    with ctx:
        sems = {nm: ctx.enter_context(nc.semaphore(nm)) for nm in sem_names}

        a.ident = sb("identB", [128, 128], BF16)
        a.ones_c = sb("ones_c", [128, 1], BF16)
        a.ones_r = sb("ones_r", [64, 128], BF16)
        a.bqT = sb("bqT_s", [128, DC], F32)
        a.bkT = sb("bkT_s", [128, DC], F32)

        a.sD1 = sb("sD1", [128, NT], F32)
        a.sD2 = sb("sD2", [128, NT], F32)
        a.rD = sb("rD", [128, NT], F32)
        a.nD = sb("nD", [128, NT], F32)
        a.tmp1 = sb("tmp1", [128, NT], F32)
        a.tieb = sb("tieb", [128, 1], F32)
        a.cnrm = sb("cnrm", [128, 1], F32)

        sLong = ExitStack()
        a.woT = sb("woT", [128, DC * 1024], BF16, sLong)
        a.bvB = sb("bvB", [1, D], BF16, sLong)
        a.boB = sb("boB", [1, D], BF16, sLong)
        with ExitStack() as sAB:
            a.w0f = [sb(f"w0f{i}", [128, 1024], F32, sAB) for i in range(2)]
            a.w0b = sb("w0b", [128, 1024], BF16, sAB)
            a.bv32 = sb("bv32", [1, D], F32, sAB)
            a.bo32 = sb("bo32", [1, D], F32, sAB)
            a.xnT = sb("xnT", [128, DC * S], BF16, sAB)
            a.sA1 = sb("sA1", [128, NT], F32, sAB)
            a.sA2 = sb("sA2", [128, NT], F32, sAB)
            a.rA = sb("rA", [128, NT], F32, sAB)
            a.nA = sb("nA", [128, NT], F32, sAB)
            a.wqT = sb("wqT", [128, DC * 1024], BF16, sAB)
            a.wkT = sb("wkT", [128, DC * 1024], BF16, sAB)
            a.wvT = sb("wvT", [128, DC * 1024], BF16, sAB)
            a.xb = [sb(f"xb{i}", [128, D], F32, sAB) for i in range(2)]
            a.xn = [sb(f"xn{i}", [128, D], BF16, sAB) for i in range(2)]
            a.stageB = [sb(f"stageB{i}", [128, S], BF16, sAB) for i in range(2)]
            a.vst = [sb(f"vst{i}", [128, D], BF16, sAB) for i in range(2)]
            a.sqd = sb("sqdA", [128, D], F32, sAB)
        with ExitStack() as sC:
            a.wf = [sb(f"wf{i}", [128, 1408], F32, sC) for i in range(2)]
            a.bmW = sb("bmW", [128, IC * 8], F32, sC)
            a.shW = sb("shW", [128, IC * 8], F32, sC)
            a.rcW = sb("rcW", [128, IC * 8], F32, sC)
            a.xnW = sb("xnW", [128, 1408], F32, sC)
            a.sgW = sb("sgW", [128, 1408], BF16, sC)
            a.acW = sb("acW", [128, 1408], BF16, sC)
            a.bbW = sb("bbW", [128, 1408], BF16, sC)
            a.dqW = sb("dqW", [128, 1408], BF16, sC)
            a.cW1 = sb("cW1", [128, 1408], BF16, sC)
            a.cW2 = sb("cW2", [128, 1408], BF16, sC)
            a.wst = [sb("wst0", [128, 1408], BF16, sC)]
            a.pT = sb("pT", [128, NT * GQ], BF16, sC)
            a.qTg = sb("qTg", [128, DC * GQ], BF16, sC)
            a.ktb = [sb(f"ktb{i}", [128, ZR * DC * 128], BF16, sC) for i in range(2)]
            a.vrn = [sb(f"vrn{i}", [128, D], BF16, sC) for i in range(2)]
            a.oT = sb("oT", [128, DC * GQ], BF16, sC)
            a.zmat = sb("zmat", [64, GQ], F32, sC)
            a.zmatB = sb("zmatB", [64, GQ], BF16, sC)
            a.xres = [sb("xres0", [128, D], F32, sC)]
            a.x2st = [sb("x2st0", [128, D], F32, sC)]

        sLong.close()
        with ExitStack() as sD:
            a.wgT = sb("wgT", [128, DC * I], BF16, sD)
            a.wuT = sb("wuT", [128, DC * I], BF16, sD)
            a.wdT = sb("wdT", [128, IC * D], BF16, sD)
            a.x2t = [sb(f"x2t{i}", [128, D], F32, sD) for i in range(2)]
            a.h2qT = [sb(f"h2qT{i}", [128, D], BF16, sD) for i in range(2)]
            a.bmD = sb("bmD", [128, IC * 8], F32, sD)
            a.shD = sb("shD", [128, IC * 8], F32, sD)
            a.rcD = sb("rcD", [128, IC * 8], F32, sD)
            a.xnD = sb("xnD", [128, I], BF16, sD)
            a.sgnD = sb("sgnD", [128, I], BF16, sD)
            a.acD = sb("acD", [128, I], BF16, sD)
            a.bbD = sb("bbD", [128, I], BF16, sD)
            a.cD1 = sb("cD1", [128, I], BF16, sD)
            a.cD2 = sb("cD2", [128, I], BF16, sD)
            a.h2b = a.cD2[:, 0:D]
            a.h3D = a.cD1
            a.sgD = sb("sgD", [128, I], BF16, sD)
            a.h3qT = sb("h3qT", [128, I], BF16, sD)
            a.yst = [sb(f"yst{i}", [128, D], F32, sD) for i in range(2)]

        pLong = ExitStack()
        a.psWT = ps("psWT", [128, 1024], BF16, pLong)
        with ExitStack() as p0:
            a.psW0 = ps("psW0", [128, 1024], BF16, p0)
            a.psTA = ps("psTA", [128, 1024], BF16, p0)
        with ExitStack() as p1:
            a.psQK = ps("psQK", [128, 2048], F32, p1)
        with ExitStack() as p2:
            a.psV = [ps(f"psV{i}", [128, D], F32, p2) for i in range(2)]
        with ExitStack() as p4:
            a.psS = [ps(f"psS{i}", [128, GQ], F32, p4) for i in range(2)]
            a.psB = [ps("psB0", [128, GQ], F32, p4)]
        with ExitStack() as p5:
            a.psO = [ps(f"psO{i}", [128, GQ], F32, p5) for i in range(3)]
        with ExitStack() as p6:
            a.psAF = [ps(f"psAF{i}", [128, D], F32, p6) for i in range(2)]
        pLong.close()
        with ExitStack() as p8:
            a.psGU = ps("psGU", [128, NBI * 512], F32, p8)
            with ExitStack() as p7:
                a.psT2 = ps("psT2", [128, 1024], BF16, p7)
            with ExitStack() as p9:
                a.psD = ps("psD", [128, D], F32, p9)

        def schedule(b: B):
            b.dall_v = True
            b.dall_s = True
            ssn = {}
            for hap, dst in [(h.ident, a.ident), (h.bqT, a.bqT), (h.bkT, a.bkT),
                             (h.bv, a.bv32), (h.bo, a.bo32)]:
                b.dma("sync", "dM", dst[:], hap[:])
            b.w("vector", "dM")
            b.v(lambda e: e.memset(a.ones_c[:], 1.0))
            b.v(lambda e: e.memset(a.ones_r[:], 1.0))
            b.v(lambda e: e.tensor_copy(a.bvB[:], a.bv32[:]))
            b.v(lambda e: e.tensor_copy(a.boB[:], a.bo32[:]))
            b.v(lambda e: e.tensor_scalar(a.bqT[:], a.bqT[:], 1.0 / 32.0, None, ALU.mult))
            b.vn(lambda e: e.memset(a.tieb[:], 2.0 ** -49))
            b.vn(lambda e: e.memset(a.cnrm[:], 1e-6))
            b.vd()
            b.v(lambda e: e.tensor_scalar(a.tieb[:], a.tieb[:], -1.0, 1.0, ALU.mult, ALU.add))
            b.v(lambda e: e.tensor_scalar(a.cnrm[:], a.cnrm[:], 1.0 / 32.0, 1.0, ALU.mult, ALU.add))
            sv_misc = b.vpub()

            wT_map = [("wq", h.wq, a.wqT), ("wk", h.wk, a.wkT),
                      ("wv", h.wv, a.wvT), ("wo", h.wo, a.woT)]
            ss_wevac = 0
            sv_wcast = {}
            wld = {}
            wt_i = 0
            for wi, (nm, hap, dst) in enumerate(wT_map):
                for t in range(DC):
                    par = wt_i % 2
                    if wt_i >= 2:
                        b.w("sync", "SV", sv_wcast[wt_i - 2])
                    wld[wt_i] = b.dma("sync", f"dA{par}", a.w0f[par][:],
                                      hap[t * 128:(t + 1) * 128, :])
                    b.w("vector", f"dA{par}", wld[wt_i])
                    if wt_i >= 1:
                        b.w("vector", "SM", ssn.get(("w0tr", wt_i - 1), 0))
                    b.v(lambda e, par=par: e.tensor_copy(a.w0b[:], a.w0f[par][:]))
                    sv_wcast[wt_i] = b.vpub()
                    b.w("tensor", "SV", sv_wcast[wt_i])
                    b.w("tensor", "SV", sv_misc)
                    if wt_i > 0:
                        b.w("tensor", "SS", ss_wevac)
                    def trs(e):
                        for dc in range(DC):
                            e.transpose(a.psW0[:, dc * 128:(dc + 1) * 128],
                                        a.w0b[:, dc * 128:(dc + 1) * 128],
                                        a.ident[:])
                    b.t(trs)
                    sm = b.tpub()
                    ssn[("w0tr", wt_i)] = sm
                    b.w("scalar", "SM", sm)
                    def evac(e, dst=dst, t=t):
                        view = dst[:].rearrange("p (dc q) -> p dc q", q=1024)
                        e.copy(view[:, :, t * 128:(t + 1) * 128], a.psW0[:])
                    b.s(evac)
                    ss_wevac = b.spub()
                    wt_i += 1

            sv_stats = {}
            ss_stats = {}
            NW0 = 4 * DC
            for i in range(NT):
                par = i % 2
                if i >= 2:
                    b.w("sync", "SS", ss_stats[i - 2])
                else:
                    b.w("sync", "SV", sv_wcast[NW0 - 2 + i])
                b.dma("sync", f"dA{par}", a.xb[par][:], h.x[i * 128:(i + 1) * 128, :])
                b.w("vector", f"dA{par}")
                b.v(lambda e, i=i, par=par: e.tensor_reduce(
                    a.sA1[:, i:i + 1], a.xb[par][:], AX.X, ALU.add))
                sv_stats[i] = b.vpub()
                b.w("scalar", f"dA{par}")
                b.w("scalar", "SV", sv_stats[i])
                b.s(lambda e, i=i, par=par: e.activation(
                    a.sqd[:], a.xb[par][:], AF.Square, accum_out=a.sA2[:, i:i + 1]))
                ss_stats[i] = b.spub()

            def ln_batch(b, s1, s2, rst, nb, tmp):
                b.v(lambda e: e.tensor_scalar(tmp[:], s1[:], 1.0 / D, None, ALU.mult))
                b.v(lambda e: e.tensor_scalar(s2[:], s2[:], 1.0 / D, None, ALU.mult))
                b.v(lambda e: e.scalar_tensor_tensor(rst[:], tmp[:], -1.0, tmp[:], ALU.mult, ALU.mult))
                b.v(lambda e: e.tensor_add(rst[:], rst[:], s2[:]))
                b.v(lambda e: e.tensor_scalar(rst[:], rst[:], LNEPS, None, ALU.add))
                sv = b.vpub()
                b.w("scalar", "SV", sv)
                b.s(lambda e: e.activation(rst[:], rst[:], AF.Sqrt))
                ssx = b.spub()
                b.w("vector", "SS", ssx)
                b.vn(lambda e: e.reciprocal(rst[:], rst[:]))
                b.vd()
                b.v(lambda e: e.scalar_tensor_tensor(nb[:], tmp[:], -1.0, rst[:], ALU.mult, ALU.mult))
                return b.vpub()

            b.w("vector", "SS", ss_stats[NT - 1])
            sv_lnA = ln_batch(b, a.sA1, a.sA2, a.rA, a.nA, a.tmp1)

            sv_norm_done = {}
            ss_h = {}
            sm_tr = {}
            sv_xnT = {}
            for i in range(NT):
                par = i % 2
                if i >= 2:
                    b.w("sync", "SS", ss_h[i - 2])
                else:
                    b.w("sync", "SS", ss_stats[NT - 2 + i])
                b.dma("sync", f"dA{par}", a.xb[par][:], h.x[i * 128:(i + 1) * 128, :])
                b.w("scalar", f"dA{par}")
                b.w("scalar", "SV", sv_lnA)
                if i >= 2:
                    b.w("scalar", "SM", sm_tr[i - 2])
                b.s(lambda e, i=i, par=par: e.activation(
                    a.xn[par][:], a.xb[par][:], AF.Identity,
                    bias=a.nA[:, i:i + 1], scale=a.rA[:, i:i + 1]))
                ss_h[i] = b.spub()
                b.w("tensor", "SS", ss_h[i])
                if i >= 1:
                    b.w("tensor", "SV", sv_xnT[i - 1])
                def trs(e, par=par):
                    for dc in range(DC):
                        e.transpose(a.psTA[:, dc * 128:(dc + 1) * 128],
                                    a.xn[par][:, dc * 128:(dc + 1) * 128], a.ident[:])
                b.t(trs)
                sm_tr[i] = b.tpub()
                b.w("vector", "SM", sm_tr[i])
                def evac(e, i=i):
                    view = a.xnT[:].rearrange("p (dc s) -> p dc s", s=S)
                    e.tensor_copy(view[:, :, i * 128:(i + 1) * 128], a.psTA[:])
                b.v(evac)
                sv_xnT[i] = b.vpub()

            b.w("tensor", "SV", sv_xnT[NT - 1])
            ss_bevac = 0
            stp = 0
            st_of_stage = {}
            SH = min(2048, S)
            for (wTn, scr, scale, biasap) in [
                    (a.wqT, h.qT_scr, 1.0 / 32.0, a.bqT),
                    (a.wkT, h.kT_scr, 1.0, a.bkT)]:
                for dq in range(DC):
                    par = stp % 2
                    stp += 1
                    for half in range(S // SH):
                        if not (wTn is a.wqT and dq == 0 and half == 0):
                            b.w("tensor", "SS", ss_bevac)
                        def mmq(e, wTn=wTn, dq=dq, half=half):
                            for dc in range(DC):
                                lhsT = wTn[:, dc * 1024 + dq * 128: dc * 1024 + (dq + 1) * 128]
                                for (off, ln) in iblk(SH):
                                    so = half * SH + off
                                    e.matmul(a.psQK[:, off:off + ln],
                                             lhsT,
                                             a.xnT[:, dc * S + so: dc * S + so + ln],
                                             start=(dc == 0), stop=(dc == DC - 1))
                        b.t(mmq)
                        sm = b.tpub()
                        b.w("scalar", "SM", sm)
                        if half == 0 and par in st_of_stage:
                            b.w("scalar", f"dSt{par}", st_of_stage[par])
                        b.s(lambda e, par=par, scale=scale, biasap=biasap, dq=dq, half=half:
                            e.activation(a.stageB[par][:, half * SH:(half + 1) * SH],
                                         a.psQK[:, 0:SH], AF.Identity,
                                         bias=biasap[:, dq:dq + 1], scale=scale))
                        ss_bevac = b.spub()
                    st_of_stage[par] = b.dma("scalar", f"dSt{par}",
                                             scr[dq], a.stageB[par][:])
            ss_vevac = {}
            for i in range(NT):
                par2 = i % 2
                if i >= 2:
                    b.w("tensor", "SS", ss_vevac[i - 2])
                else:
                    b.w("tensor", "SS", ss_bevac)
                def mmv(e, i=i, par2=par2):
                    for (off, ln) in iblk(D):
                        e.matmul(a.psV[par2][:, off:off + ln], a.ones_r[0:1, :],
                                 a.bvB[0:1, off:off + ln], start=True, stop=False)
                    for dc in range(DC):
                        lhsT = a.xnT[:, dc * S + i * 128: dc * S + (i + 1) * 128]
                        for (off, ln) in iblk(D):
                            e.matmul(a.psV[par2][:, off:off + ln], lhsT,
                                     a.wvT[:, dc * 1024 + off: dc * 1024 + off + ln],
                                     start=False, stop=(dc == DC - 1))
                b.t(mmv)
                sm = b.tpub()
                par = stp % 2
                stp += 1
                b.w("scalar", "SM", sm)
                if par in st_of_stage:
                    b.w("scalar", f"dSt{par}", st_of_stage[par])
                b.s(lambda e, par=par, par2=par2: e.copy(a.vst[par2][:], a.psV[par2][:]))
                ss_vevac[i] = b.spub()
                st_of_stage[par] = b.dma("scalar", f"dSt{par}", h.v_scr[i], a.vst[par2][:])

            b.barrier()

            def quant_ladder_w(b, src_f32, cols, sv_gate):
                blocks = cols // 16
                bm3 = src_f32[:, 0:cols].rearrange("p (bl k) -> p bl k", k=16)
                b.vn(lambda e: e.tensor_reduce(a.bmW[:, 0:blocks], bm3, AX.X, ALU.max,
                                               apply_absolute_value=True))
                b.vd()
                b.vn(lambda e: e.reciprocal(a.rcW[:, 0:blocks], a.bmW[:, 0:blocks]))
                b.vd()
                b.v(lambda e: e.tensor_scalar(a.rcW[:, 0:blocks], a.rcW[:, 0:blocks],
                                              6.0, None, ALU.mult))
                b.vn(lambda e: e.reciprocal(a.shW[:, 0:blocks], a.rcW[:, 0:blocks]))
                b.vd()
                b.v(lambda e: e.tensor_scalar(a.shW[:, 0:blocks], a.shW[:, 0:blocks],
                                              1e-12, None, ALU.max))
                b.vn(lambda e: e.reciprocal(a.rcW[:, 0:blocks], a.shW[:, 0:blocks]))
                b.vd()
                b.v(lambda e: e.tensor_scalar(a.rcW[:, 0:blocks], a.rcW[:, 0:blocks],
                                              a.tieb[:], None, ALU.mult))
                rc_b = a.rcW[:, 0:blocks].unsqueeze(2).broadcast_to([128, blocks, 16])
                sh_b = a.shW[:, 0:blocks].unsqueeze(2).broadcast_to([128, blocks, 16])
                x3 = src_f32[:, 0:cols].rearrange("p (bl k) -> p bl k", k=16)
                xn3 = a.xnW[:, 0:cols].rearrange("p (bl k) -> p bl k", k=16)
                b.v(lambda e: e.tensor_tensor(xn3, x3, rc_b, ALU.mult))
                svx = b.vpub()
                b.w("scalar", "SV", svx)
                b.s(lambda e: e.sign(a.sgW[:, 0:cols], a.xnW[:, 0:cols]))
                b.s(lambda e: e.activation(a.xnW[:, 0:cols], a.xnW[:, 0:cols], AF.Abs))
                sss = b.spub()
                b.w("vector", "SS", sss)
                aw = a.xnW[:, 0:cols]
                b.v(lambda e: e.tensor_scalar(a.acW[:, 0:cols], aw, 0.25, None, ALU.is_gt))
                for th in (0.75, 1.25, 1.75):
                    b.v(lambda e, th=th: e.scalar_tensor_tensor(
                        a.acW[:, 0:cols], aw, th, a.acW[:, 0:cols], ALU.is_gt, ALU.add))
                b.v(lambda e: e.tensor_scalar(a.bbW[:, 0:cols], aw, 2.5, None, ALU.is_gt))
                b.v(lambda e: e.scalar_tensor_tensor(
                    a.bbW[:, 0:cols], aw, 3.5, a.bbW[:, 0:cols], ALU.is_gt, ALU.add))
                b.v(lambda e: e.tensor_scalar(a.dqW[:, 0:cols], aw, 5.0, 2.0,
                                              ALU.is_gt, ALU.mult))
                b.v(lambda e: e.tensor_add(a.bbW[:, 0:cols], a.bbW[:, 0:cols],
                                           a.dqW[:, 0:cols]))
                b.v(lambda e: e.scalar_tensor_tensor(
                    a.bbW[:, 0:cols], a.acW[:, 0:cols], 0.5, a.bbW[:, 0:cols],
                    ALU.mult, ALU.add))
                b.v(lambda e: e.tensor_mul(a.bbW[:, 0:cols], a.bbW[:, 0:cols],
                                           a.sgW[:, 0:cols]))
                dq3 = a.dqW[:, 0:cols].rearrange("p (bl k) -> p bl k", k=16)
                qs3 = a.bbW[:, 0:cols].rearrange("p (bl k) -> p bl k", k=16)
                b.v(lambda e: e.tensor_tensor(dq3, qs3, sh_b, ALU.mult))
                return b.vpub()

            w1_items = ([(h.wg, h.wgT_scr, t, D, DC, 0, 0) for t in range(IC)] +
                        [(h.wu, h.wuT_scr, t, D, DC, 0, 0) for t in range(IC)] +
                        [(h.wd, h.wdT_scr, t, 1408, 11, hh * 1408, hh * 11)
                         for t in range(DC) for hh in range(2)])
            w1s = {"wp": 0, "sm_wtr": 0, "store": {}}

            def w1_load(wp):
                hsrc, scr, t, cols, nchunk, coloff, choff = w1_items[wp]
                par = wp % 2
                if wp >= 2:
                    b.w("sync", "SV", ssn[("w1ld", wp - 2)])
                b.dma("sync", f"dA{par}", a.wf[par][:, 0:cols],
                      hsrc[t * 128:(t + 1) * 128, coloff:coloff + cols])
                ssn[("w1ldc", wp)] = b.C[f"dA{par}"]

            def w1_step():
                wp = w1s["wp"]
                if wp >= len(w1_items):
                    return
                b.dall_v = False
                b.dall_s = False
                try:
                    w1_step_body(wp)
                finally:
                    b.dall_v = True
                    b.dall_s = True

            def w1_step_body(wp):
                if wp == 0:
                    w1_load(0)
                    w1_load(1)
                hsrc, scr, t, cols, nchunk, coloff, choff = w1_items[wp]
                par = wp % 2
                b.w("vector", f"dA{par}", ssn[("w1ldc", wp)])
                if wp >= 1:
                    b.w("vector", "SM", w1s["sm_wtr"])
                sv_wdeq = quant_ladder_w(b, a.wf[par], cols, None)
                ssn[("w1ld", wp)] = sv_wdeq
                if wp + 2 < len(w1_items):
                    w1_load(wp + 2)
                spar = wp % 2
                if wp >= 1:
                    osp = (wp - 1) % 2
                    b.w("tensor", f"dSt{osp}", w1s["store"].get(osp, 0))
                if wp >= 2:
                    b.w("tensor", f"dSt{spar}", w1s["store"].get(spar, 0))
                rounds = [range(r, min(r + 8, nchunk)) for r in range(0, nchunk, 8)]
                for ri, rng in enumerate(rounds):
                    b.w("tensor", "SV", sv_wdeq)
                    if ri > 0 or wp > 0:
                        b.w("tensor", "SV", ssn.get(("w1evac_last",), 0))
                    def trs(e, rng=rng):
                        for j, ch in enumerate(rng):
                            e.transpose(a.psWT[:, j * 128:(j + 1) * 128],
                                        a.dqW[:, ch * 128:(ch + 1) * 128], a.ident[:])
                    b.t(trs)
                    smx = b.tpub()
                    w1s["sm_wtr"] = smx
                    b.w("vector", "SM", smx)
                    def evac(e, rng=rng):
                        n = len(rng) * 128
                        c0 = rng[0] * 128
                        e.tensor_copy(a.wst[0][:, c0:c0 + n], a.psWT[:, 0:n])
                    b.v(evac)
                    ssn[("w1evac_last",)] = b.vpub()
                b.w("scalar", "SV", ssn[("w1evac_last",)])
                dst = scr[choff:choff + nchunk, :, t * 128:(t + 1) * 128].rearrange(
                    "ch p j -> p ch j")
                src = a.wst[0][:, 0:nchunk * 128].rearrange(
                    "p (ch j) -> p ch j", j=128)
                w1s["store"][spar] = b.dma("scalar", f"dSt{spar}", dst, src)
                w1s["wp"] = wp + 1

            b.v(lambda e: e.memset(a.zmat[:], 1.0))
            sv_zinit = b.vpub()
            b.w("sync", "SS")
            sv_x2 = {}
            ss_x2sq = {}
            st_x2 = {}
            sm_scores = {}
            ss_exp = {}
            ss_zevac = {}
            sm_bc = {}
            sv_normt = {}
            sm_group_end = 0
            xr_cnt = {}
            zspread_last = 0
            zrecip_last = 0
            zflat_busy = 0
            ktdone_last = {}
            sm_pv_last = {}
            sm_pv_end = 0
            sv_x2_last = 0
            for g in range(NG):
                if g > 0:
                    b.w("sync", "SM", sm_group_end)
                qsrc = h.qT_scr[:, :, g * GQ:(g + 1) * GQ]
                b.dma("sync", "dQt", a.qTg[:].rearrange("p (dc q) -> p dc q", q=GQ),
                      qsrc.rearrange("dc p q -> p dc q"))
                dqt_cnt = b.C["dQt"]
                ktb_cnt = {}
                for t in range(NT):
                    tb = t // ZR
                    pb = tb % 2
                    if t % ZR == 0:
                        if pb in ktdone_last:
                            b.w("sync", "SM", ktdone_last[pb])
                        ksrc = h.kT_scr[:, :, tb * ZR * 128:(tb + 1) * ZR * 128]
                        b.dma("sync", f"dKt{pb}",
                              a.ktb[pb][:].rearrange("p (dc tj) -> p dc tj", tj=ZR * 128),
                              ksrc.rearrange("dc p tj -> p dc tj"))
                        ktb_cnt[tb] = b.C[f"dKt{pb}"]
                    pS = t % 2
                    b.w("tensor", f"dKt{pb}", ktb_cnt[tb])
                    b.w("tensor", "dQt", dqt_cnt)
                    if (g, t - 2) in ss_zevac:
                        b.w("tensor", "SS", ss_zevac[(g, t - 2)])
                    elif g > 0 and t < 2:
                        b.w("tensor", "SS", ss_zevac[(g - 1, NT - 1)])
                        b.w("tensor", "SV", sv_x2_last)
                    def mms(e, t=t, pS=pS, pb=pb):
                        toff = (t % ZR) * 128
                        for dc in range(DC):
                            lhsT = a.ktb[pb][:, dc * ZR * 128 + toff:
                                             dc * ZR * 128 + toff + 128]
                            for (off, ln) in iblk(GQ):
                                e.matmul(a.psS[pS][:, off:off + ln], lhsT,
                                         a.qTg[:, dc * GQ + off: dc * GQ + off + ln],
                                         start=(dc == 0), stop=(dc == DC - 1))
                    b.t(mms)
                    sm_scores[(g, t)] = b.tpub()
                    if t == NT - 1:
                        sm_group_end = sm_scores[(g, t)]
                    b.w("scalar", "SM", sm_scores[(g, t)])
                    if t == 0 and g > 0:
                        b.w("scalar", "SM", sm_pv_end)
                    b.s(lambda e, t=t, pS=pS: e.activation(
                        a.pT[:, t * GQ:(t + 1) * GQ], a.psS[pS][:], AF.Exp))
                    ss_exp[(g, t)] = b.spub()
                    b.w("tensor", "SS", ss_exp[(g, t)])
                    def mmz(e, t=t, pS=pS):
                        for (off, ln) in iblk(GQ):
                            e.matmul(a.psS[pS][0:1, off:off + ln], a.ones_c[:],
                                     a.pT[:, t * GQ + off: t * GQ + off + ln],
                                     start=True, stop=True)
                    b.t(mmz)
                    smz = b.tpub()
                    b.w("scalar", "SM", smz)
                    b.w("scalar", "SV", zrecip_last if zrecip_last else sv_zinit)
                    b.s(lambda e, t=t, pS=pS: e.activation(
                        a.zmat[(t % ZR) * 32:(t % ZR) * 32 + 1, :],
                        a.psS[pS][0:1, :], AF.Copy, scale=32.0))
                    ss_zevac[(g, t)] = b.spub()
                    w1_step()
                    if t % ZR == ZR - 1:
                        ktdone_last[pb] = sm_scores[(g, t)]
                        r = t // ZR
                        b.w("vector", "SS", ss_zevac[(g, t)])
                        b.v(lambda e: e.tensor_scalar(a.zmat[:], a.zmat[:],
                                                      a.cnrm[0:64, :], None, ALU.mult))
                        b.vn(lambda e: e.reciprocal(a.zmat[:], a.zmat[:]))
                        b.vd()
                        b.w("vector", "SM", zflat_busy)
                        b.v(lambda e: e.tensor_copy(a.zmatB[:], a.zmat[:]))
                        zrecip_last = b.vpub()
                        for tt in range(r * ZR, (r + 1) * ZR):
                            b.w("tensor", "SV", zrecip_last)
                            if (g, tt - 1) in sv_normt:
                                b.w("tensor", "SV", sv_normt[(g, tt - 1)])
                            def mmb(e, tt=tt):
                                ro = (tt % ZR) * 32
                                for (off, ln) in iblk(GQ):
                                    e.matmul(a.psB[0][:, off:off + ln],
                                             a.ones_r[ro:ro + 1, :],
                                             a.zmatB[ro:ro + 1, off:off + ln],
                                             start=True, stop=True)
                            b.t(mmb)
                            sm_bc[(g, tt)] = b.tpub()
                            b.w("vector", "SM", sm_bc[(g, tt)])
                            b.w("vector", "SS", ss_exp[(g, tt)])
                            def nrm(e, tt=tt):
                                sl = a.pT[:, tt * GQ:(tt + 1) * GQ]
                                e.tensor_tensor(sl, sl, a.psB[0][:], ALU.mult)
                            b.v(nrm)
                            sv_normt[(g, tt)] = b.vpub()
                        zflat_busy = sm_bc[(g, (r + 1) * ZR - 1)]

                ss_oT = 0
                vt_cnt = {}
                WAVES = [(0, 3), (3, 6), (6, 8)]
                for wave, (dlo, dhi) in enumerate(WAVES):
                    for t in range(NT):
                        pv = t % 2
                        if pv in sm_pv_last:
                            b.w("sync", "SM", sm_pv_last[pv])
                        vt_cnt[(wave, t)] = b.dma("sync", f"dVt{pv}",
                                                  a.vrn[pv][:], h.v_scr[t])
                        b.w("tensor", f"dVt{pv}", vt_cnt[(wave, t)])
                        if wave == 0 and t == 0:
                            b.w("tensor", "SV", sv_normt[(g, NT - 1)])
                            if g > 0:
                                b.w("tensor", "SS", ssn[("afdone", g - 1)])
                        if wave > 0 and t == 0:
                            b.w("tensor", "SS", ss_oT)
                        def mpv(e, dlo=dlo, dhi=dhi, t=t, pv=pv):
                            for dcw in range(dhi - dlo):
                                dc = dlo + dcw
                                lhsT = a.vrn[pv][:, dc * 128:(dc + 1) * 128]
                                for (off, ln) in iblk(GQ):
                                    e.matmul(a.psO[dcw][:, off:off + ln], lhsT,
                                             a.pT[:, t * GQ + off: t * GQ + off + ln],
                                             start=(t == 0), stop=(t == NT - 1))
                        b.t(mpv)
                        sm_pv_last[pv] = b.tpub()
                    b.w("scalar", "SM", sm_pv_last[(NT - 1) % 2])
                    def oevac(e, dlo=dlo, dhi=dhi):
                        for dcw in range(dhi - dlo):
                            dc = dlo + dcw
                            e.copy(a.oT[:, dc * GQ:(dc + 1) * GQ], a.psO[dcw][:])
                    b.s(oevac)
                    ss_oT = b.spub()
                sm_pv_end = sm_pv_last[(NT - 1) % 2]

                for qt in range(TPG):
                    pq = qt % 2
                    row0 = g * GQ + qt * 128
                    prev = (g, qt - 1) if qt >= 1 else ((g - 1, TPG - 1) if g > 0 else None)
                    if prev is not None:
                        b.w("sync", "SV", sv_x2[prev])
                        b.w("sync", "SS", ss_x2sq[prev])
                    xr_cnt[(g, qt)] = b.dma("sync", f"dXr{pq}", a.xres[0][:],
                                            h.x[row0:row0 + 128, :])
                    b.w("tensor", "SS", ss_oT)
                    if (g, qt - 2) in sv_x2:
                        b.w("tensor", "SV", sv_x2[(g, qt - 2)])
                    elif g > 0:
                        b.w("tensor", "SV", sv_x2[(g - 1, TPG - 2 + qt)])
                    def mwo(e, qt=qt, pq=pq):
                        for (off, ln) in iblk(D):
                            e.matmul(a.psAF[pq][:, off:off + ln], a.ones_r[0:1, :],
                                     a.boB[0:1, off:off + ln], start=True, stop=False)
                        for kc in range(DC):
                            lhsT = a.oT[:, kc * GQ + qt * 128: kc * GQ + (qt + 1) * 128]
                            for (off, ln) in iblk(D):
                                e.matmul(a.psAF[pq][:, off:off + ln], lhsT,
                                         a.woT[:, kc * 1024 + off: kc * 1024 + off + ln],
                                         start=False, stop=(kc == DC - 1))
                    b.t(mwo)
                    smw = b.tpub()
                    if g == NG - 1 and qt == TPG - 1:
                        ssn[("afdone_last",)] = smw
                    b.w("vector", "SM", smw)
                    b.w("vector", f"dXr{pq}", xr_cnt[(g, qt)])
                    if prev is not None:
                        b.w("vector", f"dX2{prev[1] % 2}", st_x2[prev])
                        b.w("vector", "SS", ss_x2sq[prev])
                    b.v(lambda e, pq=pq: e.scalar_tensor_tensor(
                        a.x2st[0][:], a.psAF[pq][:], 1.0, a.xres[0][:],
                        ALU.mult, ALU.add))
                    idx = g * TPG + qt
                    b.v(lambda e, idx=idx: e.tensor_reduce(
                        a.sD1[:, idx:idx + 1], a.x2st[0][:], AX.X, ALU.add))
                    sv_x2[(g, qt)] = b.vpub()
                    sv_x2_last = sv_x2[(g, qt)]
                    b.w("scalar", "SV", sv_x2[(g, qt)])
                    b.s(lambda e, idx=idx: e.activation(
                        a.xres[0][:], a.x2st[0][:], AF.Square, accum_out=a.sD2[:, idx:idx + 1]))
                    ss_x2sq[(g, qt)] = b.spub()
                    st_x2[(g, qt)] = b.dma("scalar", f"dX2{pq}",
                                           h.x2_scr[row0:row0 + 128, :], a.x2st[0][:])
                ssn[("afdone", g)] = ss_x2sq[(g, TPG - 1)]

            while w1s["wp"] < len(w1_items):
                w1_step()

            b.barrier()
            b.w("sync", "dX20", b.C["dX20"])
            b.w("sync", "dX21", b.C["dX21"])

            sv_lnD = ln_batch(b, a.sD1, a.sD2, a.rD, a.nD, a.tmp1)
            b.dma("sync", "dDw", a.wgT[:].rearrange("p (dc i) -> p dc i", i=I),
                  h.wgT_scr.rearrange("dc p i -> p dc i"))
            b.dma("sync", "dDw", a.wuT[:].rearrange("p (dc i) -> p dc i", i=I),
                  h.wuT_scr.rearrange("dc p i -> p dc i"))
            b.dma("sync", "dDw", a.wdT[:].rearrange("p (ic d) -> p ic d", d=D),
                  h.wdT_scr.rearrange("ic p d -> p ic d"))
            dw_cnt = b.C["dDw"]

            def quant_ladder_a(b, src_bf, cols):
                blocks = cols // 16
                s3 = src_bf[:, 0:cols].rearrange("p (bl k) -> p bl k", k=16)
                b.vn(lambda e: e.tensor_reduce(a.bmD[:, 0:blocks], s3, AX.X, ALU.max,
                                               apply_absolute_value=True))
                b.vd()
                b.vn(lambda e: e.reciprocal(a.rcD[:, 0:blocks], a.bmD[:, 0:blocks]))
                b.vd()
                b.vn(lambda e: e.tensor_scalar(a.rcD[:, 0:blocks], a.rcD[:, 0:blocks],
                                               6.0, None, ALU.mult))
                b.vn(lambda e: e.tensor_scalar(a.shD[:, 0:blocks], a.bmD[:, 0:blocks],
                                               1.0 / 6.0, None, ALU.mult))
                b.vd()
                rc_b = a.rcD[:, 0:blocks].unsqueeze(2).broadcast_to([128, blocks, 16])
                sh_b = a.shD[:, 0:blocks].unsqueeze(2).broadcast_to([128, blocks, 16])
                xn3 = a.xnD[:, 0:cols].rearrange("p (bl k) -> p bl k", k=16)
                b.v(lambda e: e.tensor_tensor(xn3, s3, rc_b, ALU.mult))
                svx = b.vpub()
                b.w("scalar", "SV", svx)
                b.s(lambda e: e.sign(a.sgnD[:, 0:cols], a.xnD[:, 0:cols]))
                b.s(lambda e: e.activation(a.xnD[:, 0:cols], a.xnD[:, 0:cols], AF.Abs))
                sss = b.spub()
                b.w("vector", "SS", sss)
                aw = a.xnD[:, 0:cols]
                AC, BB, C1, C2_, SG = (a.acD[:, 0:cols], a.bbD[:, 0:cols],
                                       a.cD1[:, 0:cols], a.cD2[:, 0:cols],
                                       a.sgD[:, 0:cols])
                b.vn(lambda e: e.tensor_scalar(AC, aw, 0.25, None, ALU.is_gt))
                b.vn(lambda e: e.tensor_scalar(BB, aw, 0.75, None, ALU.is_gt))
                b.vn(lambda e: e.tensor_scalar(C1, aw, 1.25, None, ALU.is_gt))
                b.vn(lambda e: e.tensor_scalar(C2_, aw, 1.75, None, ALU.is_gt))
                b.vd()
                b.vn(lambda e: e.tensor_add(AC, AC, BB))
                b.vn(lambda e: e.tensor_add(C1, C1, C2_))
                b.vn(lambda e: e.tensor_scalar(SG, aw, 2.5, None, ALU.is_gt))
                b.vd()
                b.vn(lambda e: e.tensor_add(AC, AC, C1))
                b.vn(lambda e: e.tensor_scalar(BB, aw, 3.5, None, ALU.is_gt))
                b.vn(lambda e: e.tensor_scalar(C2_, aw, 5.0, 2.0, ALU.is_gt, ALU.mult))
                b.vd()
                b.vn(lambda e: e.tensor_add(BB, BB, SG))
                b.vd()
                b.vn(lambda e: e.tensor_add(BB, BB, C2_))
                b.vd()
                b.vn(lambda e: e.scalar_tensor_tensor(BB, AC, 0.5, BB, ALU.mult, ALU.add))
                b.vd()
                b.vn(lambda e: e.tensor_mul(BB, BB, a.sgnD[:, 0:cols]))
                b.vd()
                dq3 = a.sgD[:, 0:cols].rearrange("p (bl k) -> p bl k", k=16)
                qs3 = a.bbD[:, 0:cols].rearrange("p (bl k) -> p bl k", k=16)
                b.v(lambda e: e.tensor_tensor(dq3, qs3, sh_b, ALU.mult))
                return b.vpub()

            xl_cnt = {}
            st_h2 = {}
            for i in range(NT):
                par = i % 2
                if i >= 2:
                    b.w("sync", "SV", ssn[("h2xn", i - 2)])
                xl_cnt[i] = b.dma("sync", f"dXl{par}", a.x2t[par][:],
                                  h.x2_scr[i * 128:(i + 1) * 128, :])
                b.w("scalar", f"dXl{par}", xl_cnt[i])
                b.w("scalar", "SV", sv_lnD)
                if i >= 1:
                    b.w("scalar", "SV", ssn[("h2xn", i - 1)])
                b.s(lambda e, i=i, par=par: e.activation(
                    a.h2b, a.x2t[par][:], AF.Identity,
                    bias=a.nD[:, i:i + 1], scale=a.rD[:, i:i + 1]))
                ss_h2 = b.spub()
                b.w("vector", "SS", ss_h2)
                if i >= 1:
                    b.w("vector", "SM", ssn[("h2tr", i - 1)])
                sv_q2c = quant_ladder_a(b, a.h2b, D)
                ssn[("h2xn", i)] = sv_q2c
                b.w("tensor", "SV", sv_q2c)
                if i >= 1:
                    b.w("tensor", "SV", ssn[("h2ev", i - 1)])
                def tr2(e):
                    for dc in range(DC):
                        e.transpose(a.psT2[:, dc * 128:(dc + 1) * 128],
                                    a.sgD[:, dc * 128:(dc + 1) * 128], a.ident[:])
                b.t(tr2)
                smt = b.tpub()
                ssn[("h2tr", i)] = smt
                b.w("vector", "SM", smt)
                pq = i % 2
                if i >= 2:
                    b.w("vector", f"dSt{pq}", st_h2[i - 2])
                b.v(lambda e, pq=pq: e.tensor_copy(a.h2qT[pq][:], a.psT2[:]))
                ssn[("h2ev", i)] = b.vpub()
                b.w("scalar", "SV", ssn[("h2ev", i)])
                st_h2[i] = b.dma("scalar", f"dSt{pq}", h.h2qT_scr[i], a.h2qT[pq][:])

            y_cnt = {}
            sv_y = {}
            h2l_cnt = {}
            for i in range(NT):
                par = i % 2
                if i >= 2:
                    b.w("sync", "SM", ssn[("up_done", i - 2)])
                else:
                    b.w("sync", f"dSt{par}", st_h2[NT - 2 + par])
                h2l_cnt[i] = b.dma("sync", f"dKt{par}", a.h2qT[par][:], h.h2qT_scr[i])
                if i >= 2:
                    b.w("sync", "SV", sv_y[i - 2])
                xl2 = b.dma("sync", f"dXl{par}", a.x2t[par][:],
                            h.x2_scr[i * 128:(i + 1) * 128, :])
                ssn[("xl2", i)] = xl2
                b.w("tensor", f"dKt{par}", h2l_cnt[i])
                b.w("tensor", "dDw", dw_cnt)
                if i >= 1:
                    b.w("tensor", "SV", ssn[("h3mul", i - 1)])
                def mg(e, par=par):
                    for dc in range(DC):
                        lhsT = a.h2qT[par][:, dc * 128:(dc + 1) * 128]
                        for (off, ln) in iblk(I):
                            e.matmul(a.psGU[:, off:off + ln], lhsT,
                                     a.wgT[:, dc * I + off: dc * I + off + ln],
                                     start=(dc == 0), stop=(dc == DC - 1))
                b.t(mg)
                smg = b.tpub()
                b.w("scalar", "SM", smg)
                if i >= 1:
                    b.w("scalar", "SM", ssn[("tr3", i - 1)])
                b.s(lambda e: e.activation(a.sgD[:, 0:I], a.psGU[:, 0:I], AF.Silu))
                ss_sg = b.spub()
                b.w("tensor", "SS", ss_sg)
                def mu(e, par=par):
                    for dc in range(DC):
                        lhsT = a.h2qT[par][:, dc * 128:(dc + 1) * 128]
                        for (off, ln) in iblk(I):
                            e.matmul(a.psGU[:, off:off + ln], lhsT,
                                     a.wuT[:, dc * I + off: dc * I + off + ln],
                                     start=(dc == 0), stop=(dc == DC - 1))
                b.t(mu)
                smu = b.tpub()
                ssn[("up_done", i)] = smu
                b.w("vector", "SM", smu)
                if i >= 1:
                    b.w("vector", "SV", ssn[("h3q", i - 1)])
                b.v(lambda e: e.tensor_tensor(a.h3D[:, 0:I], a.psGU[:, 0:I],
                                              a.sgD[:, 0:I], ALU.mult))
                ssn[("h3mul", i)] = b.vpub()
                b.w("vector", "SV", ssn[("h3mul", i)])
                sv_q3 = quant_ladder_a(b, a.h3D, I)
                ssn[("h3q", i)] = sv_q3
                rounds = [range(r, min(r + 8, IC)) for r in range(0, IC, 8)]
                last_evac = 0
                for ri, rng in enumerate(rounds):
                    b.w("tensor", "SV", sv_q3 if ri == 0 else last_evac)
                    if ri == 0 and i >= 1:
                        b.w("tensor", "SV", ssn[("h3ev", i - 1)])
                        b.w("tensor", "SV", sv_y[i - 1])
                    def tr3(e, rng=rng):
                        for j, ch in enumerate(rng):
                            e.transpose(a.psT2[:, j * 128:(j + 1) * 128],
                                        a.sgD[:, ch * 128:(ch + 1) * 128], a.ident[:])
                    b.t(tr3)
                    smr = b.tpub()
                    b.w("vector", "SM", smr)
                    def ev3(e, rng=rng):
                        n = len(rng) * 128
                        c0 = rng[0] * 128
                        e.tensor_copy(a.h3qT[:, c0:c0 + n], a.psT2[:, 0:n])
                    b.v(ev3)
                    last_evac = b.vpub()
                ssn[("tr3", i)] = b.C["SM"]
                ssn[("h3ev", i)] = last_evac
                b.w("tensor", "SV", last_evac)
                if i >= 1:
                    b.w("tensor", "SV", sv_y[i - 1])
                def md(e):
                    for ic in range(IC):
                        lhsT = a.h3qT[:, ic * 128:(ic + 1) * 128]
                        for (off, ln) in iblk(D):
                            e.matmul(a.psD[:, off:off + ln], lhsT,
                                     a.wdT[:, ic * D + off: ic * D + off + ln],
                                     start=(ic == 0), stop=(ic == IC - 1))
                b.t(md)
                smd = b.tpub()
                b.w("vector", "SM", smd)
                b.w("vector", f"dXl{par}", ssn[("xl2", i)])
                if i >= 2:
                    b.w("vector", f"dY{par}", y_cnt[i - 2])
                b.v(lambda e, par=par: e.scalar_tensor_tensor(
                    a.yst[par][:], a.psD[:], 1.0, a.x2t[par][:], ALU.mult, ALU.add))
                sv_y[i] = b.vpub()
                b.w("scalar", "SV", sv_y[i])
                y_cnt[i] = b.dma("scalar", f"dY{par}",
                                 h.y[i * 128:(i + 1) * 128, :], a.yst[par][:])

            b.w("scalar", "dY0", b.C["dY0"])
            b.w("scalar", "dY1", b.C["dY1"])

        with nc.Block() as block:
            refC = {}

            def mk(E):
                def body(eng):
                    bb = B(E, eng, sems, safe=safe)
                    schedule(bb)
                    refC[E] = dict(bb.C)
                return body
            block.sync(mk("sync"))
            block.tensor(mk("tensor"))
            block.vector(mk("vector"))
            block.scalar(mk("scalar"))
            block.gpsimd(lambda eng: None)
            vals = list(refC.values())
            for v in vals[1:]:
                assert v == vals[0], "pass counter mismatch"

    return nc


_NC_CACHE = {}


def _get_nc(S):
    if S not in _NC_CACHE:
        _NC_CACHE[S] = build(S)
    return _NC_CACHE[S]


def make_inputs(x_b, wq, bq, wk, bk, wv, bv, wo, bo, wg, wu, wd):
    return {
        "x": np.ascontiguousarray(x_b, dtype=np.float32),
        "wq": np.ascontiguousarray(wq, dtype=np.float32),
        "wk": np.ascontiguousarray(wk, dtype=np.float32),
        "wv": np.ascontiguousarray(wv, dtype=np.float32),
        "wo": np.ascontiguousarray(wo, dtype=np.float32),
        "wg": np.ascontiguousarray(wg, dtype=np.float32),
        "wu": np.ascontiguousarray(wu, dtype=np.float32),
        "wd": np.ascontiguousarray(wd, dtype=np.float32),
        "bqT": np.ascontiguousarray(np.asarray(bq, dtype=np.float32).reshape(DC, 128).T),
        "bkT": np.ascontiguousarray(np.asarray(bk, dtype=np.float32).reshape(DC, 128).T),
        "bv_row": np.ascontiguousarray(np.asarray(bv, dtype=np.float32).reshape(1, D)),
        "bo_row": np.ascontiguousarray(np.asarray(bo, dtype=np.float32).reshape(1, D)),
        "ident": np.eye(128, dtype=ml_dtypes.bfloat16),
    }


def kernel(L0_attn_batch, L0_attn_Wq, L0_attn_bq, L0_attn_Wk, L0_attn_bk,
           L0_attn_Wv, L0_attn_bv, L0_attn_Wo, L0_attn_bo,
           L0_ffn_w_gate, L0_ffn_w_up, L0_ffn_w_down):
    xs = np.ascontiguousarray(np.asarray(L0_attn_batch, dtype=np.float32))
    Bn, S, Dm = xs.shape
    assert Dm == D
    nc = _get_nc(S)
    in_maps = []
    for c in range(Bn):
        in_maps.append(make_inputs(xs[c], L0_attn_Wq, L0_attn_bq, L0_attn_Wk,
                                   L0_attn_bk, L0_attn_Wv, L0_attn_bv,
                                   L0_attn_Wo, L0_attn_bo,
                                   L0_ffn_w_gate, L0_ffn_w_up, L0_ffn_w_down))
    res = run_bass_kernel_spmd(nc, in_maps, core_ids=list(range(Bn)))
    outs = [np.asarray(r["y"]) for r in res.results]
    if outs[0].dtype != np.float64:
        outs = [o.astype(np.float32, copy=False) for o in outs]
    return np.stack(outs)


# version 36
# speedup vs baseline: 1.0297x; 1.0041x over previous
import numpy as np
import ml_dtypes
from contextlib import ExitStack
from types import SimpleNamespace

import concourse.bass as bass
import concourse.mybir as mybir
from concourse.bass_utils import run_bass_kernel_spmd

F32 = mybir.dt.float32
BF16 = mybir.dt.bfloat16
AF = mybir.ActivationFunctionType
ALU = mybir.AluOpType
AX = mybir.AxisListType

D = 1024
I = 2816
DC = D // 128
IC = I // 128
NBD = D // 512
NBI = (I + 511) // 512
ZR = 2
LNEPS = 1e-5
RECIP_NORM = float(np.float32(32.000001))


def iblk(n):
    out = []
    o = 0
    while o < n:
        out.append((o, min(512, n - o)))
        o += 512
    return out


class Cap:

    def __init__(self, eng):
        object.__setattr__(self, "_e", eng)
        object.__setattr__(self, "last", None)

    def __getattr__(self, nm):
        a = getattr(self._e, nm)
        if callable(a):
            def w(*args, **kw):
                r = a(*args, **kw)
                if hasattr(r, "then_inc"):
                    object.__setattr__(self, "last", r)
                return r
            return w
        return a


class B:

    def __init__(self, E, eng, sems, safe=False):
        self.E = E
        self.eng = eng
        self.cap = Cap(eng) if eng is not None else None
        self.safe = safe
        self.sems = sems
        self.C = {k: 0 for k in sems}

    def w(self, engine, sem, val=None):
        if val is None:
            val = self.C[sem]
        if self.E == engine and val > 0:
            self.eng.wait_ge(self.sems[sem], int(val))

    def dma(self, issuer, sem, out, in_):
        self.C[sem] += 16
        if self.E == issuer:
            if issuer == "scalar":
                self.cap.drain()
            self.eng.dma_start(out=out, in_=in_).then_inc(self.sems[sem], 16)
        return self.C[sem]

    def t(self, fn):
        if self.E == "tensor":
            fn(self.cap)

    def tpub(self):
        self.C["SM"] += 1
        if self.E == "tensor":
            self.cap.last.then_inc(self.sems["SM"], 1)
        return self.C["SM"]

    dall_v = False
    dall_s = False

    def v(self, fn):
        if self.E == "vector":
            fn(self.cap)
            if self.dall_v:
                self.cap.drain()

    def vn(self, fn):
        if self.E == "vector":
            fn(self.cap)

    def vd(self):
        if self.E == "vector":
            self.cap.drain()

    def vpub(self):
        self.C["SV"] += 1
        if self.E == "vector":
            self.cap.drain()
            self.cap.last.then_inc(self.sems["SV"], 1)
        return self.C["SV"]

    def s(self, fn):
        if self.E == "scalar":
            fn(self.cap)
            if self.dall_s:
                self.cap.drain()

    def spub(self):
        self.C["SS"] += 1
        if self.E == "scalar":
            self.cap.drain()
            self.cap.last.then_inc(self.sems["SS"], 1)
        return self.C["SS"]

    def barrier(self):
        for e in ("sync", "tensor", "vector", "scalar"):
            for sem in self.sems:
                self.w(e, sem)


def build(S, safe=False):
    NT = S // 128
    GQ = min(1024, S)
    NG = S // GQ
    TPG = GQ // 128
    NBG = GQ // 512
    NTB = NT // ZR

    nc = bass.Bass("TRN2", target_bir_lowering=False, debug=False,
                   detect_race_conditions=bool(safe))

    h = SimpleNamespace()
    h.x = nc.dram_tensor("x", [S, D], F32, kind="ExternalInput").ap()
    for nm, shp in [("wq", [D, D]), ("wk", [D, D]), ("wv", [D, D]), ("wo", [D, D]),
                    ("wg", [I, D]), ("wu", [I, D]), ("wd", [D, I])]:
        setattr(h, nm, nc.dram_tensor(nm, shp, F32, kind="ExternalInput").ap())
    h.bqT = nc.dram_tensor("bqT", [128, DC], F32, kind="ExternalInput").ap()
    h.bkT = nc.dram_tensor("bkT", [128, DC], F32, kind="ExternalInput").ap()
    h.bv = nc.dram_tensor("bv_row", [1, D], F32, kind="ExternalInput").ap()
    h.bo = nc.dram_tensor("bo_row", [1, D], F32, kind="ExternalInput").ap()
    h.ident = nc.dram_tensor("ident", [128, 128], BF16, kind="ExternalInput").ap()
    h.y = nc.dram_tensor("y", [S, D], F32, kind="ExternalOutput").ap()

    h.qT_scr = nc.dram_tensor("qT_scr", [DC, 128, S], BF16, kind="Internal").ap()
    h.kT_scr = nc.dram_tensor("kT_scr", [DC, 128, S], BF16, kind="Internal").ap()
    h.v_scr = nc.dram_tensor("v_scr", [NT, 128, D], BF16, kind="Internal").ap()
    h.x2_scr = nc.dram_tensor("x2_scr", [S, D], F32, kind="Internal").ap()
    h.h2qT_scr = nc.dram_tensor("h2qT_scr", [NT, 128, D], BF16, kind="Internal").ap()
    h.wgT_scr = nc.dram_tensor("wgT_scr", [DC, 128, I], BF16, kind="Internal").ap()
    h.wuT_scr = nc.dram_tensor("wuT_scr", [DC, 128, I], BF16, kind="Internal").ap()
    h.wdT_scr = nc.dram_tensor("wdT_scr", [IC, 128, D], BF16, kind="Internal").ap()

    ctx = ExitStack()
    a = SimpleNamespace()

    def sb(name, shape, dt, scope=ctx):
        return scope.enter_context(nc.sbuf_tensor(name, shape, dt))

    def ps(name, shape, dt, scope):
        return scope.enter_context(nc.psum_tensor(name, shape, dt))

    sem_names = ["SM", "SV", "SS",
                 "dA0", "dA1", "dW", "dM", "dSt0", "dSt1",
                 "dKt0", "dKt1", "dQt", "dVt0", "dVt1", "dXr0", "dXr1",
                 "dX20", "dX21", "dZ", "dDw", "dXl0", "dXl1", "dY0", "dY1"]

    with ctx:
        sems = {nm: ctx.enter_context(nc.semaphore(nm)) for nm in sem_names}

        a.ident = sb("identB", [128, 128], BF16)
        a.ones_c = sb("ones_c", [128, 1], BF16)
        a.ones_r = sb("ones_r", [64, 128], BF16)
        a.bqT = sb("bqT_s", [128, DC], F32)
        a.bkT = sb("bkT_s", [128, DC], F32)

        a.sD1 = sb("sD1", [128, NT], F32)
        a.sD2 = sb("sD2", [128, NT], F32)
        a.rD = sb("rD", [128, NT], F32)
        a.nD = sb("nD", [128, NT], F32)
        a.tmp1 = sb("tmp1", [128, NT], F32)
        a.tieb = sb("tieb", [128, 1], F32)
        a.cnrm = sb("cnrm", [128, 1], F32)

        sLong = ExitStack()
        a.woT = sb("woT", [128, DC * 1024], BF16, sLong)
        a.bvB = sb("bvB", [1, D], BF16, sLong)
        a.boB = sb("boB", [1, D], BF16, sLong)
        with ExitStack() as sAB:
            a.w0f = [sb(f"w0f{i}", [128, 1024], F32, sAB) for i in range(2)]
            a.w0b = sb("w0b", [128, 1024], BF16, sAB)
            a.bv32 = sb("bv32", [1, D], F32, sAB)
            a.bo32 = sb("bo32", [1, D], F32, sAB)
            a.xnT = sb("xnT", [128, DC * S], BF16, sAB)
            a.sA1 = sb("sA1", [128, NT], F32, sAB)
            a.sA2 = sb("sA2", [128, NT], F32, sAB)
            a.rA = sb("rA", [128, NT], F32, sAB)
            a.nA = sb("nA", [128, NT], F32, sAB)
            a.wqT = sb("wqT", [128, DC * 1024], BF16, sAB)
            a.wkT = sb("wkT", [128, DC * 1024], BF16, sAB)
            a.wvT = sb("wvT", [128, DC * 1024], BF16, sAB)
            a.xb = [sb(f"xb{i}", [128, D], F32, sAB) for i in range(2)]
            a.xn = [sb(f"xn{i}", [128, D], BF16, sAB) for i in range(2)]
            a.stageB = [sb(f"stageB{i}", [128, S], BF16, sAB) for i in range(2)]
            a.vst = [sb(f"vst{i}", [128, D], BF16, sAB) for i in range(2)]
            a.sqd = sb("sqdA", [128, D], F32, sAB)
        with ExitStack() as sC:
            a.wf = [sb(f"wf{i}", [128, 1408], F32, sC) for i in range(2)]
            a.bmW = sb("bmW", [128, IC * 8], F32, sC)
            a.shW = sb("shW", [128, IC * 8], F32, sC)
            a.rcW = sb("rcW", [128, IC * 8], F32, sC)
            a.xnW = sb("xnW", [128, 1408], F32, sC)
            a.sgW = sb("sgW", [128, 1408], BF16, sC)
            a.acW = sb("acW", [128, 1408], BF16, sC)
            a.bbW = sb("bbW", [128, 1408], BF16, sC)
            a.dqW = sb("dqW", [128, 1408], BF16, sC)
            a.cW1 = sb("cW1", [128, 1408], BF16, sC)
            a.cW2 = sb("cW2", [128, 1408], BF16, sC)
            a.wst = [sb("wst0", [128, 1408], BF16, sC)]
            a.pT = sb("pT", [128, NT * GQ], BF16, sC)
            a.qTg = sb("qTg", [128, DC * GQ], BF16, sC)
            a.ktb = [sb(f"ktb{i}", [128, ZR * DC * 128], BF16, sC) for i in range(2)]
            a.vrn = [sb(f"vrn{i}", [128, D], BF16, sC) for i in range(2)]
            a.oT = sb("oT", [128, DC * GQ], BF16, sC)
            a.zmat = sb("zmat", [64, GQ], F32, sC)
            a.zmatB = sb("zmatB", [64, GQ], BF16, sC)
            a.xres = [sb("xres0", [128, D], F32, sC)]
            a.x2st = [sb("x2st0", [128, D], F32, sC)]

        sLong.close()
        with ExitStack() as sD:
            a.wgT = sb("wgT", [128, DC * I], BF16, sD)
            a.wuT = sb("wuT", [128, DC * I], BF16, sD)
            a.wdT = sb("wdT", [128, IC * D], BF16, sD)
            a.x2t = [sb(f"x2t{i}", [128, D], F32, sD) for i in range(2)]
            a.h2qT = [sb(f"h2qT{i}", [128, D], BF16, sD) for i in range(2)]
            a.bmD = sb("bmD", [128, IC * 8], F32, sD)
            a.shD = sb("shD", [128, IC * 8], F32, sD)
            a.rcD = sb("rcD", [128, IC * 8], F32, sD)
            a.xnD = sb("xnD", [128, I], BF16, sD)
            a.sgnD = sb("sgnD", [128, I], BF16, sD)
            a.acD = sb("acD", [128, I], BF16, sD)
            a.bbD = sb("bbD", [128, I], BF16, sD)
            a.cD1 = sb("cD1", [128, I], BF16, sD)
            a.cD2 = sb("cD2", [128, I], BF16, sD)
            a.h2b = a.cD2[:, 0:D]
            a.h3D = a.cD1
            a.sgD = sb("sgD", [128, I], BF16, sD)
            a.h3qT = sb("h3qT", [128, I], BF16, sD)
            a.yst = [sb(f"yst{i}", [128, D], F32, sD) for i in range(2)]

        pLong = ExitStack()
        a.psWT = ps("psWT", [128, 1024], BF16, pLong)
        with ExitStack() as p0:
            a.psW0 = ps("psW0", [128, 1024], BF16, p0)
            a.psTA = ps("psTA", [128, 1024], BF16, p0)
        with ExitStack() as p1:
            a.psQK = ps("psQK", [128, 2048], F32, p1)
        with ExitStack() as p2:
            a.psV = [ps(f"psV{i}", [128, D], F32, p2) for i in range(2)]
        with ExitStack() as p4:
            a.psS = [ps(f"psS{i}", [128, GQ], F32, p4) for i in range(2)]
            a.psB = [ps("psB0", [128, GQ], F32, p4)]
        with ExitStack() as p5:
            a.psO = [ps(f"psO{i}", [128, GQ], F32, p5) for i in range(3)]
        with ExitStack() as p6:
            a.psAF = [ps(f"psAF{i}", [128, D], F32, p6) for i in range(2)]
        pLong.close()
        with ExitStack() as p8:
            a.psGU = ps("psGU", [128, NBI * 512], F32, p8)
            with ExitStack() as p7:
                a.psT2 = ps("psT2", [128, 1024], BF16, p7)
            with ExitStack() as p9:
                a.psD = ps("psD", [128, D], F32, p9)

        def schedule(b: B):
            b.dall_v = True
            b.dall_s = True
            ssn = {}
            for hap, dst in [(h.ident, a.ident), (h.bqT, a.bqT), (h.bkT, a.bkT),
                             (h.bv, a.bv32), (h.bo, a.bo32)]:
                b.dma("sync", "dM", dst[:], hap[:])
            b.w("vector", "dM")
            b.v(lambda e: e.memset(a.ones_c[:], 1.0))
            b.v(lambda e: e.memset(a.ones_r[:], 1.0))
            b.v(lambda e: e.tensor_copy(a.bvB[:], a.bv32[:]))
            b.v(lambda e: e.tensor_copy(a.boB[:], a.bo32[:]))
            b.v(lambda e: e.tensor_scalar(a.bqT[:], a.bqT[:], 1.0 / 32.0, None, ALU.mult))
            b.vn(lambda e: e.memset(a.tieb[:], 2.0 ** -49))
            b.vn(lambda e: e.memset(a.cnrm[:], 1e-6))
            b.vd()
            b.v(lambda e: e.tensor_scalar(a.tieb[:], a.tieb[:], -1.0, 1.0, ALU.mult, ALU.add))
            b.v(lambda e: e.tensor_scalar(a.cnrm[:], a.cnrm[:], 1.0 / 32.0, 1.0, ALU.mult, ALU.add))
            sv_misc = b.vpub()

            wT_map = [("wq", h.wq, a.wqT), ("wk", h.wk, a.wkT),
                      ("wv", h.wv, a.wvT), ("wo", h.wo, a.woT)]
            ss_wevac = 0
            sv_wcast = {}
            wld = {}
            wt_i = 0
            for wi, (nm, hap, dst) in enumerate(wT_map):
                for t in range(DC):
                    par = wt_i % 2
                    if wt_i >= 2:
                        b.w("sync", "SV", sv_wcast[wt_i - 2])
                    wld[wt_i] = b.dma("sync", f"dA{par}", a.w0f[par][:],
                                      hap[t * 128:(t + 1) * 128, :])
                    b.w("vector", f"dA{par}", wld[wt_i])
                    if wt_i >= 1:
                        b.w("vector", "SM", ssn.get(("w0tr", wt_i - 1), 0))
                    b.v(lambda e, par=par: e.tensor_copy(a.w0b[:], a.w0f[par][:]))
                    sv_wcast[wt_i] = b.vpub()
                    b.w("tensor", "SV", sv_wcast[wt_i])
                    b.w("tensor", "SV", sv_misc)
                    if wt_i > 0:
                        b.w("tensor", "SS", ss_wevac)
                    def trs(e):
                        for dc in range(DC):
                            e.transpose(a.psW0[:, dc * 128:(dc + 1) * 128],
                                        a.w0b[:, dc * 128:(dc + 1) * 128],
                                        a.ident[:])
                    b.t(trs)
                    sm = b.tpub()
                    ssn[("w0tr", wt_i)] = sm
                    b.w("scalar", "SM", sm)
                    def evac(e, dst=dst, t=t):
                        view = dst[:].rearrange("p (dc q) -> p dc q", q=1024)
                        e.copy(view[:, :, t * 128:(t + 1) * 128], a.psW0[:])
                    b.s(evac)
                    ss_wevac = b.spub()
                    wt_i += 1

            sv_stats = {}
            ss_stats = {}
            NW0 = 4 * DC
            for i in range(NT):
                par = i % 2
                if i >= 2:
                    b.w("sync", "SS", ss_stats[i - 2])
                else:
                    b.w("sync", "SV", sv_wcast[NW0 - 2 + i])
                b.dma("sync", f"dA{par}", a.xb[par][:], h.x[i * 128:(i + 1) * 128, :])
                b.w("vector", f"dA{par}")
                b.v(lambda e, i=i, par=par: e.tensor_reduce(
                    a.sA1[:, i:i + 1], a.xb[par][:], AX.X, ALU.add))
                sv_stats[i] = b.vpub()
                b.w("scalar", f"dA{par}")
                b.w("scalar", "SV", sv_stats[i])
                b.s(lambda e, i=i, par=par: e.activation(
                    a.sqd[:], a.xb[par][:], AF.Square, accum_out=a.sA2[:, i:i + 1]))
                ss_stats[i] = b.spub()

            def ln_batch(b, s1, s2, rst, nb, tmp):
                sv_old, ss_old = b.dall_v, b.dall_s
                b.dall_v = True
                b.dall_s = True
                r = _ln_batch(b, s1, s2, rst, nb, tmp)
                b.dall_v = sv_old
                b.dall_s = ss_old
                return r

            def _ln_batch(b, s1, s2, rst, nb, tmp):
                b.v(lambda e: e.tensor_scalar(tmp[:], s1[:], 1.0 / D, None, ALU.mult))
                b.v(lambda e: e.tensor_scalar(s2[:], s2[:], 1.0 / D, None, ALU.mult))
                b.v(lambda e: e.scalar_tensor_tensor(rst[:], tmp[:], -1.0, tmp[:], ALU.mult, ALU.mult))
                b.v(lambda e: e.tensor_add(rst[:], rst[:], s2[:]))
                b.v(lambda e: e.tensor_scalar(rst[:], rst[:], LNEPS, None, ALU.add))
                sv = b.vpub()
                b.w("scalar", "SV", sv)
                b.s(lambda e: e.activation(rst[:], rst[:], AF.Sqrt))
                ssx = b.spub()
                b.w("vector", "SS", ssx)
                b.vn(lambda e: e.reciprocal(rst[:], rst[:]))
                b.vd()
                b.v(lambda e: e.scalar_tensor_tensor(nb[:], tmp[:], -1.0, rst[:], ALU.mult, ALU.mult))
                return b.vpub()

            b.w("vector", "SS", ss_stats[NT - 1])
            sv_lnA = ln_batch(b, a.sA1, a.sA2, a.rA, a.nA, a.tmp1)

            sv_norm_done = {}
            ss_h = {}
            sm_tr = {}
            sv_xnT = {}
            for i in range(NT):
                par = i % 2
                if i >= 2:
                    b.w("sync", "SS", ss_h[i - 2])
                else:
                    b.w("sync", "SS", ss_stats[NT - 2 + i])
                b.dma("sync", f"dA{par}", a.xb[par][:], h.x[i * 128:(i + 1) * 128, :])
                b.w("scalar", f"dA{par}")
                b.w("scalar", "SV", sv_lnA)
                if i >= 2:
                    b.w("scalar", "SM", sm_tr[i - 2])
                b.s(lambda e, i=i, par=par: e.activation(
                    a.xn[par][:], a.xb[par][:], AF.Identity,
                    bias=a.nA[:, i:i + 1], scale=a.rA[:, i:i + 1]))
                ss_h[i] = b.spub()
                b.w("tensor", "SS", ss_h[i])
                if i >= 1:
                    b.w("tensor", "SV", sv_xnT[i - 1])
                def trs(e, par=par):
                    for dc in range(DC):
                        e.transpose(a.psTA[:, dc * 128:(dc + 1) * 128],
                                    a.xn[par][:, dc * 128:(dc + 1) * 128], a.ident[:])
                b.t(trs)
                sm_tr[i] = b.tpub()
                b.w("vector", "SM", sm_tr[i])
                def evac(e, i=i):
                    view = a.xnT[:].rearrange("p (dc s) -> p dc s", s=S)
                    e.tensor_copy(view[:, :, i * 128:(i + 1) * 128], a.psTA[:])
                b.v(evac)
                sv_xnT[i] = b.vpub()

            b.w("tensor", "SV", sv_xnT[NT - 1])
            ss_bevac = 0
            stp = 0
            st_of_stage = {}
            SH = min(2048, S)
            for (wTn, scr, scale, biasap) in [
                    (a.wqT, h.qT_scr, 1.0 / 32.0, a.bqT),
                    (a.wkT, h.kT_scr, 1.0, a.bkT)]:
                for dq in range(DC):
                    par = stp % 2
                    stp += 1
                    for half in range(S // SH):
                        if not (wTn is a.wqT and dq == 0 and half == 0):
                            b.w("tensor", "SS", ss_bevac)
                        def mmq(e, wTn=wTn, dq=dq, half=half):
                            for dc in range(DC):
                                lhsT = wTn[:, dc * 1024 + dq * 128: dc * 1024 + (dq + 1) * 128]
                                for (off, ln) in iblk(SH):
                                    so = half * SH + off
                                    e.matmul(a.psQK[:, off:off + ln],
                                             lhsT,
                                             a.xnT[:, dc * S + so: dc * S + so + ln],
                                             start=(dc == 0), stop=(dc == DC - 1))
                        b.t(mmq)
                        sm = b.tpub()
                        b.w("scalar", "SM", sm)
                        if half == 0 and par in st_of_stage:
                            b.w("scalar", f"dSt{par}", st_of_stage[par])
                        b.s(lambda e, par=par, scale=scale, biasap=biasap, dq=dq, half=half:
                            e.activation(a.stageB[par][:, half * SH:(half + 1) * SH],
                                         a.psQK[:, 0:SH], AF.Identity,
                                         bias=biasap[:, dq:dq + 1], scale=scale))
                        ss_bevac = b.spub()
                    st_of_stage[par] = b.dma("scalar", f"dSt{par}",
                                             scr[dq], a.stageB[par][:])
            ss_vevac = {}
            for i in range(NT):
                par2 = i % 2
                if i >= 2:
                    b.w("tensor", "SS", ss_vevac[i - 2])
                else:
                    b.w("tensor", "SS", ss_bevac)
                def mmv(e, i=i, par2=par2):
                    for (off, ln) in iblk(D):
                        e.matmul(a.psV[par2][:, off:off + ln], a.ones_r[0:1, :],
                                 a.bvB[0:1, off:off + ln], start=True, stop=False)
                    for dc in range(DC):
                        lhsT = a.xnT[:, dc * S + i * 128: dc * S + (i + 1) * 128]
                        for (off, ln) in iblk(D):
                            e.matmul(a.psV[par2][:, off:off + ln], lhsT,
                                     a.wvT[:, dc * 1024 + off: dc * 1024 + off + ln],
                                     start=False, stop=(dc == DC - 1))
                b.t(mmv)
                sm = b.tpub()
                par = stp % 2
                stp += 1
                b.w("scalar", "SM", sm)
                if par in st_of_stage:
                    b.w("scalar", f"dSt{par}", st_of_stage[par])
                b.s(lambda e, par=par, par2=par2: e.copy(a.vst[par2][:], a.psV[par2][:]))
                ss_vevac[i] = b.spub()
                st_of_stage[par] = b.dma("scalar", f"dSt{par}", h.v_scr[i], a.vst[par2][:])

            b.barrier()

            def quant_ladder_w(b, src_f32, cols, sv_gate):
                blocks = cols // 16
                bm3 = src_f32[:, 0:cols].rearrange("p (bl k) -> p bl k", k=16)
                b.vn(lambda e: e.tensor_reduce(a.bmW[:, 0:blocks], bm3, AX.X, ALU.max,
                                               apply_absolute_value=True))
                b.vd()
                b.vn(lambda e: e.reciprocal(a.rcW[:, 0:blocks], a.bmW[:, 0:blocks]))
                b.vd()
                b.vn(lambda e: e.tensor_scalar(a.rcW[:, 0:blocks], a.rcW[:, 0:blocks],
                                               6.0, None, ALU.mult))
                b.vd()
                b.vn(lambda e: e.reciprocal(a.shW[:, 0:blocks], a.rcW[:, 0:blocks]))
                b.vd()
                b.vn(lambda e: e.tensor_scalar(a.shW[:, 0:blocks], a.shW[:, 0:blocks],
                                               1e-12, None, ALU.max))
                b.vd()
                b.vn(lambda e: e.reciprocal(a.rcW[:, 0:blocks], a.shW[:, 0:blocks]))
                b.vd()
                b.vn(lambda e: e.tensor_scalar(a.rcW[:, 0:blocks], a.rcW[:, 0:blocks],
                                               a.tieb[:], None, ALU.mult))
                b.vd()
                rc_b = a.rcW[:, 0:blocks].unsqueeze(2).broadcast_to([128, blocks, 16])
                sh_b = a.shW[:, 0:blocks].unsqueeze(2).broadcast_to([128, blocks, 16])
                x3 = src_f32[:, 0:cols].rearrange("p (bl k) -> p bl k", k=16)
                xn3 = a.xnW[:, 0:cols].rearrange("p (bl k) -> p bl k", k=16)
                b.v(lambda e: e.tensor_tensor(xn3, x3, rc_b, ALU.mult))
                svx = b.vpub()
                b.w("scalar", "SV", svx)
                b.s(lambda e: e.sign(a.sgW[:, 0:cols], a.xnW[:, 0:cols]))
                b.s(lambda e: e.activation(a.xnW[:, 0:cols], a.xnW[:, 0:cols], AF.Abs))
                sss = b.spub()
                b.w("vector", "SS", sss)
                aw = a.xnW[:, 0:cols]
                b.v(lambda e: e.tensor_scalar(a.acW[:, 0:cols], aw, 0.25, None, ALU.is_gt))
                for th in (0.75, 1.25, 1.75):
                    b.v(lambda e, th=th: e.scalar_tensor_tensor(
                        a.acW[:, 0:cols], aw, th, a.acW[:, 0:cols], ALU.is_gt, ALU.add))
                b.v(lambda e: e.tensor_scalar(a.bbW[:, 0:cols], aw, 2.5, None, ALU.is_gt))
                b.v(lambda e: e.scalar_tensor_tensor(
                    a.bbW[:, 0:cols], aw, 3.5, a.bbW[:, 0:cols], ALU.is_gt, ALU.add))
                b.v(lambda e: e.tensor_scalar(a.dqW[:, 0:cols], aw, 5.0, 2.0,
                                              ALU.is_gt, ALU.mult))
                b.v(lambda e: e.tensor_add(a.bbW[:, 0:cols], a.bbW[:, 0:cols],
                                           a.dqW[:, 0:cols]))
                b.v(lambda e: e.scalar_tensor_tensor(
                    a.bbW[:, 0:cols], a.acW[:, 0:cols], 0.5, a.bbW[:, 0:cols],
                    ALU.mult, ALU.add))
                b.v(lambda e: e.tensor_mul(a.bbW[:, 0:cols], a.bbW[:, 0:cols],
                                           a.sgW[:, 0:cols]))
                dq3 = a.dqW[:, 0:cols].rearrange("p (bl k) -> p bl k", k=16)
                qs3 = a.bbW[:, 0:cols].rearrange("p (bl k) -> p bl k", k=16)
                b.v(lambda e: e.tensor_tensor(dq3, qs3, sh_b, ALU.mult))
                return b.vpub()

            w1_items = ([(h.wg, h.wgT_scr, t, D, DC, 0, 0) for t in range(IC)] +
                        [(h.wu, h.wuT_scr, t, D, DC, 0, 0) for t in range(IC)] +
                        [(h.wd, h.wdT_scr, t, 1408, 11, hh * 1408, hh * 11)
                         for t in range(DC) for hh in range(2)])
            w1s = {"wp": 0, "sm_wtr": 0, "store": {}}

            def w1_load(wp):
                hsrc, scr, t, cols, nchunk, coloff, choff = w1_items[wp]
                par = wp % 2
                if wp >= 2:
                    b.w("sync", "SV", ssn[("w1ld", wp - 2)])
                b.dma("sync", f"dA{par}", a.wf[par][:, 0:cols],
                      hsrc[t * 128:(t + 1) * 128, coloff:coloff + cols])
                ssn[("w1ldc", wp)] = b.C[f"dA{par}"]

            def w1_step():
                wp = w1s["wp"]
                if wp >= len(w1_items):
                    return
                sv_old, ss_old = b.dall_v, b.dall_s
                b.dall_v = False
                b.dall_s = False
                try:
                    w1_step_body(wp)
                finally:
                    b.dall_v = sv_old
                    b.dall_s = ss_old

            def w1_step_body(wp):
                if wp == 0:
                    w1_load(0)
                    w1_load(1)
                hsrc, scr, t, cols, nchunk, coloff, choff = w1_items[wp]
                par = wp % 2
                b.w("vector", f"dA{par}", ssn[("w1ldc", wp)])
                if wp >= 1:
                    b.w("vector", "SM", w1s["sm_wtr"])
                sv_wdeq = quant_ladder_w(b, a.wf[par], cols, None)
                ssn[("w1ld", wp)] = sv_wdeq
                if wp + 2 < len(w1_items):
                    w1_load(wp + 2)
                spar = wp % 2
                if wp >= 1:
                    osp = (wp - 1) % 2
                    b.w("tensor", f"dSt{osp}", w1s["store"].get(osp, 0))
                if wp >= 2:
                    b.w("tensor", f"dSt{spar}", w1s["store"].get(spar, 0))
                rounds = [range(r, min(r + 8, nchunk)) for r in range(0, nchunk, 8)]
                for ri, rng in enumerate(rounds):
                    b.w("tensor", "SV", sv_wdeq)
                    if ri > 0 or wp > 0:
                        b.w("tensor", "SV", ssn.get(("w1evac_last",), 0))
                    def trs(e, rng=rng):
                        for j, ch in enumerate(rng):
                            e.transpose(a.psWT[:, j * 128:(j + 1) * 128],
                                        a.dqW[:, ch * 128:(ch + 1) * 128], a.ident[:])
                    b.t(trs)
                    smx = b.tpub()
                    w1s["sm_wtr"] = smx
                    b.w("vector", "SM", smx)
                    def evac(e, rng=rng):
                        n = len(rng) * 128
                        c0 = rng[0] * 128
                        e.tensor_copy(a.wst[0][:, c0:c0 + n], a.psWT[:, 0:n])
                    b.v(evac)
                    ssn[("w1evac_last",)] = b.vpub()
                b.w("scalar", "SV", ssn[("w1evac_last",)])
                dst = scr[choff:choff + nchunk, :, t * 128:(t + 1) * 128].rearrange(
                    "ch p j -> p ch j")
                src = a.wst[0][:, 0:nchunk * 128].rearrange(
                    "p (ch j) -> p ch j", j=128)
                w1s["store"][spar] = b.dma("scalar", f"dSt{spar}", dst, src)
                w1s["wp"] = wp + 1

            b.dall_v = False
            b.v(lambda e: e.memset(a.zmat[:], 1.0))
            sv_zinit = b.vpub()
            b.w("sync", "SS")
            sv_x2 = {}
            ss_x2sq = {}
            st_x2 = {}
            sm_scores = {}
            ss_exp = {}
            ss_zevac = {}
            sm_bc = {}
            sv_normt = {}
            sm_group_end = 0
            xr_cnt = {}
            zspread_last = 0
            zrecip_last = 0
            zflat_busy = 0
            ktdone_last = {}
            sm_pv_last = {}
            sm_pv_end = 0
            sv_x2_last = 0
            for g in range(NG):
                b.dall_v = True
                b.dall_s = True
                if g > 0:
                    b.w("sync", "SM", sm_group_end)
                qsrc = h.qT_scr[:, :, g * GQ:(g + 1) * GQ]
                b.dma("sync", "dQt", a.qTg[:].rearrange("p (dc q) -> p dc q", q=GQ),
                      qsrc.rearrange("dc p q -> p dc q"))
                dqt_cnt = b.C["dQt"]
                ktb_cnt = {}
                for t in range(NT):
                    tb = t // ZR
                    pb = tb % 2
                    if t % ZR == 0:
                        if pb in ktdone_last:
                            b.w("sync", "SM", ktdone_last[pb])
                        ksrc = h.kT_scr[:, :, tb * ZR * 128:(tb + 1) * ZR * 128]
                        b.dma("sync", f"dKt{pb}",
                              a.ktb[pb][:].rearrange("p (dc tj) -> p dc tj", tj=ZR * 128),
                              ksrc.rearrange("dc p tj -> p dc tj"))
                        ktb_cnt[tb] = b.C[f"dKt{pb}"]
                    pS = t % 2
                    b.w("tensor", f"dKt{pb}", ktb_cnt[tb])
                    b.w("tensor", "dQt", dqt_cnt)
                    if (g, t - 2) in ss_zevac:
                        b.w("tensor", "SS", ss_zevac[(g, t - 2)])
                    elif g > 0 and t < 2:
                        b.w("tensor", "SS", ss_zevac[(g - 1, NT - 1)])
                        b.w("tensor", "SV", sv_x2_last)
                    def mms(e, t=t, pS=pS, pb=pb):
                        toff = (t % ZR) * 128
                        for dc in range(DC):
                            lhsT = a.ktb[pb][:, dc * ZR * 128 + toff:
                                             dc * ZR * 128 + toff + 128]
                            for (off, ln) in iblk(GQ):
                                e.matmul(a.psS[pS][:, off:off + ln], lhsT,
                                         a.qTg[:, dc * GQ + off: dc * GQ + off + ln],
                                         start=(dc == 0), stop=(dc == DC - 1))
                    b.t(mms)
                    sm_scores[(g, t)] = b.tpub()
                    if t == NT - 1:
                        sm_group_end = sm_scores[(g, t)]
                    b.w("scalar", "SM", sm_scores[(g, t)])
                    if t == 0 and g > 0:
                        b.w("scalar", "SM", sm_pv_end)
                    b.s(lambda e, t=t, pS=pS: e.activation(
                        a.pT[:, t * GQ:(t + 1) * GQ], a.psS[pS][:], AF.Exp))
                    ss_exp[(g, t)] = b.spub()
                    b.w("tensor", "SS", ss_exp[(g, t)])
                    def mmz(e, t=t, pS=pS):
                        for (off, ln) in iblk(GQ):
                            e.matmul(a.psS[pS][0:1, off:off + ln], a.ones_c[:],
                                     a.pT[:, t * GQ + off: t * GQ + off + ln],
                                     start=True, stop=True)
                    b.t(mmz)
                    smz = b.tpub()
                    b.w("scalar", "SM", smz)
                    b.w("scalar", "SV", zrecip_last if zrecip_last else sv_zinit)
                    b.s(lambda e, t=t, pS=pS: e.activation(
                        a.zmat[(t % ZR) * 32:(t % ZR) * 32 + 1, :],
                        a.psS[pS][0:1, :], AF.Copy, scale=32.0))
                    ss_zevac[(g, t)] = b.spub()
                    w1_step()
                    if t % ZR == ZR - 1:
                        ktdone_last[pb] = sm_scores[(g, t)]
                        r = t // ZR
                        b.w("vector", "SS", ss_zevac[(g, t)])
                        b.vn(lambda e: e.tensor_scalar(a.zmat[:], a.zmat[:],
                                                       a.cnrm[0:64, :], None, ALU.mult))
                        b.vd()
                        b.vn(lambda e: e.reciprocal(a.zmat[:], a.zmat[:]))
                        b.vd()
                        b.w("vector", "SM", zflat_busy)
                        b.vn(lambda e: e.tensor_copy(a.zmatB[:], a.zmat[:]))
                        b.vd()
                        zrecip_last = b.vpub()
                        for tt in range(r * ZR, (r + 1) * ZR):
                            b.w("tensor", "SV", zrecip_last)
                            if (g, tt - 1) in sv_normt:
                                b.w("tensor", "SV", sv_normt[(g, tt - 1)])
                            def mmb(e, tt=tt):
                                ro = (tt % ZR) * 32
                                for (off, ln) in iblk(GQ):
                                    e.matmul(a.psB[0][:, off:off + ln],
                                             a.ones_r[ro:ro + 1, :],
                                             a.zmatB[ro:ro + 1, off:off + ln],
                                             start=True, stop=True)
                            b.t(mmb)
                            sm_bc[(g, tt)] = b.tpub()
                            b.w("vector", "SM", sm_bc[(g, tt)])
                            b.w("vector", "SS", ss_exp[(g, tt)])
                            def nrm(e, tt=tt):
                                sl = a.pT[:, tt * GQ:(tt + 1) * GQ]
                                e.tensor_tensor(sl, sl, a.psB[0][:], ALU.mult)
                            b.v(nrm)
                            sv_normt[(g, tt)] = b.vpub()
                        zflat_busy = sm_bc[(g, (r + 1) * ZR - 1)]

                b.dall_v = True
                b.dall_s = True
                ss_oT = 0
                vt_cnt = {}
                WAVES = [(0, 3), (3, 6), (6, 8)]
                for wave, (dlo, dhi) in enumerate(WAVES):
                    for t in range(NT):
                        pv = t % 2
                        if pv in sm_pv_last:
                            b.w("sync", "SM", sm_pv_last[pv])
                        vt_cnt[(wave, t)] = b.dma("sync", f"dVt{pv}",
                                                  a.vrn[pv][:], h.v_scr[t])
                        b.w("tensor", f"dVt{pv}", vt_cnt[(wave, t)])
                        if wave == 0 and t == 0:
                            b.w("tensor", "SV", sv_normt[(g, NT - 1)])
                            if g > 0:
                                b.w("tensor", "SS", ssn[("afdone", g - 1)])
                        if wave > 0 and t == 0:
                            b.w("tensor", "SS", ss_oT)
                        def mpv(e, dlo=dlo, dhi=dhi, t=t, pv=pv):
                            for dcw in range(dhi - dlo):
                                dc = dlo + dcw
                                lhsT = a.vrn[pv][:, dc * 128:(dc + 1) * 128]
                                for (off, ln) in iblk(GQ):
                                    e.matmul(a.psO[dcw][:, off:off + ln], lhsT,
                                             a.pT[:, t * GQ + off: t * GQ + off + ln],
                                             start=(t == 0), stop=(t == NT - 1))
                        b.t(mpv)
                        sm_pv_last[pv] = b.tpub()
                    b.w("scalar", "SM", sm_pv_last[(NT - 1) % 2])
                    def oevac(e, dlo=dlo, dhi=dhi):
                        for dcw in range(dhi - dlo):
                            dc = dlo + dcw
                            e.copy(a.oT[:, dc * GQ:(dc + 1) * GQ], a.psO[dcw][:])
                    b.s(oevac)
                    ss_oT = b.spub()
                sm_pv_end = sm_pv_last[(NT - 1) % 2]

                for qt in range(TPG):
                    pq = qt % 2
                    row0 = g * GQ + qt * 128
                    prev = (g, qt - 1) if qt >= 1 else ((g - 1, TPG - 1) if g > 0 else None)
                    if prev is not None:
                        b.w("sync", "SV", sv_x2[prev])
                        b.w("sync", "SS", ss_x2sq[prev])
                    xr_cnt[(g, qt)] = b.dma("sync", f"dXr{pq}", a.xres[0][:],
                                            h.x[row0:row0 + 128, :])
                    b.w("tensor", "SS", ss_oT)
                    if (g, qt - 2) in sv_x2:
                        b.w("tensor", "SV", sv_x2[(g, qt - 2)])
                    elif g > 0:
                        b.w("tensor", "SV", sv_x2[(g - 1, TPG - 2 + qt)])
                    def mwo(e, qt=qt, pq=pq):
                        for (off, ln) in iblk(D):
                            e.matmul(a.psAF[pq][:, off:off + ln], a.ones_r[0:1, :],
                                     a.boB[0:1, off:off + ln], start=True, stop=False)
                        for kc in range(DC):
                            lhsT = a.oT[:, kc * GQ + qt * 128: kc * GQ + (qt + 1) * 128]
                            for (off, ln) in iblk(D):
                                e.matmul(a.psAF[pq][:, off:off + ln], lhsT,
                                         a.woT[:, kc * 1024 + off: kc * 1024 + off + ln],
                                         start=False, stop=(kc == DC - 1))
                    b.t(mwo)
                    smw = b.tpub()
                    if g == NG - 1 and qt == TPG - 1:
                        ssn[("afdone_last",)] = smw
                    b.w("vector", "SM", smw)
                    b.w("vector", f"dXr{pq}", xr_cnt[(g, qt)])
                    if prev is not None:
                        b.w("vector", f"dX2{prev[1] % 2}", st_x2[prev])
                        b.w("vector", "SS", ss_x2sq[prev])
                    b.v(lambda e, pq=pq: e.scalar_tensor_tensor(
                        a.x2st[0][:], a.psAF[pq][:], 1.0, a.xres[0][:],
                        ALU.mult, ALU.add))
                    idx = g * TPG + qt
                    b.v(lambda e, idx=idx: e.tensor_reduce(
                        a.sD1[:, idx:idx + 1], a.x2st[0][:], AX.X, ALU.add))
                    sv_x2[(g, qt)] = b.vpub()
                    sv_x2_last = sv_x2[(g, qt)]
                    b.w("scalar", "SV", sv_x2[(g, qt)])
                    b.s(lambda e, idx=idx: e.activation(
                        a.xres[0][:], a.x2st[0][:], AF.Square, accum_out=a.sD2[:, idx:idx + 1]))
                    ss_x2sq[(g, qt)] = b.spub()
                    st_x2[(g, qt)] = b.dma("scalar", f"dX2{pq}",
                                           h.x2_scr[row0:row0 + 128, :], a.x2st[0][:])
                ssn[("afdone", g)] = ss_x2sq[(g, TPG - 1)]

            b.dall_v = False
            b.dall_s = False
            while w1s["wp"] < len(w1_items):
                w1_step()

            b.barrier()
            b.w("sync", "dX20", b.C["dX20"])
            b.w("sync", "dX21", b.C["dX21"])

            sv_lnD = ln_batch(b, a.sD1, a.sD2, a.rD, a.nD, a.tmp1)
            b.dma("sync", "dDw", a.wgT[:].rearrange("p (dc i) -> p dc i", i=I),
                  h.wgT_scr.rearrange("dc p i -> p dc i"))
            b.dma("sync", "dDw", a.wuT[:].rearrange("p (dc i) -> p dc i", i=I),
                  h.wuT_scr.rearrange("dc p i -> p dc i"))
            b.dma("sync", "dDw", a.wdT[:].rearrange("p (ic d) -> p ic d", d=D),
                  h.wdT_scr.rearrange("ic p d -> p ic d"))
            dw_cnt = b.C["dDw"]

            def quant_ladder_a(b, src_bf, cols):
                blocks = cols // 16
                s3 = src_bf[:, 0:cols].rearrange("p (bl k) -> p bl k", k=16)
                b.vn(lambda e: e.tensor_reduce(a.bmD[:, 0:blocks], s3, AX.X, ALU.max,
                                               apply_absolute_value=True))
                b.vd()
                b.vn(lambda e: e.reciprocal(a.rcD[:, 0:blocks], a.bmD[:, 0:blocks]))
                b.vd()
                b.vn(lambda e: e.tensor_scalar(a.rcD[:, 0:blocks], a.rcD[:, 0:blocks],
                                               6.0, None, ALU.mult))
                b.vn(lambda e: e.tensor_scalar(a.shD[:, 0:blocks], a.bmD[:, 0:blocks],
                                               1.0 / 6.0, None, ALU.mult))
                b.vd()
                rc_b = a.rcD[:, 0:blocks].unsqueeze(2).broadcast_to([128, blocks, 16])
                sh_b = a.shD[:, 0:blocks].unsqueeze(2).broadcast_to([128, blocks, 16])
                xn3 = a.xnD[:, 0:cols].rearrange("p (bl k) -> p bl k", k=16)
                b.v(lambda e: e.tensor_tensor(xn3, s3, rc_b, ALU.mult))
                svx = b.vpub()
                b.w("scalar", "SV", svx)
                b.s(lambda e: e.sign(a.sgnD[:, 0:cols], a.xnD[:, 0:cols]))
                b.s(lambda e: e.activation(a.xnD[:, 0:cols], a.xnD[:, 0:cols], AF.Abs))
                sss = b.spub()
                b.w("vector", "SS", sss)
                aw = a.xnD[:, 0:cols]
                AC, BB, C1, C2_, SG = (a.acD[:, 0:cols], a.bbD[:, 0:cols],
                                       a.cD1[:, 0:cols], a.cD2[:, 0:cols],
                                       a.sgD[:, 0:cols])
                b.vn(lambda e: e.tensor_scalar(AC, aw, 0.25, None, ALU.is_gt))
                b.vn(lambda e: e.tensor_scalar(BB, aw, 0.75, None, ALU.is_gt))
                b.vn(lambda e: e.tensor_scalar(C1, aw, 1.25, None, ALU.is_gt))
                b.vn(lambda e: e.tensor_scalar(C2_, aw, 1.75, None, ALU.is_gt))
                b.vd()
                b.vn(lambda e: e.tensor_add(AC, AC, BB))
                b.vn(lambda e: e.tensor_add(C1, C1, C2_))
                b.vn(lambda e: e.tensor_scalar(SG, aw, 2.5, None, ALU.is_gt))
                b.vd()
                b.vn(lambda e: e.tensor_add(AC, AC, C1))
                b.vn(lambda e: e.tensor_scalar(BB, aw, 3.5, None, ALU.is_gt))
                b.vn(lambda e: e.tensor_scalar(C2_, aw, 5.0, 2.0, ALU.is_gt, ALU.mult))
                b.vd()
                b.vn(lambda e: e.tensor_add(BB, BB, SG))
                b.vd()
                b.vn(lambda e: e.tensor_add(BB, BB, C2_))
                b.vd()
                b.vn(lambda e: e.scalar_tensor_tensor(BB, AC, 0.5, BB, ALU.mult, ALU.add))
                b.vd()
                b.vn(lambda e: e.tensor_mul(BB, BB, a.sgnD[:, 0:cols]))
                b.vd()
                dq3 = a.sgD[:, 0:cols].rearrange("p (bl k) -> p bl k", k=16)
                qs3 = a.bbD[:, 0:cols].rearrange("p (bl k) -> p bl k", k=16)
                b.v(lambda e: e.tensor_tensor(dq3, qs3, sh_b, ALU.mult))
                return b.vpub()

            b.dall_v = True
            b.dall_s = True
            xl_cnt = {}
            st_h2 = {}
            for i in range(NT):
                par = i % 2
                if i >= 2:
                    b.w("sync", "SV", ssn[("h2xn", i - 2)])
                xl_cnt[i] = b.dma("sync", f"dXl{par}", a.x2t[par][:],
                                  h.x2_scr[i * 128:(i + 1) * 128, :])
                b.w("scalar", f"dXl{par}", xl_cnt[i])
                b.w("scalar", "SV", sv_lnD)
                if i >= 1:
                    b.w("scalar", "SV", ssn[("h2xn", i - 1)])
                b.s(lambda e, i=i, par=par: e.activation(
                    a.h2b, a.x2t[par][:], AF.Identity,
                    bias=a.nD[:, i:i + 1], scale=a.rD[:, i:i + 1]))
                ss_h2 = b.spub()
                b.w("vector", "SS", ss_h2)
                if i >= 1:
                    b.w("vector", "SM", ssn[("h2tr", i - 1)])
                sv_q2c = quant_ladder_a(b, a.h2b, D)
                ssn[("h2xn", i)] = sv_q2c
                b.w("tensor", "SV", sv_q2c)
                if i >= 1:
                    b.w("tensor", "SV", ssn[("h2ev", i - 1)])
                def tr2(e):
                    for dc in range(DC):
                        e.transpose(a.psT2[:, dc * 128:(dc + 1) * 128],
                                    a.sgD[:, dc * 128:(dc + 1) * 128], a.ident[:])
                b.t(tr2)
                smt = b.tpub()
                ssn[("h2tr", i)] = smt
                b.w("vector", "SM", smt)
                pq = i % 2
                if i >= 2:
                    b.w("vector", f"dSt{pq}", st_h2[i - 2])
                b.v(lambda e, pq=pq: e.tensor_copy(a.h2qT[pq][:], a.psT2[:]))
                ssn[("h2ev", i)] = b.vpub()
                b.w("scalar", "SV", ssn[("h2ev", i)])
                st_h2[i] = b.dma("scalar", f"dSt{pq}", h.h2qT_scr[i], a.h2qT[pq][:])

            b.dall_v = False
            b.dall_s = False
            y_cnt = {}
            sv_y = {}
            h2l_cnt = {}
            for i in range(NT):
                par = i % 2
                if i >= 2:
                    b.w("sync", "SM", ssn[("up_done", i - 2)])
                else:
                    b.w("sync", f"dSt{par}", st_h2[NT - 2 + par])
                h2l_cnt[i] = b.dma("sync", f"dKt{par}", a.h2qT[par][:], h.h2qT_scr[i])
                if i >= 2:
                    b.w("sync", "SV", sv_y[i - 2])
                xl2 = b.dma("sync", f"dXl{par}", a.x2t[par][:],
                            h.x2_scr[i * 128:(i + 1) * 128, :])
                ssn[("xl2", i)] = xl2
                b.w("tensor", f"dKt{par}", h2l_cnt[i])
                b.w("tensor", "dDw", dw_cnt)
                if i >= 1:
                    b.w("tensor", "SV", ssn[("h3mul", i - 1)])
                def mg(e, par=par):
                    for dc in range(DC):
                        lhsT = a.h2qT[par][:, dc * 128:(dc + 1) * 128]
                        for (off, ln) in iblk(I):
                            e.matmul(a.psGU[:, off:off + ln], lhsT,
                                     a.wgT[:, dc * I + off: dc * I + off + ln],
                                     start=(dc == 0), stop=(dc == DC - 1))
                b.t(mg)
                smg = b.tpub()
                b.w("scalar", "SM", smg)
                if i >= 1:
                    b.w("scalar", "SM", ssn[("tr3", i - 1)])
                b.s(lambda e: e.activation(a.sgD[:, 0:I], a.psGU[:, 0:I], AF.Silu))
                ss_sg = b.spub()
                b.w("tensor", "SS", ss_sg)
                def mu(e, par=par):
                    for dc in range(DC):
                        lhsT = a.h2qT[par][:, dc * 128:(dc + 1) * 128]
                        for (off, ln) in iblk(I):
                            e.matmul(a.psGU[:, off:off + ln], lhsT,
                                     a.wuT[:, dc * I + off: dc * I + off + ln],
                                     start=(dc == 0), stop=(dc == DC - 1))
                b.t(mu)
                smu = b.tpub()
                ssn[("up_done", i)] = smu
                b.w("vector", "SM", smu)
                if i >= 1:
                    b.w("vector", "SV", ssn[("h3q", i - 1)])
                b.v(lambda e: e.tensor_tensor(a.h3D[:, 0:I], a.psGU[:, 0:I],
                                              a.sgD[:, 0:I], ALU.mult))
                ssn[("h3mul", i)] = b.vpub()
                b.w("vector", "SV", ssn[("h3mul", i)])
                sv_q3 = quant_ladder_a(b, a.h3D, I)
                ssn[("h3q", i)] = sv_q3
                rounds = [range(r, min(r + 8, IC)) for r in range(0, IC, 8)]
                last_evac = 0
                for ri, rng in enumerate(rounds):
                    b.w("tensor", "SV", sv_q3 if ri == 0 else last_evac)
                    if ri == 0 and i >= 1:
                        b.w("tensor", "SV", ssn[("h3ev", i - 1)])
                        b.w("tensor", "SV", sv_y[i - 1])
                    def tr3(e, rng=rng):
                        for j, ch in enumerate(rng):
                            e.transpose(a.psT2[:, j * 128:(j + 1) * 128],
                                        a.sgD[:, ch * 128:(ch + 1) * 128], a.ident[:])
                    b.t(tr3)
                    smr = b.tpub()
                    b.w("vector", "SM", smr)
                    def ev3(e, rng=rng):
                        n = len(rng) * 128
                        c0 = rng[0] * 128
                        e.tensor_copy(a.h3qT[:, c0:c0 + n], a.psT2[:, 0:n])
                    b.v(ev3)
                    last_evac = b.vpub()
                ssn[("tr3", i)] = b.C["SM"]
                ssn[("h3ev", i)] = last_evac
                b.w("tensor", "SV", last_evac)
                if i >= 1:
                    b.w("tensor", "SV", sv_y[i - 1])
                def md(e):
                    for ic in range(IC):
                        lhsT = a.h3qT[:, ic * 128:(ic + 1) * 128]
                        for (off, ln) in iblk(D):
                            e.matmul(a.psD[:, off:off + ln], lhsT,
                                     a.wdT[:, ic * D + off: ic * D + off + ln],
                                     start=(ic == 0), stop=(ic == IC - 1))
                b.t(md)
                smd = b.tpub()
                b.w("vector", "SM", smd)
                b.w("vector", f"dXl{par}", ssn[("xl2", i)])
                if i >= 2:
                    b.w("vector", f"dY{par}", y_cnt[i - 2])
                b.v(lambda e, par=par: e.scalar_tensor_tensor(
                    a.yst[par][:], a.psD[:], 1.0, a.x2t[par][:], ALU.mult, ALU.add))
                sv_y[i] = b.vpub()
                b.w("scalar", "SV", sv_y[i])
                y_cnt[i] = b.dma("scalar", f"dY{par}",
                                 h.y[i * 128:(i + 1) * 128, :], a.yst[par][:])

            b.w("scalar", "dY0", b.C["dY0"])
            b.w("scalar", "dY1", b.C["dY1"])

        with nc.Block() as block:
            refC = {}

            def mk(E):
                def body(eng):
                    bb = B(E, eng, sems, safe=safe)
                    schedule(bb)
                    refC[E] = dict(bb.C)
                return body
            block.sync(mk("sync"))
            block.tensor(mk("tensor"))
            block.vector(mk("vector"))
            block.scalar(mk("scalar"))
            block.gpsimd(lambda eng: None)
            vals = list(refC.values())
            for v in vals[1:]:
                assert v == vals[0], "pass counter mismatch"

    return nc


_NC_CACHE = {}


def _get_nc(S):
    if S not in _NC_CACHE:
        _NC_CACHE[S] = build(S)
    return _NC_CACHE[S]


def make_inputs(x_b, wq, bq, wk, bk, wv, bv, wo, bo, wg, wu, wd):
    return {
        "x": np.ascontiguousarray(x_b, dtype=np.float32),
        "wq": np.ascontiguousarray(wq, dtype=np.float32),
        "wk": np.ascontiguousarray(wk, dtype=np.float32),
        "wv": np.ascontiguousarray(wv, dtype=np.float32),
        "wo": np.ascontiguousarray(wo, dtype=np.float32),
        "wg": np.ascontiguousarray(wg, dtype=np.float32),
        "wu": np.ascontiguousarray(wu, dtype=np.float32),
        "wd": np.ascontiguousarray(wd, dtype=np.float32),
        "bqT": np.ascontiguousarray(np.asarray(bq, dtype=np.float32).reshape(DC, 128).T),
        "bkT": np.ascontiguousarray(np.asarray(bk, dtype=np.float32).reshape(DC, 128).T),
        "bv_row": np.ascontiguousarray(np.asarray(bv, dtype=np.float32).reshape(1, D)),
        "bo_row": np.ascontiguousarray(np.asarray(bo, dtype=np.float32).reshape(1, D)),
        "ident": np.eye(128, dtype=ml_dtypes.bfloat16),
    }


def kernel(L0_attn_batch, L0_attn_Wq, L0_attn_bq, L0_attn_Wk, L0_attn_bk,
           L0_attn_Wv, L0_attn_bv, L0_attn_Wo, L0_attn_bo,
           L0_ffn_w_gate, L0_ffn_w_up, L0_ffn_w_down):
    xs = np.ascontiguousarray(np.asarray(L0_attn_batch, dtype=np.float32))
    Bn, S, Dm = xs.shape
    assert Dm == D
    nc = _get_nc(S)
    in_maps = []
    for c in range(Bn):
        in_maps.append(make_inputs(xs[c], L0_attn_Wq, L0_attn_bq, L0_attn_Wk,
                                   L0_attn_bk, L0_attn_Wv, L0_attn_bv,
                                   L0_attn_Wo, L0_attn_bo,
                                   L0_ffn_w_gate, L0_ffn_w_up, L0_ffn_w_down))
    res = run_bass_kernel_spmd(nc, in_maps, core_ids=list(range(Bn)))
    outs = [np.asarray(r["y"]) for r in res.results]
    if outs[0].dtype != np.float64:
        outs = [o.astype(np.float32, copy=False) for o in outs]
    return np.stack(outs)


# version 39 (final)
# speedup vs baseline: 1.0854x; 1.0541x over previous
import numpy as np
import ml_dtypes
from contextlib import ExitStack
from types import SimpleNamespace

import concourse.bass as bass
import concourse.mybir as mybir
from concourse.bass_utils import run_bass_kernel_spmd

F32 = mybir.dt.float32
BF16 = mybir.dt.bfloat16
AF = mybir.ActivationFunctionType
ALU = mybir.AluOpType
AX = mybir.AxisListType

D = 1024
I = 2816
DC = D // 128
IC = I // 128
NBD = D // 512
NBI = (I + 511) // 512
ZR = 2
LNEPS = 1e-5
RECIP_NORM = float(np.float32(32.000001))


def iblk(n):
    out = []
    o = 0
    while o < n:
        out.append((o, min(512, n - o)))
        o += 512
    return out


class Cap:

    def __init__(self, eng):
        object.__setattr__(self, "_e", eng)
        object.__setattr__(self, "last", None)

    def __getattr__(self, nm):
        a = getattr(self._e, nm)
        if callable(a):
            def w(*args, **kw):
                r = a(*args, **kw)
                if hasattr(r, "then_inc"):
                    object.__setattr__(self, "last", r)
                return r
            return w
        return a


class B:

    def __init__(self, E, eng, sems, safe=False):
        self.E = E
        self.eng = eng
        self.cap = Cap(eng) if eng is not None else None
        self.safe = safe
        self.sems = sems
        self.C = {k: 0 for k in sems}

    def w(self, engine, sem, val=None):
        if val is None:
            val = self.C[sem]
        if self.E == engine and val > 0:
            self.eng.wait_ge(self.sems[sem], int(val))

    def dma(self, issuer, sem, out, in_):
        self.C[sem] += 16
        if self.E == issuer:
            if issuer == "scalar":
                self.cap.drain()
            self.eng.dma_start(out=out, in_=in_).then_inc(self.sems[sem], 16)
        return self.C[sem]

    def t(self, fn):
        if self.E == "tensor":
            fn(self.cap)

    def tpub(self):
        self.C["SM"] += 1
        if self.E == "tensor":
            self.cap.last.then_inc(self.sems["SM"], 1)
        return self.C["SM"]

    dall_v = False
    dall_s = False

    def v(self, fn):
        if self.E == "vector":
            fn(self.cap)
            if self.dall_v:
                self.cap.drain()

    def vn(self, fn):
        if self.E == "vector":
            fn(self.cap)

    def vd(self):
        if self.E == "vector":
            self.cap.drain()

    def vpub(self):
        self.C["SV"] += 1
        if self.E == "vector":
            self.cap.drain()
            self.cap.last.then_inc(self.sems["SV"], 1)
        return self.C["SV"]

    def s(self, fn):
        if self.E == "scalar":
            fn(self.cap)
            if self.dall_s:
                self.cap.drain()

    def spub(self):
        self.C["SS"] += 1
        if self.E == "scalar":
            self.cap.drain()
            self.cap.last.then_inc(self.sems["SS"], 1)
        return self.C["SS"]

    def barrier(self):
        for e in ("sync", "tensor", "vector", "scalar"):
            for sem in self.sems:
                self.w(e, sem)


def build(S, safe=False):
    NT = S // 128
    GQ = min(1024, S)
    NG = S // GQ
    TPG = GQ // 128
    NBG = GQ // 512
    NTB = NT // ZR

    nc = bass.Bass("TRN2", target_bir_lowering=False, debug=False,
                   detect_race_conditions=bool(safe))

    h = SimpleNamespace()
    h.x = nc.dram_tensor("x", [S, D], F32, kind="ExternalInput").ap()
    for nm, shp in [("wq", [D, D]), ("wk", [D, D]), ("wv", [D, D]), ("wo", [D, D]),
                    ("wg", [I, D]), ("wu", [I, D]), ("wd", [D, I])]:
        setattr(h, nm, nc.dram_tensor(nm, shp, F32, kind="ExternalInput").ap())
    h.bqT = nc.dram_tensor("bqT", [128, DC], F32, kind="ExternalInput").ap()
    h.bkT = nc.dram_tensor("bkT", [128, DC], F32, kind="ExternalInput").ap()
    h.bv = nc.dram_tensor("bv_row", [1, D], F32, kind="ExternalInput").ap()
    h.bo = nc.dram_tensor("bo_row", [1, D], F32, kind="ExternalInput").ap()
    h.ident = nc.dram_tensor("ident", [128, 128], BF16, kind="ExternalInput").ap()
    h.y = nc.dram_tensor("y", [S, D], F32, kind="ExternalOutput").ap()

    h.qT_scr = nc.dram_tensor("qT_scr", [DC, 128, S], BF16, kind="Internal").ap()
    h.kT_scr = nc.dram_tensor("kT_scr", [DC, 128, S], BF16, kind="Internal").ap()
    h.v_scr = nc.dram_tensor("v_scr", [NT, 128, D], BF16, kind="Internal").ap()
    h.x2_scr = nc.dram_tensor("x2_scr", [S, D], F32, kind="Internal").ap()
    h.h2qT_scr = nc.dram_tensor("h2qT_scr", [NT, 128, D], BF16, kind="Internal").ap()
    h.wgT_scr = nc.dram_tensor("wgT_scr", [DC, 128, I], BF16, kind="Internal").ap()
    h.wuT_scr = nc.dram_tensor("wuT_scr", [DC, 128, I], BF16, kind="Internal").ap()
    h.wdT_scr = nc.dram_tensor("wdT_scr", [IC, 128, D], BF16, kind="Internal").ap()

    ctx = ExitStack()
    a = SimpleNamespace()

    def sb(name, shape, dt, scope=ctx):
        return scope.enter_context(nc.sbuf_tensor(name, shape, dt))

    def ps(name, shape, dt, scope):
        return scope.enter_context(nc.psum_tensor(name, shape, dt))

    sem_names = ["SM", "SV", "SS",
                 "dA0", "dA1", "dW", "dM", "dSt0", "dSt1",
                 "dKt0", "dKt1", "dQt", "dVt0", "dVt1", "dXr0", "dXr1",
                 "dX20", "dX21", "dZ", "dDw", "dXl0", "dXl1", "dY0", "dY1"]

    with ctx:
        sems = {nm: ctx.enter_context(nc.semaphore(nm)) for nm in sem_names}

        a.ident = sb("identB", [128, 128], BF16)
        a.ones_c = sb("ones_c", [128, 1], BF16)
        a.ones_r = sb("ones_r", [64, 128], BF16)
        a.bqT = sb("bqT_s", [128, DC], F32)
        a.bkT = sb("bkT_s", [128, DC], F32)

        a.sD1 = sb("sD1", [128, NT], F32)
        a.sD2 = sb("sD2", [128, NT], F32)
        a.rD = sb("rD", [128, NT], F32)
        a.nD = sb("nD", [128, NT], F32)
        a.tmp1 = sb("tmp1", [128, NT], F32)
        a.tieb = sb("tieb", [128, 1], F32)
        a.cnrm = sb("cnrm", [128, 1], F32)

        sLong = ExitStack()
        a.woT = sb("woT", [128, DC * 1024], BF16, sLong)
        a.bvB = sb("bvB", [1, D], BF16, sLong)
        a.boB = sb("boB", [1, D], BF16, sLong)
        with ExitStack() as sAB:
            a.w0f = [sb(f"w0f{i}", [128, 1024], F32, sAB) for i in range(2)]
            a.w0b = sb("w0b", [128, 1024], BF16, sAB)
            a.bv32 = sb("bv32", [1, D], F32, sAB)
            a.bo32 = sb("bo32", [1, D], F32, sAB)
            a.xnT = sb("xnT", [128, DC * S], BF16, sAB)
            a.sA1 = sb("sA1", [128, NT], F32, sAB)
            a.sA2 = sb("sA2", [128, NT], F32, sAB)
            a.rA = sb("rA", [128, NT], F32, sAB)
            a.nA = sb("nA", [128, NT], F32, sAB)
            a.wqT = sb("wqT", [128, DC * 1024], BF16, sAB)
            a.wkT = sb("wkT", [128, DC * 1024], BF16, sAB)
            a.wvT = sb("wvT", [128, DC * 1024], BF16, sAB)
            a.xb = [sb(f"xb{i}", [128, D], F32, sAB) for i in range(2)]
            a.xn = [sb(f"xn{i}", [128, D], BF16, sAB) for i in range(2)]
            a.stageB = [sb(f"stageB{i}", [128, S], BF16, sAB) for i in range(2)]
            a.vst = [sb(f"vst{i}", [128, D], BF16, sAB) for i in range(2)]
            a.sqd = sb("sqdA", [128, D], F32, sAB)
        with ExitStack() as sC:
            a.wf = [sb(f"wf{i}", [128, 1408], F32, sC) for i in range(2)]
            a.bmW = sb("bmW", [128, IC * 8], F32, sC)
            a.shW = sb("shW", [128, IC * 8], F32, sC)
            a.rcW = sb("rcW", [128, IC * 8], F32, sC)
            a.xnW = sb("xnW", [128, 1408], F32, sC)
            a.sgW = sb("sgW", [128, 1408], BF16, sC)
            a.acW = sb("acW", [128, 1408], BF16, sC)
            a.bbW = sb("bbW", [128, 1408], BF16, sC)
            a.dqW = sb("dqW", [128, 1408], BF16, sC)
            a.cW1 = sb("cW1", [128, 1408], BF16, sC)
            a.cW2 = sb("cW2", [128, 1408], BF16, sC)
            a.wst = [sb("wst0", [128, 1408], BF16, sC)]
            a.pT = sb("pT", [128, NT * GQ], BF16, sC)
            a.qTg = sb("qTg", [128, DC * GQ], BF16, sC)
            a.ktb = [sb(f"ktb{i}", [128, ZR * DC * 128], BF16, sC) for i in range(2)]
            a.vrn = [sb(f"vrn{i}", [128, D], BF16, sC) for i in range(2)]
            a.oT = sb("oT", [128, DC * GQ], BF16, sC)
            a.zmat = sb("zmat", [64, GQ], F32, sC)
            a.zmatB = sb("zmatB", [64, GQ], BF16, sC)
            a.xres = [sb("xres0", [128, D], F32, sC)]
            a.x2st = [sb("x2st0", [128, D], F32, sC)]

        sLong.close()
        with ExitStack() as sD:
            a.wgT = sb("wgT", [128, DC * I], BF16, sD)
            a.wuT = sb("wuT", [128, DC * I], BF16, sD)
            a.wdT = sb("wdT", [128, IC * D], BF16, sD)
            a.x2t = [sb(f"x2t{i}", [128, D], F32, sD) for i in range(2)]
            a.h2qT = [sb(f"h2qT{i}", [128, D], BF16, sD) for i in range(2)]
            a.bmD = sb("bmD", [128, IC * 8], F32, sD)
            a.shD = sb("shD", [128, IC * 8], F32, sD)
            a.rcD = sb("rcD", [128, IC * 8], F32, sD)
            a.xnD = sb("xnD", [128, I], BF16, sD)
            a.sgnD = sb("sgnD", [128, I], BF16, sD)
            a.acD = sb("acD", [128, I], BF16, sD)
            a.bbD = sb("bbD", [128, I], BF16, sD)
            a.cD1 = sb("cD1", [128, I], BF16, sD)
            a.cD2 = sb("cD2", [128, I], BF16, sD)
            a.h2b = a.cD2[:, 0:D]
            a.h3D = a.cD1
            a.sgD = sb("sgD", [128, I], BF16, sD)
            a.h3qT = sb("h3qT", [128, I], BF16, sD)
            a.yst = [sb(f"yst{i}", [128, D], F32, sD) for i in range(2)]

        pLong = ExitStack()
        a.psWT = ps("psWT", [128, 1024], BF16, pLong)
        with ExitStack() as p0:
            a.psW0 = ps("psW0", [128, 1024], BF16, p0)
            a.psTA = ps("psTA", [128, 1024], BF16, p0)
        with ExitStack() as p1:
            a.psQK = ps("psQK", [128, 2048], F32, p1)
        with ExitStack() as p2:
            a.psV = [ps(f"psV{i}", [128, D], F32, p2) for i in range(2)]
        with ExitStack() as p4:
            a.psS = [ps(f"psS{i}", [128, GQ], F32, p4) for i in range(2)]
            a.psB = [ps("psB0", [128, GQ], F32, p4)]
        with ExitStack() as p5:
            a.psO = [ps(f"psO{i}", [128, GQ], F32, p5) for i in range(3)]
        with ExitStack() as p6:
            a.psAF = [ps(f"psAF{i}", [128, D], F32, p6) for i in range(2)]
        pLong.close()
        with ExitStack() as p8:
            a.psGU = ps("psGU", [128, NBI * 512], F32, p8)
            with ExitStack() as p7:
                a.psT2 = ps("psT2", [128, 1024], BF16, p7)
            with ExitStack() as p9:
                a.psD = ps("psD", [128, D], F32, p9)

        def schedule(b: B):
            b.dall_v = True
            b.dall_s = True
            ssn = {}
            for hap, dst in [(h.ident, a.ident), (h.bqT, a.bqT), (h.bkT, a.bkT),
                             (h.bv, a.bv32), (h.bo, a.bo32)]:
                b.dma("sync", "dM", dst[:], hap[:])
            b.w("vector", "dM")
            b.v(lambda e: e.memset(a.ones_c[:], 1.0))
            b.v(lambda e: e.memset(a.ones_r[:], 1.0))
            b.v(lambda e: e.tensor_copy(a.bvB[:], a.bv32[:]))
            b.v(lambda e: e.tensor_copy(a.boB[:], a.bo32[:]))
            b.v(lambda e: e.tensor_scalar(a.bqT[:], a.bqT[:], 1.0 / 32.0, None, ALU.mult))
            b.vn(lambda e: e.memset(a.tieb[:], 2.0 ** -49))
            b.vn(lambda e: e.memset(a.cnrm[:], 1e-6))
            b.vd()
            b.v(lambda e: e.tensor_scalar(a.tieb[:], a.tieb[:], -1.0, 1.0, ALU.mult, ALU.add))
            b.v(lambda e: e.tensor_scalar(a.cnrm[:], a.cnrm[:], 1.0 / 32.0, 1.0, ALU.mult, ALU.add))
            sv_misc = b.vpub()

            wT_map = [("wq", h.wq, a.wqT), ("wk", h.wk, a.wkT),
                      ("wv", h.wv, a.wvT), ("wo", h.wo, a.woT)]
            ss_wevac = 0
            sv_wcast = {}
            wld = {}
            wt_i = 0
            for wi, (nm, hap, dst) in enumerate(wT_map):
                for t in range(DC):
                    par = wt_i % 2
                    if wt_i >= 2:
                        b.w("sync", "SV", sv_wcast[wt_i - 2])
                    wld[wt_i] = b.dma("sync", f"dA{par}", a.w0f[par][:],
                                      hap[t * 128:(t + 1) * 128, :])
                    b.w("vector", f"dA{par}", wld[wt_i])
                    if wt_i >= 1:
                        b.w("vector", "SM", ssn.get(("w0tr", wt_i - 1), 0))
                    b.v(lambda e, par=par: e.tensor_copy(a.w0b[:], a.w0f[par][:]))
                    sv_wcast[wt_i] = b.vpub()
                    b.w("tensor", "SV", sv_wcast[wt_i])
                    b.w("tensor", "SV", sv_misc)
                    if wt_i > 0:
                        b.w("tensor", "SS", ss_wevac)
                    def trs(e):
                        for dc in range(DC):
                            e.transpose(a.psW0[:, dc * 128:(dc + 1) * 128],
                                        a.w0b[:, dc * 128:(dc + 1) * 128],
                                        a.ident[:])
                    b.t(trs)
                    sm = b.tpub()
                    ssn[("w0tr", wt_i)] = sm
                    b.w("scalar", "SM", sm)
                    def evac(e, dst=dst, t=t):
                        view = dst[:].rearrange("p (dc q) -> p dc q", q=1024)
                        e.copy(view[:, :, t * 128:(t + 1) * 128], a.psW0[:])
                    b.s(evac)
                    ss_wevac = b.spub()
                    wt_i += 1

            sv_stats = {}
            ss_stats = {}
            NW0 = 4 * DC
            for i in range(NT):
                par = i % 2
                if i >= 2:
                    b.w("sync", "SS", ss_stats[i - 2])
                else:
                    b.w("sync", "SV", sv_wcast[NW0 - 2 + i])
                b.dma("sync", f"dA{par}", a.xb[par][:], h.x[i * 128:(i + 1) * 128, :])
                b.w("vector", f"dA{par}")
                b.v(lambda e, i=i, par=par: e.tensor_reduce(
                    a.sA1[:, i:i + 1], a.xb[par][:], AX.X, ALU.add))
                sv_stats[i] = b.vpub()
                b.w("scalar", f"dA{par}")
                b.w("scalar", "SV", sv_stats[i])
                b.s(lambda e, i=i, par=par: e.activation(
                    a.sqd[:], a.xb[par][:], AF.Square, accum_out=a.sA2[:, i:i + 1]))
                ss_stats[i] = b.spub()

            def ln_batch(b, s1, s2, rst, nb, tmp):
                sv_old, ss_old = b.dall_v, b.dall_s
                b.dall_v = True
                b.dall_s = True
                r = _ln_batch(b, s1, s2, rst, nb, tmp)
                b.dall_v = sv_old
                b.dall_s = ss_old
                return r

            def _ln_batch(b, s1, s2, rst, nb, tmp):
                b.v(lambda e: e.tensor_scalar(tmp[:], s1[:], 1.0 / D, None, ALU.mult))
                b.v(lambda e: e.tensor_scalar(s2[:], s2[:], 1.0 / D, None, ALU.mult))
                b.v(lambda e: e.scalar_tensor_tensor(rst[:], tmp[:], -1.0, tmp[:], ALU.mult, ALU.mult))
                b.v(lambda e: e.tensor_add(rst[:], rst[:], s2[:]))
                b.v(lambda e: e.tensor_scalar(rst[:], rst[:], LNEPS, None, ALU.add))
                sv = b.vpub()
                b.w("scalar", "SV", sv)
                b.s(lambda e: e.activation(rst[:], rst[:], AF.Sqrt))
                ssx = b.spub()
                b.w("vector", "SS", ssx)
                b.vn(lambda e: e.reciprocal(rst[:], rst[:]))
                b.vd()
                b.v(lambda e: e.scalar_tensor_tensor(nb[:], tmp[:], -1.0, rst[:], ALU.mult, ALU.mult))
                return b.vpub()

            b.w("vector", "SS", ss_stats[NT - 1])
            sv_lnA = ln_batch(b, a.sA1, a.sA2, a.rA, a.nA, a.tmp1)

            sv_norm_done = {}
            ss_h = {}
            sm_tr = {}
            sv_xnT = {}
            for i in range(NT):
                par = i % 2
                if i >= 2:
                    b.w("sync", "SS", ss_h[i - 2])
                else:
                    b.w("sync", "SS", ss_stats[NT - 2 + i])
                b.dma("sync", f"dA{par}", a.xb[par][:], h.x[i * 128:(i + 1) * 128, :])
                b.w("scalar", f"dA{par}")
                b.w("scalar", "SV", sv_lnA)
                if i >= 2:
                    b.w("scalar", "SM", sm_tr[i - 2])
                b.s(lambda e, i=i, par=par: e.activation(
                    a.xn[par][:], a.xb[par][:], AF.Identity,
                    bias=a.nA[:, i:i + 1], scale=a.rA[:, i:i + 1]))
                ss_h[i] = b.spub()
                b.w("tensor", "SS", ss_h[i])
                if i >= 1:
                    b.w("tensor", "SV", sv_xnT[i - 1])
                def trs(e, par=par):
                    for dc in range(DC):
                        e.transpose(a.psTA[:, dc * 128:(dc + 1) * 128],
                                    a.xn[par][:, dc * 128:(dc + 1) * 128], a.ident[:])
                b.t(trs)
                sm_tr[i] = b.tpub()
                b.w("vector", "SM", sm_tr[i])
                def evac(e, i=i):
                    view = a.xnT[:].rearrange("p (dc s) -> p dc s", s=S)
                    e.tensor_copy(view[:, :, i * 128:(i + 1) * 128], a.psTA[:])
                b.v(evac)
                sv_xnT[i] = b.vpub()

            b.w("tensor", "SV", sv_xnT[NT - 1])
            ss_bevac = 0
            stp = 0
            st_of_stage = {}
            SH = min(2048, S)
            for (wTn, scr, scale, biasap) in [
                    (a.wqT, h.qT_scr, 1.0 / 32.0, a.bqT),
                    (a.wkT, h.kT_scr, 1.0, a.bkT)]:
                for dq in range(DC):
                    par = stp % 2
                    stp += 1
                    for half in range(S // SH):
                        if not (wTn is a.wqT and dq == 0 and half == 0):
                            b.w("tensor", "SS", ss_bevac)
                        def mmq(e, wTn=wTn, dq=dq, half=half):
                            for dc in range(DC):
                                lhsT = wTn[:, dc * 1024 + dq * 128: dc * 1024 + (dq + 1) * 128]
                                for (off, ln) in iblk(SH):
                                    so = half * SH + off
                                    e.matmul(a.psQK[:, off:off + ln],
                                             lhsT,
                                             a.xnT[:, dc * S + so: dc * S + so + ln],
                                             start=(dc == 0), stop=(dc == DC - 1))
                        b.t(mmq)
                        sm = b.tpub()
                        b.w("scalar", "SM", sm)
                        if half == 0 and par in st_of_stage:
                            b.w("scalar", f"dSt{par}", st_of_stage[par])
                        b.s(lambda e, par=par, scale=scale, biasap=biasap, dq=dq, half=half:
                            e.activation(a.stageB[par][:, half * SH:(half + 1) * SH],
                                         a.psQK[:, 0:SH], AF.Identity,
                                         bias=biasap[:, dq:dq + 1], scale=scale))
                        ss_bevac = b.spub()
                    st_of_stage[par] = b.dma("scalar", f"dSt{par}",
                                             scr[dq], a.stageB[par][:])
            ss_vevac = {}
            for i in range(NT):
                par2 = i % 2
                if i >= 2:
                    b.w("tensor", "SS", ss_vevac[i - 2])
                else:
                    b.w("tensor", "SS", ss_bevac)
                def mmv(e, i=i, par2=par2):
                    for (off, ln) in iblk(D):
                        e.matmul(a.psV[par2][:, off:off + ln], a.ones_r[0:1, :],
                                 a.bvB[0:1, off:off + ln], start=True, stop=False)
                    for dc in range(DC):
                        lhsT = a.xnT[:, dc * S + i * 128: dc * S + (i + 1) * 128]
                        for (off, ln) in iblk(D):
                            e.matmul(a.psV[par2][:, off:off + ln], lhsT,
                                     a.wvT[:, dc * 1024 + off: dc * 1024 + off + ln],
                                     start=False, stop=(dc == DC - 1))
                b.t(mmv)
                sm = b.tpub()
                par = stp % 2
                stp += 1
                b.w("scalar", "SM", sm)
                if par in st_of_stage:
                    b.w("scalar", f"dSt{par}", st_of_stage[par])
                b.s(lambda e, par=par, par2=par2: e.copy(a.vst[par2][:], a.psV[par2][:]))
                ss_vevac[i] = b.spub()
                st_of_stage[par] = b.dma("scalar", f"dSt{par}", h.v_scr[i], a.vst[par2][:])

            b.barrier()

            def quant_ladder_w(b, src_f32, cols, sv_gate):
                blocks = cols // 16
                bm3 = src_f32[:, 0:cols].rearrange("p (bl k) -> p bl k", k=16)
                b.vn(lambda e: e.tensor_reduce(a.bmW[:, 0:blocks], bm3, AX.X, ALU.max,
                                               apply_absolute_value=True))
                b.vd()
                b.vn(lambda e: e.reciprocal(a.rcW[:, 0:blocks], a.bmW[:, 0:blocks]))
                b.vd()
                b.vn(lambda e: e.tensor_scalar(a.rcW[:, 0:blocks], a.rcW[:, 0:blocks],
                                               6.0, None, ALU.mult))
                b.vd()
                b.vn(lambda e: e.reciprocal(a.shW[:, 0:blocks], a.rcW[:, 0:blocks]))
                b.vd()
                b.vn(lambda e: e.tensor_scalar(a.shW[:, 0:blocks], a.shW[:, 0:blocks],
                                               1e-12, None, ALU.max))
                b.vd()
                b.vn(lambda e: e.reciprocal(a.rcW[:, 0:blocks], a.shW[:, 0:blocks]))
                b.vd()
                b.vn(lambda e: e.tensor_scalar(a.rcW[:, 0:blocks], a.rcW[:, 0:blocks],
                                               a.tieb[:], None, ALU.mult))
                b.vd()
                rc_b = a.rcW[:, 0:blocks].unsqueeze(2).broadcast_to([128, blocks, 16])
                sh_b = a.shW[:, 0:blocks].unsqueeze(2).broadcast_to([128, blocks, 16])
                x3 = src_f32[:, 0:cols].rearrange("p (bl k) -> p bl k", k=16)
                xn3 = a.xnW[:, 0:cols].rearrange("p (bl k) -> p bl k", k=16)
                b.v(lambda e: e.tensor_tensor(xn3, x3, rc_b, ALU.mult))
                svx = b.vpub()
                b.w("scalar", "SV", svx)
                b.s(lambda e: e.sign(a.sgW[:, 0:cols], a.xnW[:, 0:cols]))
                b.s(lambda e: e.activation(a.xnW[:, 0:cols], a.xnW[:, 0:cols], AF.Abs))
                sss = b.spub()
                b.w("vector", "SS", sss)
                aw = a.xnW[:, 0:cols]
                b.v(lambda e: e.tensor_scalar(a.acW[:, 0:cols], aw, 0.25, None, ALU.is_gt))
                for th in (0.75, 1.25, 1.75):
                    b.v(lambda e, th=th: e.scalar_tensor_tensor(
                        a.acW[:, 0:cols], aw, th, a.acW[:, 0:cols], ALU.is_gt, ALU.add))
                b.v(lambda e: e.tensor_scalar(a.bbW[:, 0:cols], aw, 2.5, None, ALU.is_gt))
                b.v(lambda e: e.scalar_tensor_tensor(
                    a.bbW[:, 0:cols], aw, 3.5, a.bbW[:, 0:cols], ALU.is_gt, ALU.add))
                b.v(lambda e: e.tensor_scalar(a.dqW[:, 0:cols], aw, 5.0, 2.0,
                                              ALU.is_gt, ALU.mult))
                b.v(lambda e: e.tensor_add(a.bbW[:, 0:cols], a.bbW[:, 0:cols],
                                           a.dqW[:, 0:cols]))
                b.v(lambda e: e.scalar_tensor_tensor(
                    a.bbW[:, 0:cols], a.acW[:, 0:cols], 0.5, a.bbW[:, 0:cols],
                    ALU.mult, ALU.add))
                b.v(lambda e: e.tensor_mul(a.bbW[:, 0:cols], a.bbW[:, 0:cols],
                                           a.sgW[:, 0:cols]))
                dq3 = a.dqW[:, 0:cols].rearrange("p (bl k) -> p bl k", k=16)
                qs3 = a.bbW[:, 0:cols].rearrange("p (bl k) -> p bl k", k=16)
                b.v(lambda e: e.tensor_tensor(dq3, qs3, sh_b, ALU.mult))
                return b.vpub()

            w1_items = ([(h.wg, h.wgT_scr, t, D, DC, 0, 0) for t in range(IC)] +
                        [(h.wu, h.wuT_scr, t, D, DC, 0, 0) for t in range(IC)] +
                        [(h.wd, h.wdT_scr, t, 1408, 11, hh * 1408, hh * 11)
                         for t in range(DC) for hh in range(2)])
            w1s = {"wp": 0, "sm_wtr": 0, "store": {}}

            def w1_load(wp):
                hsrc, scr, t, cols, nchunk, coloff, choff = w1_items[wp]
                par = wp % 2
                if wp >= 2:
                    b.w("sync", "SV", ssn[("w1ld", wp - 2)])
                b.dma("sync", f"dA{par}", a.wf[par][:, 0:cols],
                      hsrc[t * 128:(t + 1) * 128, coloff:coloff + cols])
                ssn[("w1ldc", wp)] = b.C[f"dA{par}"]

            def w1_step():
                wp = w1s["wp"]
                if wp >= len(w1_items):
                    return
                sv_old, ss_old = b.dall_v, b.dall_s
                b.dall_v = False
                b.dall_s = False
                try:
                    w1_step_body(wp)
                finally:
                    b.dall_v = sv_old
                    b.dall_s = ss_old

            def w1_step_body(wp):
                if wp == 0:
                    w1_load(0)
                    w1_load(1)
                hsrc, scr, t, cols, nchunk, coloff, choff = w1_items[wp]
                par = wp % 2
                b.w("vector", f"dA{par}", ssn[("w1ldc", wp)])
                if wp >= 1:
                    b.w("vector", "SM", w1s["sm_wtr"])
                sv_wdeq = quant_ladder_w(b, a.wf[par], cols, None)
                ssn[("w1ld", wp)] = sv_wdeq
                if wp + 2 < len(w1_items):
                    w1_load(wp + 2)
                spar = wp % 2
                if wp >= 1:
                    osp = (wp - 1) % 2
                    b.w("tensor", f"dSt{osp}", w1s["store"].get(osp, 0))
                if wp >= 2:
                    b.w("tensor", f"dSt{spar}", w1s["store"].get(spar, 0))
                rounds = [range(r, min(r + 8, nchunk)) for r in range(0, nchunk, 8)]
                for ri, rng in enumerate(rounds):
                    b.w("tensor", "SV", sv_wdeq)
                    if ri > 0 or wp > 0:
                        b.w("tensor", "SV", ssn.get(("w1evac_last",), 0))
                    def trs(e, rng=rng):
                        for j, ch in enumerate(rng):
                            e.transpose(a.psWT[:, j * 128:(j + 1) * 128],
                                        a.dqW[:, ch * 128:(ch + 1) * 128], a.ident[:])
                    b.t(trs)
                    smx = b.tpub()
                    w1s["sm_wtr"] = smx
                    b.w("vector", "SM", smx)
                    def evac(e, rng=rng):
                        n = len(rng) * 128
                        c0 = rng[0] * 128
                        e.tensor_copy(a.wst[0][:, c0:c0 + n], a.psWT[:, 0:n])
                    b.v(evac)
                    ssn[("w1evac_last",)] = b.vpub()
                b.w("scalar", "SV", ssn[("w1evac_last",)])
                dst = scr[choff:choff + nchunk, :, t * 128:(t + 1) * 128].rearrange(
                    "ch p j -> p ch j")
                src = a.wst[0][:, 0:nchunk * 128].rearrange(
                    "p (ch j) -> p ch j", j=128)
                w1s["store"][spar] = b.dma("scalar", f"dSt{spar}", dst, src)
                w1s["wp"] = wp + 1

            b.dall_v = False
            b.v(lambda e: e.memset(a.zmat[:], 1.0))
            sv_zinit = b.vpub()
            b.w("sync", "SS")
            sv_x2 = {}
            ss_x2sq = {}
            st_x2 = {}
            sm_scores = {}
            ss_exp = {}
            ss_zevac = {}
            sm_bc = {}
            sv_normt = {}
            sm_group_end = 0
            xr_cnt = {}
            zspread_last = 0
            zrecip_last = 0
            zflat_busy = 0
            ktdone_last = {}
            sm_pv_last = {}
            sm_pv_end = 0
            sv_x2_last = 0
            for g in range(NG):
                b.dall_v = True
                b.dall_s = True
                if g > 0:
                    b.w("sync", "SM", sm_group_end)
                qsrc = h.qT_scr[:, :, g * GQ:(g + 1) * GQ]
                b.dma("sync", "dQt", a.qTg[:].rearrange("p (dc q) -> p dc q", q=GQ),
                      qsrc.rearrange("dc p q -> p dc q"))
                dqt_cnt = b.C["dQt"]
                ktb_cnt = {}
                for t in range(NT):
                    tb = t // ZR
                    pb = tb % 2
                    if t % ZR == 0:
                        if pb in ktdone_last:
                            b.w("sync", "SM", ktdone_last[pb])
                        ksrc = h.kT_scr[:, :, tb * ZR * 128:(tb + 1) * ZR * 128]
                        b.dma("sync", f"dKt{pb}",
                              a.ktb[pb][:].rearrange("p (dc tj) -> p dc tj", tj=ZR * 128),
                              ksrc.rearrange("dc p tj -> p dc tj"))
                        ktb_cnt[tb] = b.C[f"dKt{pb}"]
                    pS = t % 2
                    b.w("tensor", f"dKt{pb}", ktb_cnt[tb])
                    b.w("tensor", "dQt", dqt_cnt)
                    if (g, t - 2) in ss_zevac:
                        b.w("tensor", "SS", ss_zevac[(g, t - 2)])
                    elif g > 0 and t < 2:
                        b.w("tensor", "SS", ss_zevac[(g - 1, NT - 1)])
                        b.w("tensor", "SV", sv_x2_last)
                    def mms(e, t=t, pS=pS, pb=pb):
                        toff = (t % ZR) * 128
                        for dc in range(DC):
                            lhsT = a.ktb[pb][:, dc * ZR * 128 + toff:
                                             dc * ZR * 128 + toff + 128]
                            for (off, ln) in iblk(GQ):
                                e.matmul(a.psS[pS][:, off:off + ln], lhsT,
                                         a.qTg[:, dc * GQ + off: dc * GQ + off + ln],
                                         start=(dc == 0), stop=(dc == DC - 1))
                    b.t(mms)
                    sm_scores[(g, t)] = b.tpub()
                    if t == NT - 1:
                        sm_group_end = sm_scores[(g, t)]
                    b.w("scalar", "SM", sm_scores[(g, t)])
                    if t == 0 and g > 0:
                        b.w("scalar", "SM", sm_pv_end)
                    b.s(lambda e, t=t, pS=pS: e.activation(
                        a.pT[:, t * GQ:(t + 1) * GQ], a.psS[pS][:], AF.Exp))
                    ss_exp[(g, t)] = b.spub()
                    b.w("tensor", "SS", ss_exp[(g, t)])
                    def mmz(e, t=t, pS=pS):
                        for (off, ln) in iblk(GQ):
                            e.matmul(a.psS[pS][0:1, off:off + ln], a.ones_c[:],
                                     a.pT[:, t * GQ + off: t * GQ + off + ln],
                                     start=True, stop=True)
                    b.t(mmz)
                    smz = b.tpub()
                    b.w("scalar", "SM", smz)
                    if zrecip_last == 0:
                        b.w("scalar", "SV", sv_zinit)
                    b.s(lambda e, t=t, pS=pS: e.activation(
                        a.zmat[(t % ZR) * 32:(t % ZR) * 32 + 1, :],
                        a.psS[pS][0:1, :], AF.Ln, scale=RECIP_NORM))
                    ss_zevac[(g, t)] = b.spub()
                    w1_step()
                    if t % ZR == ZR - 1:
                        ktdone_last[pb] = sm_scores[(g, t)]
                        r = t // ZR
                        b.w("scalar", "SM", zflat_busy)
                        b.s(lambda e: e.activation(a.zmatB[:], a.zmat[:],
                                                   AF.Exp, scale=-1.0))
                        zrecip_last = b.spub()
                        for tt in range(r * ZR, (r + 1) * ZR):
                            b.w("tensor", "SS", zrecip_last)
                            if (g, tt - 1) in sv_normt:
                                b.w("tensor", "SV", sv_normt[(g, tt - 1)])
                            def mmb(e, tt=tt):
                                ro = (tt % ZR) * 32
                                for (off, ln) in iblk(GQ):
                                    e.matmul(a.psB[0][:, off:off + ln],
                                             a.ones_r[ro:ro + 1, :],
                                             a.zmatB[ro:ro + 1, off:off + ln],
                                             start=True, stop=True)
                            b.t(mmb)
                            sm_bc[(g, tt)] = b.tpub()
                            b.w("vector", "SM", sm_bc[(g, tt)])
                            b.w("vector", "SS", ss_exp[(g, tt)])
                            def nrm(e, tt=tt):
                                sl = a.pT[:, tt * GQ:(tt + 1) * GQ]
                                e.tensor_tensor(sl, sl, a.psB[0][:], ALU.mult)
                            b.v(nrm)
                            sv_normt[(g, tt)] = b.vpub()
                        zflat_busy = sm_bc[(g, (r + 1) * ZR - 1)]

                b.dall_v = True
                b.dall_s = True
                ss_oT = 0
                vt_cnt = {}
                WAVES = [(0, 3), (3, 6), (6, 8)]
                for wave, (dlo, dhi) in enumerate(WAVES):
                    for t in range(NT):
                        pv = t % 2
                        if pv in sm_pv_last:
                            b.w("sync", "SM", sm_pv_last[pv])
                        vt_cnt[(wave, t)] = b.dma("sync", f"dVt{pv}",
                                                  a.vrn[pv][:], h.v_scr[t])
                        b.w("tensor", f"dVt{pv}", vt_cnt[(wave, t)])
                        if wave == 0 and t == 0:
                            b.w("tensor", "SV", sv_normt[(g, NT - 1)])
                            if g > 0:
                                b.w("tensor", "SS", ssn[("afdone", g - 1)])
                        if wave > 0 and t == 0:
                            b.w("tensor", "SS", ss_oT)
                        def mpv(e, dlo=dlo, dhi=dhi, t=t, pv=pv):
                            for dcw in range(dhi - dlo):
                                dc = dlo + dcw
                                lhsT = a.vrn[pv][:, dc * 128:(dc + 1) * 128]
                                for (off, ln) in iblk(GQ):
                                    e.matmul(a.psO[dcw][:, off:off + ln], lhsT,
                                             a.pT[:, t * GQ + off: t * GQ + off + ln],
                                             start=(t == 0), stop=(t == NT - 1))
                        b.t(mpv)
                        sm_pv_last[pv] = b.tpub()
                    b.w("scalar", "SM", sm_pv_last[(NT - 1) % 2])
                    def oevac(e, dlo=dlo, dhi=dhi):
                        for dcw in range(dhi - dlo):
                            dc = dlo + dcw
                            e.copy(a.oT[:, dc * GQ:(dc + 1) * GQ], a.psO[dcw][:])
                    b.s(oevac)
                    ss_oT = b.spub()
                sm_pv_end = sm_pv_last[(NT - 1) % 2]

                for qt in range(TPG):
                    pq = qt % 2
                    row0 = g * GQ + qt * 128
                    prev = (g, qt - 1) if qt >= 1 else ((g - 1, TPG - 1) if g > 0 else None)
                    if prev is not None:
                        b.w("sync", "SV", sv_x2[prev])
                        b.w("sync", "SS", ss_x2sq[prev])
                    xr_cnt[(g, qt)] = b.dma("sync", f"dXr{pq}", a.xres[0][:],
                                            h.x[row0:row0 + 128, :])
                    b.w("tensor", "SS", ss_oT)
                    if (g, qt - 2) in sv_x2:
                        b.w("tensor", "SV", sv_x2[(g, qt - 2)])
                    elif g > 0:
                        b.w("tensor", "SV", sv_x2[(g - 1, TPG - 2 + qt)])
                    def mwo(e, qt=qt, pq=pq):
                        for (off, ln) in iblk(D):
                            e.matmul(a.psAF[pq][:, off:off + ln], a.ones_r[0:1, :],
                                     a.boB[0:1, off:off + ln], start=True, stop=False)
                        for kc in range(DC):
                            lhsT = a.oT[:, kc * GQ + qt * 128: kc * GQ + (qt + 1) * 128]
                            for (off, ln) in iblk(D):
                                e.matmul(a.psAF[pq][:, off:off + ln], lhsT,
                                         a.woT[:, kc * 1024 + off: kc * 1024 + off + ln],
                                         start=False, stop=(kc == DC - 1))
                    b.t(mwo)
                    smw = b.tpub()
                    if g == NG - 1 and qt == TPG - 1:
                        ssn[("afdone_last",)] = smw
                    b.w("vector", "SM", smw)
                    b.w("vector", f"dXr{pq}", xr_cnt[(g, qt)])
                    if prev is not None:
                        b.w("vector", f"dX2{prev[1] % 2}", st_x2[prev])
                        b.w("vector", "SS", ss_x2sq[prev])
                    b.v(lambda e, pq=pq: e.scalar_tensor_tensor(
                        a.x2st[0][:], a.psAF[pq][:], 1.0, a.xres[0][:],
                        ALU.mult, ALU.add))
                    idx = g * TPG + qt
                    b.v(lambda e, idx=idx: e.tensor_reduce(
                        a.sD1[:, idx:idx + 1], a.x2st[0][:], AX.X, ALU.add))
                    sv_x2[(g, qt)] = b.vpub()
                    sv_x2_last = sv_x2[(g, qt)]
                    b.w("scalar", "SV", sv_x2[(g, qt)])
                    b.s(lambda e, idx=idx: e.activation(
                        a.xres[0][:], a.x2st[0][:], AF.Square, accum_out=a.sD2[:, idx:idx + 1]))
                    ss_x2sq[(g, qt)] = b.spub()
                    st_x2[(g, qt)] = b.dma("scalar", f"dX2{pq}",
                                           h.x2_scr[row0:row0 + 128, :], a.x2st[0][:])
                ssn[("afdone", g)] = ss_x2sq[(g, TPG - 1)]

            b.dall_v = False
            b.dall_s = False
            while w1s["wp"] < len(w1_items):
                w1_step()

            b.barrier()
            b.w("sync", "dX20", b.C["dX20"])
            b.w("sync", "dX21", b.C["dX21"])

            sv_lnD = ln_batch(b, a.sD1, a.sD2, a.rD, a.nD, a.tmp1)
            b.dma("sync", "dDw", a.wgT[:].rearrange("p (dc i) -> p dc i", i=I),
                  h.wgT_scr.rearrange("dc p i -> p dc i"))
            b.dma("sync", "dDw", a.wuT[:].rearrange("p (dc i) -> p dc i", i=I),
                  h.wuT_scr.rearrange("dc p i -> p dc i"))
            b.dma("sync", "dDw", a.wdT[:].rearrange("p (ic d) -> p ic d", d=D),
                  h.wdT_scr.rearrange("ic p d -> p ic d"))
            dw_cnt = b.C["dDw"]

            def quant_ladder_a(b, src_bf, cols):
                blocks = cols // 16
                s3 = src_bf[:, 0:cols].rearrange("p (bl k) -> p bl k", k=16)
                b.vn(lambda e: e.tensor_reduce(a.bmD[:, 0:blocks], s3, AX.X, ALU.max,
                                               apply_absolute_value=True))
                b.vd()
                b.vn(lambda e: e.reciprocal(a.rcD[:, 0:blocks], a.bmD[:, 0:blocks]))
                b.vd()
                b.vn(lambda e: e.tensor_scalar(a.rcD[:, 0:blocks], a.rcD[:, 0:blocks],
                                               6.0, None, ALU.mult))
                b.vn(lambda e: e.tensor_scalar(a.shD[:, 0:blocks], a.bmD[:, 0:blocks],
                                               1.0 / 6.0, None, ALU.mult))
                b.vd()
                rc_b = a.rcD[:, 0:blocks].unsqueeze(2).broadcast_to([128, blocks, 16])
                sh_b = a.shD[:, 0:blocks].unsqueeze(2).broadcast_to([128, blocks, 16])
                xn3 = a.xnD[:, 0:cols].rearrange("p (bl k) -> p bl k", k=16)
                b.v(lambda e: e.tensor_tensor(xn3, s3, rc_b, ALU.mult))
                svx = b.vpub()
                b.w("scalar", "SV", svx)
                b.s(lambda e: e.sign(a.sgnD[:, 0:cols], a.xnD[:, 0:cols]))
                b.s(lambda e: e.activation(a.xnD[:, 0:cols], a.xnD[:, 0:cols], AF.Abs))
                sss = b.spub()
                b.w("vector", "SS", sss)
                aw = a.xnD[:, 0:cols]
                AC, BB, C1, C2_, SG = (a.acD[:, 0:cols], a.bbD[:, 0:cols],
                                       a.cD1[:, 0:cols], a.cD2[:, 0:cols],
                                       a.sgD[:, 0:cols])
                b.vn(lambda e: e.tensor_scalar(AC, aw, 0.25, None, ALU.is_gt))
                b.vn(lambda e: e.tensor_scalar(BB, aw, 0.75, None, ALU.is_gt))
                b.vn(lambda e: e.tensor_scalar(C1, aw, 1.25, None, ALU.is_gt))
                b.vn(lambda e: e.tensor_scalar(C2_, aw, 1.75, None, ALU.is_gt))
                b.vd()
                b.vn(lambda e: e.tensor_add(AC, AC, BB))
                b.vn(lambda e: e.tensor_add(C1, C1, C2_))
                b.vn(lambda e: e.tensor_scalar(SG, aw, 2.5, None, ALU.is_gt))
                b.vd()
                b.vn(lambda e: e.tensor_add(AC, AC, C1))
                b.vn(lambda e: e.tensor_scalar(BB, aw, 3.5, None, ALU.is_gt))
                b.vn(lambda e: e.tensor_scalar(C2_, aw, 5.0, 2.0, ALU.is_gt, ALU.mult))
                b.vd()
                b.vn(lambda e: e.tensor_add(BB, BB, SG))
                b.vd()
                b.vn(lambda e: e.tensor_add(BB, BB, C2_))
                b.vd()
                b.vn(lambda e: e.scalar_tensor_tensor(BB, AC, 0.5, BB, ALU.mult, ALU.add))
                b.vd()
                b.vn(lambda e: e.tensor_mul(BB, BB, a.sgnD[:, 0:cols]))
                b.vd()
                dq3 = a.sgD[:, 0:cols].rearrange("p (bl k) -> p bl k", k=16)
                qs3 = a.bbD[:, 0:cols].rearrange("p (bl k) -> p bl k", k=16)
                b.v(lambda e: e.tensor_tensor(dq3, qs3, sh_b, ALU.mult))
                return b.vpub()

            b.dall_v = True
            b.dall_s = True
            xl_cnt = {}
            st_h2 = {}
            for i in range(NT):
                par = i % 2
                if i >= 2:
                    b.w("sync", "SV", ssn[("h2xn", i - 2)])
                xl_cnt[i] = b.dma("sync", f"dXl{par}", a.x2t[par][:],
                                  h.x2_scr[i * 128:(i + 1) * 128, :])
                b.w("scalar", f"dXl{par}", xl_cnt[i])
                b.w("scalar", "SV", sv_lnD)
                if i >= 1:
                    b.w("scalar", "SV", ssn[("h2xn", i - 1)])
                b.s(lambda e, i=i, par=par: e.activation(
                    a.h2b, a.x2t[par][:], AF.Identity,
                    bias=a.nD[:, i:i + 1], scale=a.rD[:, i:i + 1]))
                ss_h2 = b.spub()
                b.w("vector", "SS", ss_h2)
                if i >= 1:
                    b.w("vector", "SM", ssn[("h2tr", i - 1)])
                sv_q2c = quant_ladder_a(b, a.h2b, D)
                ssn[("h2xn", i)] = sv_q2c
                b.w("tensor", "SV", sv_q2c)
                if i >= 1:
                    b.w("tensor", "SV", ssn[("h2ev", i - 1)])
                def tr2(e):
                    for dc in range(DC):
                        e.transpose(a.psT2[:, dc * 128:(dc + 1) * 128],
                                    a.sgD[:, dc * 128:(dc + 1) * 128], a.ident[:])
                b.t(tr2)
                smt = b.tpub()
                ssn[("h2tr", i)] = smt
                b.w("vector", "SM", smt)
                pq = i % 2
                if i >= 2:
                    b.w("vector", f"dSt{pq}", st_h2[i - 2])
                b.v(lambda e, pq=pq: e.tensor_copy(a.h2qT[pq][:], a.psT2[:]))
                ssn[("h2ev", i)] = b.vpub()
                b.w("scalar", "SV", ssn[("h2ev", i)])
                st_h2[i] = b.dma("scalar", f"dSt{pq}", h.h2qT_scr[i], a.h2qT[pq][:])

            b.dall_v = False
            b.dall_s = False
            y_cnt = {}
            sv_y = {}
            h2l_cnt = {}
            for i in range(NT):
                par = i % 2
                if i >= 2:
                    b.w("sync", "SM", ssn[("up_done", i - 2)])
                else:
                    b.w("sync", f"dSt{par}", st_h2[NT - 2 + par])
                h2l_cnt[i] = b.dma("sync", f"dKt{par}", a.h2qT[par][:], h.h2qT_scr[i])
                if i >= 2:
                    b.w("sync", "SV", sv_y[i - 2])
                xl2 = b.dma("sync", f"dXl{par}", a.x2t[par][:],
                            h.x2_scr[i * 128:(i + 1) * 128, :])
                ssn[("xl2", i)] = xl2
                b.w("tensor", f"dKt{par}", h2l_cnt[i])
                b.w("tensor", "dDw", dw_cnt)
                if i >= 1:
                    b.w("tensor", "SV", ssn[("h3mul", i - 1)])
                def mg(e, par=par):
                    for dc in range(DC):
                        lhsT = a.h2qT[par][:, dc * 128:(dc + 1) * 128]
                        for (off, ln) in iblk(I):
                            e.matmul(a.psGU[:, off:off + ln], lhsT,
                                     a.wgT[:, dc * I + off: dc * I + off + ln],
                                     start=(dc == 0), stop=(dc == DC - 1))
                b.t(mg)
                smg = b.tpub()
                b.w("scalar", "SM", smg)
                if i >= 1:
                    b.w("scalar", "SM", ssn[("tr3", i - 1)])
                b.s(lambda e: e.activation(a.sgD[:, 0:I], a.psGU[:, 0:I], AF.Silu))
                ss_sg = b.spub()
                b.w("tensor", "SS", ss_sg)
                def mu(e, par=par):
                    for dc in range(DC):
                        lhsT = a.h2qT[par][:, dc * 128:(dc + 1) * 128]
                        for (off, ln) in iblk(I):
                            e.matmul(a.psGU[:, off:off + ln], lhsT,
                                     a.wuT[:, dc * I + off: dc * I + off + ln],
                                     start=(dc == 0), stop=(dc == DC - 1))
                b.t(mu)
                smu = b.tpub()
                ssn[("up_done", i)] = smu
                b.w("vector", "SM", smu)
                if i >= 1:
                    b.w("vector", "SV", ssn[("h3q", i - 1)])
                b.v(lambda e: e.tensor_tensor(a.h3D[:, 0:I], a.psGU[:, 0:I],
                                              a.sgD[:, 0:I], ALU.mult))
                ssn[("h3mul", i)] = b.vpub()
                b.w("vector", "SV", ssn[("h3mul", i)])
                sv_q3 = quant_ladder_a(b, a.h3D, I)
                ssn[("h3q", i)] = sv_q3
                rounds = [range(r, min(r + 8, IC)) for r in range(0, IC, 8)]
                last_evac = 0
                for ri, rng in enumerate(rounds):
                    b.w("tensor", "SV", sv_q3 if ri == 0 else last_evac)
                    if ri == 0 and i >= 1:
                        b.w("tensor", "SV", ssn[("h3ev", i - 1)])
                        b.w("tensor", "SV", sv_y[i - 1])
                    def tr3(e, rng=rng):
                        for j, ch in enumerate(rng):
                            e.transpose(a.psT2[:, j * 128:(j + 1) * 128],
                                        a.sgD[:, ch * 128:(ch + 1) * 128], a.ident[:])
                    b.t(tr3)
                    smr = b.tpub()
                    b.w("vector", "SM", smr)
                    def ev3(e, rng=rng):
                        n = len(rng) * 128
                        c0 = rng[0] * 128
                        e.tensor_copy(a.h3qT[:, c0:c0 + n], a.psT2[:, 0:n])
                    b.v(ev3)
                    last_evac = b.vpub()
                ssn[("tr3", i)] = b.C["SM"]
                ssn[("h3ev", i)] = last_evac
                b.w("tensor", "SV", last_evac)
                if i >= 1:
                    b.w("tensor", "SV", sv_y[i - 1])
                def md(e):
                    for ic in range(IC):
                        lhsT = a.h3qT[:, ic * 128:(ic + 1) * 128]
                        for (off, ln) in iblk(D):
                            e.matmul(a.psD[:, off:off + ln], lhsT,
                                     a.wdT[:, ic * D + off: ic * D + off + ln],
                                     start=(ic == 0), stop=(ic == IC - 1))
                b.t(md)
                smd = b.tpub()
                b.w("vector", "SM", smd)
                b.w("vector", f"dXl{par}", ssn[("xl2", i)])
                if i >= 2:
                    b.w("vector", f"dY{par}", y_cnt[i - 2])
                b.v(lambda e, par=par: e.scalar_tensor_tensor(
                    a.yst[par][:], a.psD[:], 1.0, a.x2t[par][:], ALU.mult, ALU.add))
                sv_y[i] = b.vpub()
                b.w("scalar", "SV", sv_y[i])
                y_cnt[i] = b.dma("scalar", f"dY{par}",
                                 h.y[i * 128:(i + 1) * 128, :], a.yst[par][:])

            b.w("scalar", "dY0", b.C["dY0"])
            b.w("scalar", "dY1", b.C["dY1"])

        with nc.Block() as block:
            refC = {}

            def mk(E):
                def body(eng):
                    bb = B(E, eng, sems, safe=safe)
                    schedule(bb)
                    refC[E] = dict(bb.C)
                return body
            block.sync(mk("sync"))
            block.tensor(mk("tensor"))
            block.vector(mk("vector"))
            block.scalar(mk("scalar"))
            block.gpsimd(lambda eng: None)
            vals = list(refC.values())
            for v in vals[1:]:
                assert v == vals[0], "pass counter mismatch"

    return nc


_NC_CACHE = {}


def _get_nc(S):
    if S not in _NC_CACHE:
        _NC_CACHE[S] = build(S)
    return _NC_CACHE[S]


def make_inputs(x_b, wq, bq, wk, bk, wv, bv, wo, bo, wg, wu, wd):
    return {
        "x": np.ascontiguousarray(x_b, dtype=np.float32),
        "wq": np.ascontiguousarray(wq, dtype=np.float32),
        "wk": np.ascontiguousarray(wk, dtype=np.float32),
        "wv": np.ascontiguousarray(wv, dtype=np.float32),
        "wo": np.ascontiguousarray(wo, dtype=np.float32),
        "wg": np.ascontiguousarray(wg, dtype=np.float32),
        "wu": np.ascontiguousarray(wu, dtype=np.float32),
        "wd": np.ascontiguousarray(wd, dtype=np.float32),
        "bqT": np.ascontiguousarray(np.asarray(bq, dtype=np.float32).reshape(DC, 128).T),
        "bkT": np.ascontiguousarray(np.asarray(bk, dtype=np.float32).reshape(DC, 128).T),
        "bv_row": np.ascontiguousarray(np.asarray(bv, dtype=np.float32).reshape(1, D)),
        "bo_row": np.ascontiguousarray(np.asarray(bo, dtype=np.float32).reshape(1, D)),
        "ident": np.eye(128, dtype=ml_dtypes.bfloat16),
    }


def kernel(L0_attn_batch, L0_attn_Wq, L0_attn_bq, L0_attn_Wk, L0_attn_bk,
           L0_attn_Wv, L0_attn_bv, L0_attn_Wo, L0_attn_bo,
           L0_ffn_w_gate, L0_ffn_w_up, L0_ffn_w_down):
    xs = np.ascontiguousarray(np.asarray(L0_attn_batch, dtype=np.float32))
    Bn, S, Dm = xs.shape
    assert Dm == D
    nc = _get_nc(S)
    in_maps = []
    for c in range(Bn):
        in_maps.append(make_inputs(xs[c], L0_attn_Wq, L0_attn_bq, L0_attn_Wk,
                                   L0_attn_bk, L0_attn_Wv, L0_attn_bv,
                                   L0_attn_Wo, L0_attn_bo,
                                   L0_ffn_w_gate, L0_ffn_w_up, L0_ffn_w_down))
    res = run_bass_kernel_spmd(nc, in_maps, core_ids=list(range(Bn)))
    outs = [np.asarray(r["y"]) for r in res.results]
    if outs[0].dtype != np.float64:
        outs = [o.astype(np.float32, copy=False) for o in outs]
    return np.stack(outs)
